# version 03
# baseline (speedup 1.0000x reference)
import sys
if "/opt/trn_rl_repo" not in sys.path:
    sys.path.insert(0, "/opt/trn_rl_repo")

import os
import numpy as np
import concourse.mybir as mybir

SKIP_BIASMM = os.environ.get("TTT_SKIP_BIASMM", "0") == "1"
SKIP_GRADS = os.environ.get("TTT_SKIP_GRADS", "0") == "1"
SKIP_GATES = os.environ.get("TTT_SKIP_GATES", "0") == "1"
TR_F32 = os.environ.get("TTT_TR_F32", "0") == "1"
SKIP_AR = os.environ.get("TTT_SKIP_AR", "0") == "1"
PHASE = int(os.environ.get("TTT_PHASE", "99"))
SUB = int(os.environ.get("TTT_SUB", "99"))

F32 = mybir.dt.float32
F32R = mybir.dt.float32r
AF = mybir.ActivationFunctionType
ALU = mybir.AluOpType

NCH = 64
D = 2048
MC = 256
CK = 128
KD = 16
LR_MEM = 0.01
GFAC = 2.0 / (CK * D)


class Core:
    def __init__(self, nc, n_cores=8):
        self.nc = nc
        self.ops = {e: [] for e in ("tensor", "vector", "scalar", "sync", "gpsimd")}
        self.cnt = {}
        self.n_cores = n_cores
        self.ctx = []
        self.sems = {}


    def em(self, eng, fn):
        self.ops[eng].append(fn)

    def drain(self, eng):
        self.ops[eng].append(lambda e: e.drain())

    def tick(self, eng, fn, sem_name):
        sem = self.sems[sem_name]
        self.ops[eng].append(lambda e, fn=fn, sem=sem: fn(e).then_inc(sem, 1))
        return self.inc(sem_name, 1)

    def dma16(self, eng, fn, sem_name):
        sem = self.sems[sem_name]
        self.ops[eng].append(lambda e, fn=fn, sem=sem: fn(e).then_inc(sem, 16))
        return self.inc(sem_name, 16)

    def wait(self, eng, sem_name, val):
        if val <= 0:
            return
        sem = self.sems[sem_name]
        self.ops[eng].append(lambda e, sem=sem, val=val: e.wait_ge(sem, val))

    def inc(self, name, by):
        self.cnt[name] = self.cnt.get(name, 0) + by
        return self.cnt[name]

    def val(self, name):
        return self.cnt.get(name, 0)

    def t_mm(self, psum_ap, lhsT, rhs, start, stop, tick=False):
        def fn(e, a=psum_ap, l=lhsT, r=rhs, st=start, sp=stop):
            return e.matmul(a, l, r, start=st, stop=sp)
        if tick:
            return self.tick("tensor", fn, "cT")
        self.em("tensor", fn)

    def t_tr(self, psum_ap, in_ap, f32r):
        if TR_F32 and f32r:
            in_ap = in_ap.bitcast(F32)
            f32r = False
        ident = self.identr[:, :] if f32r else self.ident[:, :]
        if f32r:
            psum_ap = psum_ap.bitcast(F32R)

        def fn(e, a=psum_ap, i=in_ap, idt=ident):
            return e.transpose(a, i, idt)
        return self.tick("tensor", fn, "cT")

    def new_sem(self, name):
        cm = self.nc.semaphore(name)
        self.sems[name] = cm.__enter__()
        self.ctx.append(cm)

    def sbuf(self, name, shape, dtyp=F32):
        cm = self.nc.sbuf_tensor(name, shape, dtyp)
        h = cm.__enter__()
        self.ctx.append((name, cm))
        return h

    def free_sbuf(self, names):
        kept = []
        for item in reversed(self.ctx):
            if isinstance(item, tuple) and item[0] in names:
                item[1].__exit__(None, None, None)
            else:
                kept.append(item)
        self.ctx = kept[::-1]

    def declare_common(self):
        nc = self.nc
        dt = nc.dram_tensor
        self.d_ident = dt("ttt_ident", [128, 128], F32, kind="ExternalInput").ap()
        self.d_onesr = dt("ttt_onesr", [1, 128], F32, kind="ExternalInput").ap()
        self.d_onesc = dt("ttt_onesc", [128, 1], F32, kind="ExternalInput").ap()
        self.d_epscol = dt("ttt_epscol", [128, 1], F32, kind="ExternalInput").ap()
        self.d_inv128 = dt("ttt_inv128", [1, 1], F32, kind="ExternalInput").ap()
        self.d_identr = dt("ttt_identr", [128, 128], F32R, kind="ExternalInput").ap()
        self.d_onesrr = dt("ttt_onesrr", [1, 128], F32R, kind="ExternalInput").ap()
        self.d_identb = dt("ttt_identb", [128, 128], mybir.dt.bfloat16,
                           kind="ExternalInput").ap()
        self.ident = self.sbuf("identsb", [128, 128])
        self.identb = self.sbuf("identbsb", [128, 128], mybir.dt.bfloat16)
        self.identr = self.sbuf("identrsb", [128, 128], F32R)
        self.onesrr = self.sbuf("onesrrsb", [1, 128], F32R)
        self.onesr = self.sbuf("onesrsb", [1, 128])
        self.onesc = self.sbuf("onescsb", [128, 1])
        self.epscol = self.sbuf("epscol", [128, 1])
        self.inv128 = self.sbuf("inv128", [1, 1])
        ps = nc.psum_tensor
        self.P0 = ps("PS0", [128, 512], F32).__enter__()
        self.P1 = ps("PS1", [128, 512], F32).__enter__()
        self.P2 = ps("PS2", [128, 512], F32).__enter__()
        self.P3 = ps("PS3", [128, 512], F32).__enter__()
        self.P4 = ps("PS4", [128, 512], F32).__enter__()
        self.P5 = ps("PS5", [128, 512], F32).__enter__()
        self.P6 = ps("PS6", [128, 512], F32).__enter__()
        self.P7 = ps("PS7", [128, 512], F32).__enter__()
        for s in ("cT", "cV", "cS"):
            self.new_sem(s)

    def load_common(self):
        self.new_sem("sconst")
        B = self

        def ld(e, dst, src):
            return e.dma_start(out=dst, in_=src)
        B.dma16("sync", lambda e: ld(e, B.ident[:, :], B.d_ident), "sconst")
        B.dma16("sync", lambda e: ld(e, B.identr[:, :], B.d_identr), "sconst")
        B.dma16("sync", lambda e: ld(e, B.identb[:, :], B.d_identb), "sconst")
        B.dma16("sync", lambda e: ld(e, B.onesrr[:, :], B.d_onesrr), "sconst")
        B.dma16("sync", lambda e: ld(e, B.onesr[:, :], B.d_onesr), "sconst")
        B.dma16("sync", lambda e: ld(e, B.onesc[:, :], B.d_onesc), "sconst")
        B.dma16("sync", lambda e: ld(e, B.epscol[:, :], B.d_epscol), "sconst")
        B.dma16("sync", lambda e: ld(e, B.inv128[:, :], B.d_inv128), "sconst")
        return B.val("sconst")

    def replay(self, block):
        ops = self.ops

        @block.tensor
        def _(e):
            for f in ops["tensor"]:
                f(e)

        @block.vector
        def _(e):
            for f in ops["vector"]:
                f(e)

        @block.scalar
        def _(e):
            for f in ops["scalar"]:
                f(e)

        @block.sync
        def _(e):
            for f in ops["sync"]:
                f(e)

        @block.gpsimd
        def _(e):
            for f in ops["gpsimd"]:
                f(e)


def common_inputs():
    import ml_dtypes
    return {
        "ttt_identb": np.eye(128, dtype=np.float32).astype(ml_dtypes.bfloat16),
        "ttt_ident": np.eye(128, dtype=np.float32),
        "ttt_identr": np.eye(128, dtype=np.float32),
        "ttt_onesr": np.ones((1, 128), np.float32),
        "ttt_onesrr": np.ones((1, 128), np.float32),
        "ttt_onesc": np.ones((128, 1), np.float32),
        "ttt_epscol": np.full((128, 1), 1e-5, np.float32),
        "ttt_inv128": np.full((1, 1), 1.0 / 128, np.float32),
    }



class TTTBuilder(Core):
    def __init__(self, nc, n_cores=8):
        super().__init__(nc, n_cores)
        self.chunk_sx = {}
        self.fwd_z_tick = {}
        self.pred_free = 0
        self.square_tick = {}
        self.tdh_tick = {}
        self.x2_tick = 0
        self.dsb_free = 0

    def declare(self, x2_dst=None, early=None):
        nc = self.nc
        dt = nc.dram_tensor
        self.d_x = dt("ttt_x", [NCH * CK, D], F32, kind="ExternalInput").ap()
        self.d_W0T = dt("ttt_W0T", [D, MC], F32, kind="ExternalInput").ap()
        self.d_b0 = dt("ttt_b0", [1, MC], F32, kind="ExternalInput").ap()
        self.d_W1T = dt("ttt_W1T", [MC, D], F32, kind="ExternalInput").ap()
        self.d_W1n = dt("ttt_W1n", [D, MC], F32R, kind="ExternalInput").ap()
        self.d_b1 = dt("ttt_b1", [1, D], F32R, kind="ExternalInput").ap()
        self.d_lrfgT = dt("ttt_lrfgT", [D, 2], F32, kind="ExternalInput").ap()
        self.d_lrb = dt("ttt_lrb", [1, 1], F32, kind="ExternalInput").ap()
        self.d_fgb = dt("ttt_fgb", [1, 1], F32, kind="ExternalInput").ap()
        if x2_dst is None:
            x2_dst = dt("ttt_x2", [NCH * CK, D], F32, kind="ExternalOutput").ap()
        self.d_x2 = x2_dst
        self.bin = [dt(f"ttt_arin{i}", [2 * CK, D], F32) for i in range(2)]
        self.bout = [dt(f"ttt_arout{i}", [2 * CK, D], F32) for i in range(2)]

        self.declare_common()
        if early is not None:
            early(self)
        self._sbuf_mark = len(self.ctx)
        S = self.sbuf
        self.W0T = S("W0T", [128, KD * MC])
        self.W1T = S("W1T", [128, 2 * D])
        self.W1n = S("W1n", [128, KD * MC], F32R)
        self.b0 = S("b0sb", [128, 2])
        self.b1 = S("b1sb", [1, D], F32R)
        self.lrfgT = S("lrfgT", [128, KD * 2])
        self.lrbcol = S("lrbcol", [128, 1])
        self.fgb11 = S("fgb11", [1, 1])
        self.lrb11 = S("lrb11", [1, 1])
        self.xraw = [S(f"xraw{i}", [128, D]) for i in range(3)]
        self.ctok = [S(f"ctok{i}", [128, D]) for i in range(2)]
        self.cT = [S(f"cTs{i}", [128, KD * 128]) for i in range(2)]
        self.zsb = [S(f"zsb{i}", [128, 2 * MC]) for i in range(2)]
        self.H2 = [S(f"H2sb{i}", [128, 2 * MC]) for i in range(2)]
        self.pred = S("predsb", [128, 2 * D])
        self.part = S("partsb", [128, 2 * D])
        self.dsb = S("dsb", [128, D], F32R)
        self.dT = S("dTsb", [128, KD * 128], F32R)
        self.hs = S("hssb", [128, MC], F32R)
        self.dzf = S("dzfsb", [128, MC], F32R)
        self.dzt = S("dztsb", [128, MC], F32R)
        self.x2sb = S("x2sb", [128, D])
        self.sg = S("sgsb", [128, 2 * 128])
        self.sp1 = S("sp1sb", [128, 2 * 128])
        self.sp2 = S("sp2sb", [128, 2 * 128])
        self.stat = S("statsb", [128, 8])
        self.gred = S("gredsb", [128, 2])
        self.scal = [S(f"scal{i}", [1, 4]) for i in range(3)]
        self.fk = [S(f"fkcol{i}", [128, 2]) for i in range(3)]
        self.k8m = [S(f"k8mat{i}", [128, 128], F32R) for i in range(3)]
        self.k8r = S("k8row", [1, 128])
        self.gb0c = S("gb0c", [128, 2])
        self.ctokr = S("ctokr", [128, D], F32R)

        for s in ("sx", "sxc0", "sxc1", "sxc2", "sbo", "sbi", "sx2", "cc"):
            self.new_sem(s)

    TTT_BIG = ("W0T", "W1T", "W1n", "xraw0", "xraw1", "xraw2", "ctok0",
               "ctok1", "cTs0", "cTs1", "zsb0", "zsb1", "H2sb0", "H2sb1",
               "predsb", "partsb", "dsb", "dTsb", "hssb", "dzfsb", "dztsb",
               "x2sb", "sgsb", "sp1sb", "sp2sb", "k8mat0", "k8mat1",
               "k8mat2", "ctokr", "lrfgT")

    def emit(self, step_hook=None):
        self.step_hook = step_hook
        if PHASE == 0:
            self.prologue_noflood()
            self.debug_exit()
            return
        self.prologue()
        if PHASE == 1:
            self.debug_exit()
            return
        for s in range(min(NCH, PHASE - 1)):
            self.step(s)
        if PHASE - 1 < NCH:
            self.debug_exit()
            return
        self.epilogue()

    def prologue_noflood(self):
        real_fwd = self.fwd
        self.fwd = lambda s: None
        try:
            self.prologue()
        finally:
            self.fwd = real_fwd

    def debug_exit(self):
        B = self
        B.wait("gpsimd", "cV", B.val("cV"))
        B.wait("gpsimd", "cT", B.val("cT"))
        B.wait("gpsimd", "cS", B.val("cS"))
        B.dma16("gpsimd", lambda e: e.dma_start(
            out=B.d_x2[0:128, :], in_=B.ctok[0][:, :]), "sx2")
        B.wait("gpsimd", "sx2", B.val("sx2"))

    def prologue(self):
        B = self

        def ld(e, dst, src, allow_slow_non_contiguous=False):
            return e.dma_start(out=dst, in_=src,
                               allow_slow_non_contiguous=allow_slow_non_contiguous)

        B.dma16("sync", lambda e: ld(
            e, B.W0T[:, :].rearrange("p (j m) -> p j m", m=MC),
            B.d_W0T.rearrange("(j p) m -> p j m", p=128)), "sx")
        B.dma16("sync", lambda e: ld(
            e, B.W1T[:, :].rearrange("p (t m) -> p t m", m=D),
            B.d_W1T.rearrange("(t p) m -> p t m", p=128)), "sx")
        B.dma16("sync", lambda e: ld(
            e, B.W1n[:, :].rearrange("p (j m) -> p j m", m=MC),
            B.d_W1n.rearrange("(j p) m -> p j m", p=128)), "sx")
        B.dma16("sync", lambda e: ld(
            e, B.b0[:, :], B.d_b0.rearrange("o (t p) -> (o p) t", p=128),
        allow_slow_non_contiguous=True), "sx")
        B.dma16("sync", lambda e: ld(e, B.b1[:, :], B.d_b1), "sx")
        B.dma16("sync", lambda e: ld(
            e, B.lrfgT[:, :].rearrange("p (j m) -> p j m", m=2),
            B.d_lrfgT.rearrange("(j p) m -> p j m", p=128)), "sx")
        B.dma16("sync", lambda e: ld(e, B.fgb11[:, :], B.d_fgb), "sx")
        B.dma16("sync", lambda e: ld(e, B.lrb11[:, :], B.d_lrb), "sx")
        ncst = B.load_common()
        nload = B.val("sx")
        B.wait("vector", "sconst", ncst)
        B.wait("tensor", "sconst", ncst)
        B.wait("scalar", "sconst", ncst)

        for c in range(2):
            B.chunk_sx[c] = ("sxc%d" % (c % 3), B.dma16("sync", lambda e, c=c: ld(
                e, B.xraw[c % 3][:, :], B.d_x[128 * c:128 * (c + 1), :]),
                "sxc%d" % (c % 3)))

        B.wait("vector", "sx", nload)
        B.em("vector", lambda e: e.tensor_scalar(
            B.b1[:, :], B.b1[:, :], 0.125, None, op0=ALU.mult))
        B.part_zero = B.tick(
            "vector", lambda e: e.memset(B.part[:, 0:D], 0.0), "cV")
        B.wait("tensor", "sx", nload)
        tb = B.t_mm(B.P6[:, 0:1], B.onesr[0:1, :], B.lrb11[0:1, :], True, True,
                    tick=True)
        B.wait("vector", "cT", tb)
        B.tick("vector", lambda e: e.tensor_copy(B.lrbcol[:, :], B.P6[:, 0:1]),
               "cV")

        self.prep(0)
        self.prep(1)
        self.fwd(0)

    def prep(self, c):
        B = self
        sl = c % 3
        sl2 = c % 2
        xr = B.xraw[sl]
        B.wait("scalar", B.chunk_sx[c][0], B.chunk_sx[c][1])
        if c >= 3:
            B.wait("scalar", "sx2", 16 * (c - 2))
        B.wait("scalar", "cV", B.x2_tick)
        ssq = B.tick("scalar", lambda e: e.activation(
            B.x2sb[:, :], xr[:, :], AF.Square), "cS")
        B.square_tick[c] = ssq
        B.wait("vector", "cS", ssq)
        B.wait("vector", "cT", B.fwd_z_tick.get(c - 1, 0))
        B.em("vector", lambda e: e.tensor_reduce(
            B.stat[:, 2:3], xr[:, :], axis=mybir.AxisListType.X, op=ALU.add))
        B.em("vector", lambda e: e.tensor_reduce(
            B.stat[:, 3:4], B.x2sb[:, :], axis=mybir.AxisListType.X, op=ALU.add))
        B.drain("vector")
        B.em("vector", lambda e: e.tensor_scalar(
            B.stat[:, 0:1], B.stat[:, 2:3], 1.0 / D, None, op0=ALU.mult))
        B.em("vector", lambda e: e.tensor_scalar(
            B.stat[:, 4:5], B.stat[:, 3:4], 1.0 / D, None, op0=ALU.mult))
        B.drain("vector")
        B.em("vector", lambda e: e.tensor_tensor(
            B.stat[:, 5:6], B.stat[:, 0:1], B.stat[:, 0:1], op=ALU.mult))
        B.drain("vector")
        vv = B.tick("vector", lambda e: e.tensor_tensor(
            B.stat[:, 6:7], B.stat[:, 4:5], B.stat[:, 5:6], op=ALU.subtract),
            "cV")
        B.wait("scalar", "cV", vv)
        ssd = B.tick("scalar", lambda e: e.activation(
            B.stat[:, 5:6], B.stat[:, 6:7], AF.Sqrt,
            bias=B.epscol[:, 0:1]), "cS")
        B.wait("vector", "cS", ssd)
        B.em("vector", lambda e: e.reciprocal(B.stat[:, 1:2], B.stat[:, 5:6]))
        B.drain("vector")
        vc = B.tick("vector", lambda e, sl2=sl2: e.tensor_scalar(
            B.ctok[sl2][:, :], xr[:, :], B.stat[:, 0:1], B.stat[:, 1:2],
            op0=ALU.subtract, op1=ALU.mult), "cV")

        B.wait("tensor", "cV", vc)
        evac_t = []
        for jj in range(KD):
            if jj >= 2:
                B.wait("tensor", "cV", evac_t[jj - 2])
            pb = (B.P4 if jj % 2 == 0 else B.P5)[:, 0:128]
            tt = B.t_tr(pb, B.ctok[sl2][:, 128 * jj:128 * (jj + 1)], f32r=False)
            B.wait("vector", "cT", tt)
            ev = B.tick("vector", lambda e, pb=pb, jj=jj, sl2=sl2: e.tensor_copy(
                B.cT[sl2][:, 128 * jj:128 * (jj + 1)], pb), "cV")
            evac_t.append(ev)

        B.wait("tensor", "cV", evac_t[-1])
        for jj in range(KD):
            B.t_mm(B.P6[:, 0:2], B.cT[sl2][:, 128 * jj:128 * (jj + 1)],
                   B.lrfgT[:, 2 * jj:2 * (jj + 1)], jj == 0, jj == KD - 1,
                   tick=(jj == KD - 1))
        tg = B.val("cT")
        B.wait("scalar", "cT", tg)
        B.em("scalar", lambda e: e.activation(
            B.gred[:, 0:1], B.P6[:, 0:1], AF.Sigmoid, bias=B.lrbcol[:, 0:1]))
        sg2 = B.tick("scalar", lambda e: e.copy(B.gred[:, 1:2], B.P6[:, 1:2]),
                     "cS")
        B.wait("tensor", "cS", sg2)
        tr2 = B.t_mm(B.P6[0:1, 4:6], B.onesc[:, 0:1], B.gred[:, :], True, True,
                     tick=True)
        sc = B.scal[sl]
        B.wait("scalar", "cT", tr2)
        sf = B.tick("scalar", lambda e, sc=sc: e.activation(
            sc[0:1, 0:1], B.P6[0:1, 5:6], AF.Sigmoid,
            bias=B.fgb11[0:1, 0:1], scale=B.inv128[0:1, 0:1]), "cS")
        B.wait("vector", "cS", sf)
        B.em("vector", lambda e, sc=sc: e.tensor_scalar(
            sc[0:1, 1:2], B.P6[0:1, 4:5], LR_MEM / 128.0 * GFAC, None,
            op0=ALU.mult))
        B.drain("vector")
        B.em("vector", lambda e, sc=sc: e.tensor_scalar(
            sc[0:1, 2:3], sc[0:1, 1:2], 0.125, None, op0=ALU.mult))
        B.drain("vector")
        vk = B.tick("vector", lambda e, sc=sc: e.tensor_scalar(
            B.k8r[0:1, :], B.onesr[0:1, :], sc[0:1, 2:3], None, op0=ALU.mult),
            "cV")
        B.wait("tensor", "cV", vk)
        B.t_mm(B.P6[:, 8:10], B.onesr[0:1, :], sc[0:1, 0:2], True, True)
        tbc = B.t_mm(B.P7[:, 0:128], B.onesr[0:1, :], B.k8r[0:1, :], True, True,
                     tick=True)
        B.wait("vector", "cT", tbc)
        B.em("vector", lambda e, sl=sl: e.tensor_copy(
            B.fk[sl][:, 0:2], B.P6[:, 8:10]))
        B.tick("vector", lambda e, sl=sl: e.tensor_copy(
            B.k8m[sl][:, :], B.P7[:, 0:128]), "cV")

    def fwd(self, s):
        B = self
        halves = []
        if s >= 1:
            halves.append((0, (s - 1) % 2))
        if s <= NCH - 1:
            halves.append((1, s % 2))
        slot = s % 2

        B.wait("tensor", "cV", B.val("cV"))
        first = True
        for jj in range(KD):
            for t2 in range(2):
                for h, csl in halves:
                    last = (jj == KD - 1 and t2 == 1 and (h, csl) == halves[-1])
                    B.t_mm(B.P0[:, 256 * t2 + 128 * h:256 * t2 + 128 * h + 128],
                           B.W0T[:, MC * jj + 128 * t2:MC * jj + 128 * t2 + 128],
                           B.cT[csl][:, 128 * jj:128 * (jj + 1)],
                           first, last, tick=last)
                    first = False
        tz = B.val("cT")
        B.fwd_z_tick[s] = tz
        B.wait("scalar", "cT", tz)
        B.wait("vector", "cT", tz)
        B.drain("vector")
        lo = 0 if s >= 1 else 128
        hi = 256 if s <= NCH - 1 else 128
        for t2 in range(2):
            def sil(e, t2=t2, slot=slot, lo=lo, hi=hi):
                return e.activation(
                    B.H2[slot][:, 256 * t2 + lo:256 * t2 + hi],
                    B.P0[:, 256 * t2 + lo:256 * t2 + hi], AF.Silu,
                    bias=B.b0[:, t2:t2 + 1])
            if t2 == 1:
                sh = B.tick("scalar", sil, "cS")
            else:
                B.em("scalar", sil)

            def zc(e, t2=t2, slot=slot, lo=lo, hi=hi):
                return e.tensor_scalar(
                    B.zsb[slot][:, 256 * t2 + lo:256 * t2 + hi],
                    B.P0[:, 256 * t2 + lo:256 * t2 + hi],
                    B.b0[:, t2:t2 + 1], None, op0=ALU.add)
            if t2 == 1:
                shv = B.tick("vector", zc, "cV")
            else:
                B.em("vector", zc)

        B.wait("tensor", "cS", sh)
        B.wait("tensor", "cV", shv)
        if s >= 1:
            B.wait("vector", "sbi", 32 * s)
        pv = []
        for hi, (h, csl) in enumerate(halves):
            for ch in range(4):
                pb = B.P1 if (hi * 4 + ch) % 2 == 0 else B.P2
                if len(pv) >= 2:
                    B.wait("tensor", "cV", pv[-2])
                for t2 in range(2):
                    B.t_mm(pb[:, :],
                           B.H2[slot][:, 256 * t2 + 128 * h:
                                      256 * t2 + 128 * h + 128],
                           B.W1T[:, D * t2 + 512 * ch:D * t2 + 512 * (ch + 1)],
                           t2 == 0, False)
                if SKIP_BIASMM:
                    tpm = B.t_mm(pb[:, :],
                                 B.H2[slot][:, 128 * h:128 * h + 128],
                                 B.W1T[:, 512 * ch:512 * (ch + 1)],
                                 False, True, tick=True)
                else:
                    tpm = B.tick("tensor", lambda e, pb=pb, ch=ch: e.matmul(
                        pb[:, :], B.onesrr[0:1, :],
                        B.b1[0:1, 512 * ch:512 * (ch + 1)],
                        start=False, stop=True), "cT")
                B.wait("vector", "cT", tpm)
                ev = B.tick("vector",
                            lambda e, pb=pb, h=h, ch=ch: e.tensor_copy(
                                B.part[:, D * h + 512 * ch:
                                       D * h + 512 * (ch + 1)], pb[:, :]),
                            "cV")
                pv.append(ev)
        B.wait("scalar", "cV", pv[-1])
        if s == 0:
            B.wait("scalar", "cV", B.part_zero)
        if s >= 2:
            B.wait("scalar", "cc", s - 1)
        for h in range(2):
            B.dma16("scalar", lambda e, h=h, slot=slot: e.dma_start(
                out=B.bin[slot][128 * h:128 * (h + 1), :],
                in_=B.part[:, D * h:D * (h + 1)]), "sbi")
        B.wait("gpsimd", "sbi", B.val("sbi"))
        if s >= 2:
            B.wait("gpsimd", "sbo", 32 * (s - 1))
        B.em("gpsimd", lambda e, slot=slot, sem=B.sems["cc"]: e.collective_compute(
            "AllReduce", ALU.add, replica_groups=[list(range(B.n_cores))],
            ins=[B.bin[slot].ap().opt()],
            outs=[B.bout[slot].ap().opt()]).then_inc(sem, 1))
        B.inc("cc", 1)

        B.wait("sync", "cc", B.val("cc"))
        B.wait("sync", "cV", B.pred_free)
        for h in range(2):
            B.dma16("sync", lambda e, h=h, slot=slot: e.dma_start(
                out=B.pred[:, D * h:D * (h + 1)],
                in_=B.bout[slot][128 * h:128 * (h + 1), :]), "sbo")

    def step(self, s):
        B = self
        slot = s % 2
        csl = s % 3
        csl2 = s % 2
        kap_ap = B.fk[csl][:, 1:2]
        f_ap = B.fk[csl][:, 0:1]
        f11 = B.scal[csl][0:1, 0:1]

        if self.step_hook is not None:
            self.step_hook(s, 0)

        B.drain("vector")
        B.wait("vector", "sbo", 32 * (s + 1))
        B.wait("vector", "cT", B.dsb_free)
        vd = B.tick("vector", lambda e, csl2=csl2: e.tensor_tensor(
            B.dsb[:, :], B.pred[:, D:2 * D], B.ctok[csl2][:, :],
            op=ALU.subtract), "cV")
        if s >= 1:
            if s >= 2:
                B.wait("vector", "sx2", 16 * (s - 1))
            B.wait("vector", "cS", B.square_tick.get(s + 1, 0))
            vx2 = B.tick("vector", lambda e, s=s: e.tensor_tensor(
                B.x2sb[:, :], B.xraw[(s - 1) % 3][:, :],
                B.pred[:, 0:D], op=ALU.add), "cV")
            B.wait("gpsimd", "cV", vx2)
            B.dma16("gpsimd", lambda e, s=s: e.dma_start(
                out=B.d_x2[128 * (s - 1):128 * s, :], in_=B.x2sb[:, :]), "sx2")
            B.pred_free = vx2
            B.x2_tick = vx2
        else:
            B.pred_free = vd

        if s + 2 <= NCH - 1:
            if s >= 1:
                B.wait("sync", "sx2", 16 * s)
            B.chunk_sx[s + 2] = ("sxc%d" % ((s + 2) % 3),
                                 B.dma16("sync", lambda e, s=s: e.dma_start(
                                     out=B.xraw[(s + 2) % 3][:, :],
                                     in_=B.d_x[128 * (s + 2):128 * (s + 3), :]),
                                     "sxc%d" % ((s + 2) % 3)))

        B.wait("tensor", "cV", vd)
        evt = []
        for jj in range(KD):
            if jj >= 2:
                B.wait("tensor", "cV", evt[jj - 2])
            pb = (B.P4 if jj % 2 == 0 else B.P5)[:, 0:128]
            tt = B.t_tr(pb, B.dsb[:, 128 * jj:128 * (jj + 1)], f32r=True)
            B.wait("vector", "cT", tt)
            ev = B.tick("vector", lambda e, pb=pb, jj=jj: e.tensor_copy(
                B.dT[:, 128 * jj:128 * (jj + 1)], pb), "cV")
            evt.append(ev)

        if SUB <= 2:
            return
        B.tick("vector", lambda e, csl2=csl2: e.tensor_copy(
            B.ctokr[:, :], B.ctok[csl2][:, :]), "cV")

        B.wait("tensor", "cV", evt[-1])
        for t2 in range(2):
            pb = (B.P4 if t2 % 2 == 0 else B.P5)[:, 128:256]
            th = B.t_tr(pb, B.H2[slot][:, 256 * t2 + 128:256 * t2 + 256],
                        f32r=False)
            B.wait("vector", "cT", th)
            B.tick("vector", lambda e, pb=pb, t2=t2, kap_ap=kap_ap:
                   e.tensor_scalar(B.hs[:, 128 * t2:128 * (t2 + 1)], pb,
                                   kap_ap, None, op0=ALU.mult), "cV")
        sh = B.val("cV")

        if SUB <= 3:
            return
        B.wait("tensor", "cV", evt[-1])
        first = True
        for jj in range(KD):
            for mc in range(2):
                last = (jj == KD - 1 and mc == 1)
                B.t_mm(B.P3[:, 128 * mc:128 * (mc + 1)],
                       B.W1n[:, MC * jj + 128 * mc:MC * jj + 128 * mc + 128],
                       B.dT[:, 128 * jj:128 * (jj + 1)],
                       first, last, tick=last)
                first = False
        tdh = B.val("cT")
        B.tdh_tick[s] = tdh

        if SUB <= 4:
            return
        B.wait("tensor", "cV", sh)
        gwv = []
        for t2 in range(2):
            for ch in range(4):
                pb = B.P1 if (t2 * 4 + ch) % 2 == 0 else B.P2
                if len(gwv) >= 2:
                    B.wait("tensor", "cV", gwv[-2])
                tm = B.t_mm(pb[:, :],
                            B.hs[:, 128 * t2:128 * (t2 + 1)],
                            B.dsb[:, 512 * ch:512 * (ch + 1)],
                            True, True, tick=True)
                B.wait("vector", "cT", tm)
                ev = B.tick(
                    "vector",
                    lambda e, pb=pb, t2=t2, ch=ch, f_ap=f_ap:
                    e.scalar_tensor_tensor(
                        B.W1T[:, D * t2 + 512 * ch:D * t2 + 512 * (ch + 1)],
                        B.W1T[:, D * t2 + 512 * ch:D * t2 + 512 * (ch + 1)],
                        f_ap, pb[:, :], op0=ALU.mult, op1=ALU.subtract), "cV")
                gwv.append(ev)

        if SUB <= 5:
            return
        if self.step_hook is not None:
            self.step_hook(s, 1)
        B.wait("scalar", "cT", tdh)
        for t2 in range(2):
            zh = B.zsb[slot][:, 256 * t2 + 128:256 * t2 + 256]
            B.tick("scalar", lambda e, t2=t2, zh=zh: e.activation(
                B.sg[:, 128 * t2:128 * (t2 + 1)], zh, AF.Sigmoid), "cS")
        ssg = B.val("cS")
        B.wait("vector", "cS", ssg)
        for t2 in range(2):
            zh = B.zsb[slot][:, 256 * t2 + 128:256 * t2 + 256]
            sgh = B.sg[:, 128 * t2:128 * (t2 + 1)]
            s1 = B.sp1[:, 128 * t2:128 * (t2 + 1)]
            s2 = B.sp2[:, 128 * t2:128 * (t2 + 1)]
            dzh = B.dzf[:, 128 * t2:128 * (t2 + 1)]
            B.em("vector", lambda e, zh=zh, sgh=sgh, s1=s1: e.tensor_tensor(
                s1, zh, sgh, op=ALU.mult))
            B.drain("vector")
            B.em("vector", lambda e, sgh=sgh, s1=s1, s2=s2: e.tensor_tensor(
                s2, s1, sgh, op=ALU.mult))
            B.drain("vector")
            B.em("vector", lambda e, s1=s1, s2=s2: e.tensor_tensor(
                s1, s1, s2, op=ALU.subtract))
            B.drain("vector")
            B.em("vector", lambda e, sgh=sgh, s1=s1: e.tensor_tensor(
                s1, sgh, s1, op=ALU.add))
            B.drain("vector")
            B.tick("vector", lambda e, t2=t2, s1=s1, dzh=dzh: e.tensor_tensor(
                dzh, B.P3[:, 128 * t2:128 * (t2 + 1)], s1, op=ALU.mult), "cV")
        vdz = B.val("cV")

        if SUB <= 6:
            return
        B.wait("tensor", "cV", vdz)
        for t2 in range(2):
            pb = (B.P4 if t2 % 2 == 0 else B.P5)[:, 256:384]
            tt = B.t_tr(pb, B.dzf[:, 128 * t2:128 * (t2 + 1)], f32r=True)
            B.wait("vector", "cT", tt)
            B.tick("vector", lambda e, pb=pb, t2=t2, kap_ap=kap_ap:
                   e.tensor_scalar(B.dzt[:, 128 * t2:128 * (t2 + 1)], pb,
                                   kap_ap, None, op0=ALU.mult), "cV")
        sdz = B.val("cV")

        B.wait("tensor", "cV", sdz)
        g0v = []
        for jj in range(KD):
            pb = B.P1 if jj % 2 == 0 else B.P2
            if len(g0v) >= 2:
                B.wait("tensor", "cV", g0v[-2])
            tm = B.t_mm(pb[:, 0:256],
                        B.ctokr[:, 128 * jj:128 * (jj + 1)],
                        B.dzt[:, :], True, True, tick=True)
            B.wait("vector", "cT", tm)
            ev = B.tick("vector", lambda e, pb=pb, jj=jj, f_ap=f_ap:
                        e.scalar_tensor_tensor(
                            B.W0T[:, MC * jj:MC * (jj + 1)],
                            B.W0T[:, MC * jj:MC * (jj + 1)],
                            f_ap, pb[:, 0:256],
                            op0=ALU.mult, op1=ALU.subtract), "cV")
            g0v.append(ev)
        w0done = g0v[-1]

        if SUB <= 7:
            return
        gb1v = []
        for ch in range(4):
            if len(gb1v) >= 1:
                B.wait("tensor", "cV", gb1v[-1])
            tm = B.t_mm(B.P7[:, 0:512], B.k8m[csl][:, :],
                        B.dsb[:, 512 * ch:512 * (ch + 1)],
                        True, True, tick=True)
            B.wait("vector", "cT", tm)
            ev = B.tick("vector", lambda e, ch=ch, f11=f11:
                        e.scalar_tensor_tensor(
                            B.b1[0:1, 512 * ch:512 * (ch + 1)],
                            B.b1[0:1, 512 * ch:512 * (ch + 1)],
                            f11, B.P7[0:1, 0:512],
                            op0=ALU.mult, op1=ALU.subtract), "cV")
            gb1v.append(ev)
        B.dsb_free = B.val("cT")

        B.drain("vector")
        for t2 in range(2):
            B.em("vector", lambda e, t2=t2: e.tensor_reduce(
                B.gb0c[:, t2:t2 + 1], B.dzf[:, 128 * t2:128 * (t2 + 1)],
                axis=mybir.AxisListType.X, op=ALU.add))
        B.drain("vector")
        B.em("vector", lambda e, kap_ap=kap_ap: e.tensor_scalar(
            B.gb0c[:, 0:2], B.gb0c[:, 0:2], kap_ap, None, op0=ALU.mult))
        B.drain("vector")
        B.tick("vector", lambda e, f_ap=f_ap: e.scalar_tensor_tensor(
            B.b0[:, 0:2], B.b0[:, 0:2], f_ap, B.gb0c[:, 0:2],
            op0=ALU.mult, op1=ALU.subtract), "cV")
        bdone = B.val("cV")

        if SUB <= 8:
            return
        if self.step_hook is not None:
            self.step_hook(s, 2)
        B.wait("tensor", "cV", bdone)
        self.fwd(s + 1)
        if s + 2 <= NCH - 1:
            self.prep(s + 2)

    def epilogue(self):
        B = self
        s = NCH
        slot = s % 2
        B.wait("vector", "sbo", 32 * (s + 1))
        B.wait("vector", "sx2", 16 * (s - 1))
        vx2 = B.tick("vector", lambda e: e.tensor_tensor(
            B.x2sb[:, :], B.xraw[(s - 1) % 3][:, :],
            B.pred[:, 0:D], op=ALU.add), "cV")
        B.wait("gpsimd", "cV", vx2)
        B.dma16("gpsimd", lambda e: e.dma_start(
            out=B.d_x2[128 * (NCH - 1):128 * NCH, :], in_=B.x2sb[:, :]), "sx2")
        B.wait("gpsimd", "sx2", B.val("sx2"))


def host_inputs(x, W0, b0, W1, b1, lr_w, lr_b, fg_w, fg_b, core):
    r = core
    msl = slice(MC * r, MC * (r + 1))
    return {
        "ttt_x": np.ascontiguousarray(x.reshape(NCH * CK, D)),
        "ttt_W0T": np.ascontiguousarray(W0[msl, :].T),
        "ttt_b0": np.ascontiguousarray(b0[msl].reshape(1, MC)),
        "ttt_W1T": np.ascontiguousarray(W1[:, msl].T),
        "ttt_W1n": np.ascontiguousarray(W1[:, msl]),
        "ttt_b1": np.ascontiguousarray(b1.reshape(1, D)),
        "ttt_lrfgT": np.ascontiguousarray(
            np.stack([lr_w.reshape(D), fg_w.reshape(D)], axis=1)),
        "ttt_lrb": np.asarray(lr_b, np.float32).reshape(1, 1),
        "ttt_fgb": np.asarray(fg_b, np.float32).reshape(1, 1),
        **common_inputs(),
    }


import numpy as np
import bass_rust
import concourse.mybir as mybir

F32 = mybir.dt.float32
BF16 = mybir.dt.bfloat16
AF = mybir.ActivationFunctionType
ALU = mybir.AluOpType

D = 2048
FF = 8192
TB = 1024
NT = 8
KD = 16
KF = 64
NW = 1.0 / (FF * D)


class PWStream:
    NSLOT = 4

    def __init__(self, core):
        self.B = core
        self.nc = core.nc

    def declare(self):
        nc = self.nc
        B = self.B
        dt = nc.dram_tensor
        self.d_w1T = dt("ffn_w1T", [D, FF], F32, kind="ExternalInput").ap()
        self.d_w2T = dt("ffn_w2T", [FF, D], F32, kind="ExternalInput").ap()
        self.pwbuf = B.sbuf("pw_buf", [128, 1024])
        self.pacc = B.sbuf("pw_acc", [128, 4])
        for i in range(2):
            B.new_sem("pw%d" % i)
        self.chunks = []
        for j in range(D // 128):
            for k in range(FF // 256):
                self.chunks.append(
                    (self.d_w1T[128 * j:128 * (j + 1), 256 * k:256 * (k + 1)],
                     0))
        for j in range(FF // 128):
            for k in range(D // 256):
                self.chunks.append(
                    (self.d_w2T[128 * j:128 * (j + 1), 256 * k:256 * (k + 1)],
                     1))
        self.npairs = len(self.chunks) // 2
        self.ci = 0
        self.ri = 0
        self.red_tick = {}
        self._init_done = False

    def _trigger_pair(self):
        B = self.B
        p = self.ci
        q = p % 2
        if p >= 2:
            B.wait("scalar", "cV", self.red_tick.pop(p - 2))
        for h in range(2):
            src, _ = self.chunks[2 * p + h]
            B.dma16("scalar", lambda e, src=src, q=q, h=h: e.dma_start(
                out=self.pwbuf[:, 512 * q + 256 * h:512 * q + 256 * (h + 1)],
                in_=src), "pw%d" % q)
        self.ci += 1

    def _reduce_pair(self):
        B = self.B
        p = self.ri
        _, col = self.chunks[2 * p]
        q = p % 2
        B.wait("vector", "pw%d" % q, 32 * (p // 2 + 1))
        B.drain("vector")
        B.em("vector", lambda e, q=q: e.tensor_reduce(
            self.pacc[:, 2:3], self.pwbuf[:, 512 * q:512 * (q + 1)],
            axis=mybir.AxisListType.X, op=ALU.add,
            apply_absolute_value=True))
        B.drain("vector")
        t = B.tick("vector", lambda e, col=col: e.tensor_tensor(
            self.pacc[:, col:col + 1], self.pacc[:, col:col + 1],
            self.pacc[:, 2:3], op=ALU.add), "cV")
        self.red_tick[p] = t
        self.ri += 1

    def hook(self, quota):
        B = self.B
        if not self._init_done:
            self._init_done = True
            B.tick("vector", lambda e: e.memset(self.pacc[:, :], 0.0), "cV")
            B.drain("vector")
            for _ in range(min(2, self.npairs)):
                self._trigger_pair()
        n = 0
        while n < quota and self.ri < self.npairs:
            self._reduce_pair()
            if self.ci < self.npairs:
                self._trigger_pair()
            n += 1

    def finale(self):
        while self.ri < self.npairs:
            self._reduce_pair()


class FFNBuilder:
    def __init__(self, core_builder):
        self.B = core_builder
        self.nc = core_builder.nc

    def declare(self, pw=None):
        nc = self.nc
        B = self.B
        dt = nc.dram_tensor
        if pw is not None:
            self.d_w1T = pw.d_w1T
            self.d_w2T = pw.d_w2T
        else:
            self.d_w1T = dt("ffn_w1T", [D, FF], F32,
                            kind="ExternalInput").ap()
            self.d_w2T = dt("ffn_w2T", [FF, D], F32,
                            kind="ExternalInput").ap()
        self.d_b1 = dt("ffn_b1", [1, FF], F32, kind="ExternalInput").ap()
        self.d_gamma = dt("ffn_gamma", [1, FF], F32, kind="ExternalInput").ap()
        self.d_beta = dt("ffn_beta", [1, FF], F32, kind="ExternalInput").ap()
        self.d_b2 = dt("ffn_b2", [1, D], F32, kind="ExternalInput").ap()
        self.d_xrows = dt("ffn_xrows", [128, NT], mybir.dt.int32,
                          kind="ExternalInput").ap()
        self.d_xrows16 = dt("ffn_xrows16", [128, NT], mybir.dt.int32,
                            kind="ExternalInput").ap()
        self.d_yT = dt("ffn_yT", [D, TB], F32, kind="ExternalOutput").ap()
        self.d_h = dt("ffn_hscr", [TB, FF], F32)

        S = B.sbuf
        self.hqT = S("f_hqT", [128, KF * TB], BF16)
        self.wf1 = [S(f"f_wf1_{i}", [128, KD * 128]) for i in range(2)]
        self.w1q = [S(f"f_w1q_{i}", [128, KD * 128], BF16) for i in range(2)]
        self.qa = S("f_qa", [128, KD * 128], BF16)
        self.s1 = S("f_s1", [128, D])
        self.s2 = S("f_s2", [128, D])
        self.qbf = S("f_qbf", [128, D], BF16)
        self.bnb = S("f_bnb", [128, NT * KF * 6])
        _qaf = self.qa.bitcast(F32)
        self.gbc = _qaf[:, 0:512]
        self.bbc = _qaf[:, 512:1024]
        self.htile = [S(f"f_ht{i}", [128, 128]) for i in range(2)]
        self.hgl = [S(f"f_hg{i}", [128, 128]) for i in range(2)]
        self.b1bc = S("f_b1bc", [128, 128])
        self.fst = S("f_stat", [128, 16])
        self.bx = S("f_bx", [128, NT])
        self.mv = S("f_mv", [128, NT * 2])
        self.mu = S("f_mu", [128, NT])
        self.rho = S("f_rho", [128, NT])
        self.mx2 = S("f_mx2", [128, NT])
        self.s2q = S("f_s2q", [128, NT])
        self.b2neg = S("f_b2neg", [128, NT])
        self.thc = S("f_thc", [128, 4])
        self.swc = S("f_swc", [128, 2])
        self.scl = S("f_scl", [1, 8])
        self.b2row = S("f_b2row", [1, TB])
        self.row8 = S("f_row8", [8, 128])
        self.b2c = S("f_b2c", [128, KD])
        self.bet2b = self.s2[:, TB:2 * TB]
        self.ytT = self.s2[:, 0:TB]
        self.x2c = self.s1[:, 0:NT * 128]
        self.rs1 = S("f_rs1", [1, 512])
        self.rs2 = S("f_rs2", [1, 512])
        self.xrows = S("f_xrows_sb", [128, NT], mybir.dt.int32)
        self.xrows16 = S("f_xrows16_sb", [128, NT], mybir.dt.int32)
        self.xqT = self.hqT

        for s in ("fw0", "fw1", "fh0", "fh1", "fo", "fo0", "fo1", "fx",
                  "fy", "fr", "fxr", "fg"):
            B.new_sem(s)

    def emit(self, barrier=None, pw=None):
        B = self.B
        if barrier:
            for eng in ("sync", "gpsimd", "vector", "scalar", "tensor"):
                for sem_name, v in barrier.items():
                    B.wait(eng, sem_name, v)
        B.dma16("sync", lambda e: e.dma_start(
            out=self.xrows[:, :], in_=self.d_xrows), "fxr")
        B.dma16("sync", lambda e: e.dma_start(
            out=self.xrows16[:, :], in_=self.d_xrows16), "fxr")
        self._xrows_fr = B.val("fxr")
        if pw is not None:
            self.phaseW_finale(pw)
        else:
            self.phaseW()
        self.phaseX()
        self.phaseF1()
        self.phaseF3()
        self.phaseF4()
        self.phaseF5()

    def phaseW(self):
        B = self.B
        FB = self
        ms = B.tick("vector", lambda e: e.memset(FB.fst[:, :], 0.0), "cV")
        B.wait("vector", "cV", ms)
        self.rd_tick = {0: 0, 1: 0}
        nld = {0: 0, 1: 0}

        def stream(src_ap, acc_col, i, total_i):
            sl = i % 2
            sem = "fw%d" % sl
            if self.rd_tick[sl]:
                B.wait("sync", "cV", self.rd_tick[sl])
            nld[sl] += 1
            B.dma16("sync", lambda e, src_ap=src_ap, sl=sl: e.dma_start(
                out=FB.wf1[sl][:, :], in_=src_ap), sem)
            B.wait("vector", sem, 16 * nld[sl])
            B.em("vector", lambda e, sl=sl: e.tensor_reduce(
                FB.fst[:, 2:3], FB.wf1[sl][:, :], axis=mybir.AxisListType.X,
                op=ALU.add, apply_absolute_value=True))
            B.drain("vector")
            t = B.tick("vector", lambda e, acc_col=acc_col: e.tensor_tensor(
                FB.fst[:, acc_col:acc_col + 1], FB.fst[:, acc_col:acc_col + 1],
                FB.fst[:, 2:3], op=ALU.add), "cV")
            B.drain("vector")
            self.rd_tick[sl] = t

        i = 0
        for dd in range(KD):
            for cc in range(4):
                stream(self.d_w1T[128 * dd:128 * (dd + 1),
                                  2048 * cc:2048 * (cc + 1)], 0, i, 128)
                i += 1
        for fj in range(KF):
            stream(self.d_w2T[128 * fj:128 * (fj + 1), :], 1, i, 128)
            i += 1
        self.w_nld = dict(nld)
        self._w_tail(FB.fst[:, 0:2])

    def phaseW_finale(self, pw):
        pw.finale()
        self.rd_tick = {0: 0, 1: 0}
        self.w_nld = {0: 0, 1: 0}
        self._w_tail(pw.pacc[:, 0:2])

    def _w_tail(self, acc_ap):
        B = self.B
        FB = self
        B.wait("tensor", "cV", B.val("cV"))
        tm = B.t_mm(B.P6[0:1, 0:2], B.onesc[:, 0:1], acc_ap,
                    True, True, tick=True)
        B.wait("vector", "cT", tm)
        B.em("vector", lambda e: e.tensor_scalar(
            FB.scl[0:1, 0:2], B.P6[0:1, 0:2], NW, 1e-5,
            op0=ALU.mult, op1=ALU.max))
        B.drain("vector")
        B.em("vector", lambda e: e.tensor_scalar(
            FB.scl[0:1, 2:4], FB.scl[0:1, 0:2], 0.5, None, op0=ALU.mult))
        B.drain("vector")
        t = B.tick("vector", lambda e: e.tensor_scalar(
            FB.scl[0:1, 4:6], FB.scl[0:1, 2:4], -1.0, None, op0=ALU.mult),
            "cV")
        B.wait("tensor", "cV", t)
        B.t_mm(B.P6[:, 4:8], B.onesr[0:1, :], FB.scl[0:1, 2:6], True, True)
        tb = B.t_mm(B.P6[:, 8:10], B.onesr[0:1, :], FB.scl[0:1, 0:2],
                    True, True, tick=True)
        B.wait("vector", "cT", tb)
        B.em("vector", lambda e: e.tensor_copy(FB.thc[:, 0:4], B.P6[:, 4:8]))
        self.w_done = B.tick("vector", lambda e: e.tensor_copy(
            FB.swc[:, 0:2], B.P6[:, 8:10]), "cV")

    def phaseX(self):
        B = self.B
        FB = self
        self.xev = [0] * KD
        nfx = B.val("fx")
        B.wait("gpsimd", "fxr", self._xrows_fr)
        for tt in range(NT):
            B.wait("gpsimd", "cV", getattr(self, "_x_s1r", 0))
            nfx = B.dma16("gpsimd", lambda e, tt=tt: e.indirect_dma_start(
                out=FB.s1[:, :], out_offset=None,
                in_=B.d_x2,
                in_offset=bass_rust.IndirectOffsetOnAxis(
                    ap=FB.xrows[:, tt:tt + 1], axis=0)), "fg")
            B.wait("vector", "fg", nfx)
            B.drain("vector")
            for q in range(4):
                B.em("vector", lambda e, q=q, tt=tt: e.bn_stats(
                    FB.bnb[:, 24 * tt + 6 * q:24 * tt + 6 * (q + 1)],
                    FB.s1[:, 512 * q:512 * (q + 1)]))
            B.drain("vector")
            vb = B.tick("vector", lambda e, tt=tt: e.bn_aggr(
                FB.mv[:, 2 * tt:2 * (tt + 1)],
                FB.bnb[:, 24 * tt:24 * tt + 24]), "cV")
            B.wait("scalar", "cV", vb)
            sq = B.tick("scalar", lambda e, tt=tt: e.activation(
                FB.rho[:, tt:tt + 1], FB.mv[:, 2 * tt + 1:2 * tt + 2],
                AF.Sqrt, bias=B.epscol[:, 0:1]), "cS")
            B.wait("vector", "cS", sq)
            B.em("vector", lambda e, tt=tt: e.reciprocal(
                FB.rho[:, tt:tt + 1], FB.rho[:, tt:tt + 1]))
            B.drain("vector")
            B.em("vector", lambda e, tt=tt: e.tensor_scalar(
                FB.s2[:, :], FB.s1[:, :], FB.mv[:, 2 * tt:2 * tt + 1],
                FB.rho[:, tt:tt + 1], op0=ALU.subtract, op1=ALU.mult))
            B.drain("vector")
            B.em("vector", lambda e: e.tensor_reduce(
                FB.fst[:, 4:5], FB.s2[:, :], axis=mybir.AxisListType.X,
                op=ALU.max, apply_absolute_value=True))
            B.drain("vector")
            B.em("vector", lambda e: e.tensor_scalar(
                FB.fst[:, 5:6], FB.fst[:, 4:5], 1e-5, None, op0=ALU.max))
            B.drain("vector")
            if tt == 0:
                B.wait("vector", "cV", self.w_done)
            B.em("vector", lambda e, tt=tt: e.scalar_tensor_tensor(
                FB.bx[:, tt:tt + 1], FB.fst[:, 5:6], -1.0 / 127.0,
                FB.swc[:, 0:1], op0=ALU.mult, op1=ALU.mult))
            B.em("vector", lambda e: e.reciprocal(
                FB.fst[:, 6:7], FB.fst[:, 5:6]))
            B.drain("vector")
            B.em("vector", lambda e: e.tensor_scalar(
                FB.fst[:, 7:8], FB.fst[:, 6:7], 127.0, None, op0=ALU.mult))
            B.drain("vector")
            B.em("vector", lambda e: e.tensor_scalar(
                FB.s1[:, :], FB.s2[:, :], FB.fst[:, 7:8], None,
                op0=ALU.mult))
            B.drain("vector")
            B.em("vector", lambda e: e.tensor_copy(
                FB.s2[:, :].bitcast(mybir.dt.int32), FB.s1[:, :]))
            B.drain("vector")
            vq = B.tick("vector", lambda e: e.tensor_copy(
                FB.qbf[:, :], FB.s2[:, :].bitcast(mybir.dt.int32)), "cV")
            self._x_s1r = vq
            B.wait("tensor", "cV", vq)
            for dd in range(KD):
                pb = (B.P4 if dd % 2 == 0 else B.P5)[:, 0:64].bitcast(BF16)
                if self.xev[dd - 2] if dd >= 2 else 0:
                    B.wait("tensor", "cV", self.xev[dd - 2])
                tk = B.tick("tensor", lambda e, pb=pb, dd=dd: e.transpose(
                    pb, FB.qbf[:, 128 * dd:128 * (dd + 1)],
                    B.identb[:, :]), "cT")
                B.wait("vector", "cT", tk)
                self.xev[dd] = B.tick(
                    "vector", lambda e, pb=pb, dd=dd, tt=tt: e.tensor_copy(
                        FB.xqT[:, TB * dd + 128 * tt:TB * dd + 128 * (tt + 1)],
                        pb), "cV")
        self.x_done = B.val("cV")

    def phaseF1(self):
        B = self.B
        FB = self
        q_tick = {0: 0, 1: 0}
        mm_tick = {0: 0, 1: 0}
        self.f1_nld = {0: self.w_nld[0], 1: self.w_nld[1]}
        h_ev = [0, 0]
        pv_ev = [0, 0]
        fo_at = [0, 0]
        hst = 0
        self._rs1_free = 0
        for fc in range(KF):
            sl = fc % 2
            sem = "fw%d" % sl
            if q_tick[sl]:
                B.wait("sync", "cV", q_tick[sl])
            elif self.rd_tick[sl]:
                B.wait("sync", "cV", self.rd_tick[sl])
            self.f1_nld[sl] += 1
            B.dma16("sync", lambda e, fc=fc, sl=sl: e.dma_start(
                out=FB.wf1[sl][:, :].rearrange("p (j m) -> p j m", m=128),
                in_=FB.d_w1T.rearrange("(j p) m -> p j m", p=128)
                [:, :, 128 * fc:128 * (fc + 1)]), sem)
            B.wait("vector", sem, 16 * self.f1_nld[sl])
            if fc == 0:
                B.wait("vector", "cV", self.x_done)
            if mm_tick[sl]:
                B.wait("vector", "cT", mm_tick[sl])
            B.drain("vector")
            B.em("vector", lambda e, sl=sl: e.tensor_scalar(
                FB.qa[:, :], FB.wf1[sl][:, :], FB.thc[:, 0:1], None,
                op0=ALU.is_gt))
            B.drain("vector")
            qt = B.tick("vector", lambda e, sl=sl: e.scalar_tensor_tensor(
                FB.w1q[sl][:, :], FB.wf1[sl][:, :], FB.thc[:, 2:3],
                FB.qa[:, :], op0=ALU.is_lt, op1=ALU.subtract), "cV")
            q_tick[sl] = qt
            B.wait("sync", "cT", self._rs1_free)
            nfr = B.dma16("sync", lambda e, fc=fc: e.dma_start(
                out=FB.rs1[0:1, 0:128],
                in_=FB.d_b1[0:1, 128 * fc:128 * (fc + 1)]), "fr")
            B.wait("tensor", "cV", qt)
            B.wait("tensor", "fr", nfr)
            bb = B.t_mm(B.P6[:, 0:128], B.onesr[0:1, :],
                        FB.rs1[0:1, 0:128], True, True, tick=True)
            self._rs1_free = bb
            B.wait("vector", "cT", bb)
            bbv = B.tick("vector", lambda e: e.tensor_copy(
                FB.b1bc[:, :], B.P6[:, 0:128]), "cV")
            B.wait("tensor", "cV", bbv)
            for tt in range(NT):
                pb = B.P1 if tt % 2 == 0 else B.P2
                if hst >= 2:
                    B.wait("tensor", "cV", pv_ev[hst % 2])
                for dd in range(KD):
                    B.t_mm(pb[:, 0:128],
                           FB.xqT[:, TB * dd + 128 * tt:
                                  TB * dd + 128 * (tt + 1)],
                           FB.w1q[sl][:, 128 * dd:128 * (dd + 1)],
                           dd == 0, dd == KD - 1, tick=(dd == KD - 1))
                mt = B.val("cT")
                if tt == NT - 1:
                    mm_tick[sl] = mt
                hb = hst % 2
                B.wait("vector", "cT", mt)
                if hst >= 2:
                    B.wait("vector", "cS", h_ev[hb])
                B.drain("vector")
                ve = B.tick("vector", lambda e, pb=pb, hb=hb, tt=tt:
                            e.scalar_tensor_tensor(
                                FB.htile[hb][:, :], pb[:, 0:128],
                                FB.bx[:, tt:tt + 1], FB.b1bc[:, :],
                                op0=ALU.mult, op1=ALU.add), "cV")
                pv_ev[hb] = ve
                B.wait("scalar", "cV", ve)
                if hst >= 2:
                    B.wait("scalar", "fo%d" % hb, fo_at[hb])
                sg = B.tick("scalar", lambda e, hb=hb: e.activation(
                    FB.hgl[hb][:, :], FB.htile[hb][:, :], AF.Gelu), "cS")
                h_ev[hb] = sg
                B.wait("vector", "cS", sg)
                B.drain("vector")
                B.em("vector", lambda e, hb=hb, tt=tt, fc=fc: e.bn_stats(
                    FB.bnb[:, (tt * KF + fc) * 6:(tt * KF + fc) * 6 + 6],
                    FB.hgl[hb][:, :]))
                B.wait("scalar", "cS", sg)
                fo_at[hb] = B.dma16(
                    "scalar", lambda e, hb=hb, tt=tt, fc=fc: e.dma_start(
                        out=FB.d_h[128 * tt:128 * (tt + 1),
                                   128 * fc:128 * (fc + 1)],
                        in_=FB.hgl[hb][:, :]), "fo%d" % hb)
                hst += 1
        self.f1_fo = (B.val("fo0"), B.val("fo1"))
        self.f1_q = dict(q_tick)
        self.f1_mm = dict(mm_tick)

    def phaseF3(self):
        B = self.B
        FB = self
        B.drain("vector")
        for tt in range(NT):
            vb = B.tick("vector", lambda e, tt=tt: e.bn_aggr(
                FB.mv[:, 2 * tt:2 * (tt + 1)],
                FB.bnb[:, tt * KF * 6:(tt + 1) * KF * 6]), "cV")
            B.drain("vector")
            B.em("vector", lambda e, tt=tt: e.tensor_copy(
                FB.mu[:, tt:tt + 1], FB.mv[:, 2 * tt:2 * tt + 1]))
            B.wait("scalar", "cV", B.val("cV"))
            sq = B.tick("scalar", lambda e, tt=tt: e.activation(
                FB.rho[:, tt:tt + 1], FB.mv[:, 2 * tt + 1:2 * tt + 2],
                AF.Sqrt, bias=B.epscol[:, 0:1]), "cS")
            B.wait("vector", "cS", sq)
            B.em("vector", lambda e, tt=tt: e.reciprocal(
                FB.rho[:, tt:tt + 1], FB.rho[:, tt:tt + 1]))
            B.drain("vector")
        mz = B.tick("vector", lambda e: e.memset(FB.mx2[:, :], 0.0), "cV")
        B.wait("vector", "cV", mz)
        B.wait("vector", "fo0", self.f1_fo[0])
        B.wait("vector", "fo1", self.f1_fo[1])
        nfx = B.val("fx")
        nfo = B.val("fo")
        for gc in range(KF // 4):
            B.wait("sync", "cT", getattr(self, "_rs1_free", 0))
            nfr = B.dma16("sync", lambda e, gc=gc: e.dma_start(
                out=FB.rs1[0:1, :],
                in_=FB.d_gamma[0:1, 512 * gc:512 * (gc + 1)]), "fr")
            nfr = B.dma16("sync", lambda e, gc=gc: e.dma_start(
                out=FB.rs2[0:1, :],
                in_=FB.d_beta[0:1, 512 * gc:512 * (gc + 1)]), "fr")
            B.wait("tensor", "fr", nfr)
            gb = B.t_mm(B.P6[:, 0:512], B.onesr[0:1, :],
                        FB.rs1[0:1, :], True, True, tick=True)
            B.wait("vector", "cT", gb)
            B.tick("vector", lambda e: e.tensor_copy(
                FB.gbc[:, :], B.P6[:, 0:512]), "cV")
            bb = B.t_mm(B.P7[:, 0:512], B.onesr[0:1, :],
                        FB.rs2[0:1, :], True, True, tick=True)
            self._rs1_free = bb
            B.wait("vector", "cT", bb)
            B.tick("vector", lambda e: e.tensor_copy(
                FB.bbc[:, :], B.P7[:, 0:512]), "cV")
            B.drain("vector")
            for tt in range(NT):
                B.wait("sync", "cV", getattr(self, "_f3_s1r", 0))
                nfx = B.dma16("sync", lambda e, tt=tt, gc=gc: e.dma_start(
                    out=FB.s1[:, 0:512],
                    in_=FB.d_h[128 * tt:128 * (tt + 1),
                               512 * gc:512 * (gc + 1)]), "fx")
                B.wait("vector", "fx", nfx)
                B.wait("vector", "fo", nfo)
                B.drain("vector")
                s1r = B.tick("vector", lambda e, tt=tt: e.tensor_scalar(
                    FB.s2[:, 0:512], FB.s1[:, 0:512], FB.mu[:, tt:tt + 1],
                    FB.rho[:, tt:tt + 1], op0=ALU.subtract, op1=ALU.mult),
                    "cV")
                self._f3_s1r = s1r
                B.drain("vector")
                B.em("vector", lambda e: e.tensor_tensor(
                    FB.s2[:, 0:512], FB.s2[:, 0:512], FB.gbc[:, :],
                    op=ALU.mult))
                B.drain("vector")
                B.em("vector", lambda e: e.tensor_tensor(
                    FB.s2[:, 0:512], FB.s2[:, 0:512], FB.bbc[:, :],
                    op=ALU.add))
                B.drain("vector")
                B.em("vector", lambda e: e.tensor_reduce(
                    FB.fst[:, 8:9], FB.s2[:, 0:512],
                    axis=mybir.AxisListType.X, op=ALU.max,
                    apply_absolute_value=True))
                B.drain("vector")
                B.em("vector", lambda e, tt=tt: e.tensor_tensor(
                    FB.mx2[:, tt:tt + 1], FB.mx2[:, tt:tt + 1],
                    FB.fst[:, 8:9], op=ALU.max))
                B.drain("vector")
                hv = B.tick("vector", lambda e: e.tensor_copy(
                    FB.fst[:, 10:11], FB.fst[:, 10:11]), "cV")
                B.wait("scalar", "cV", hv)
                nfo = B.dma16("scalar",
                              lambda e, tt=tt, gc=gc: e.dma_start(
                                  out=FB.d_h[128 * tt:128 * (tt + 1),
                                             512 * gc:512 * (gc + 1)],
                                  in_=FB.s2[:, 0:512]), "fo")
        self.f3_fo = B.val("fo")

    def phaseF4(self):
        B = self.B
        FB = self
        B.drain("vector")
        for tt in range(NT):
            B.em("vector", lambda e, tt=tt: e.tensor_scalar(
                FB.fst[:, 8:9], FB.mx2[:, tt:tt + 1], 1e-5, None,
                op0=ALU.max))
            B.drain("vector")
            B.em("vector", lambda e, tt=tt: e.scalar_tensor_tensor(
                FB.b2neg[:, tt:tt + 1], FB.fst[:, 8:9], -1.0 / 127.0,
                FB.swc[:, 1:2], op0=ALU.mult, op1=ALU.mult))
            B.em("vector", lambda e: e.reciprocal(
                FB.fst[:, 9:10], FB.fst[:, 8:9]))
            B.drain("vector")
            B.em("vector", lambda e, tt=tt: e.tensor_scalar(
                FB.s2q[:, tt:tt + 1], FB.fst[:, 9:10], 127.0, None,
                op0=ALU.mult))
            B.drain("vector")
        B.wait("vector", "fo", self.f3_fo)
        self.fev = [0] * KD
        self._f4_s1r = getattr(self, "_f3_s1r", 0)
        B.wait("sync", "fx", B.val("fx"))
        nfx = B.val("fx")
        for tt in range(NT):
            for gc in range(4):
                B.wait("sync", "cV", getattr(self, "_f4_s1r", 0))
                nfx = B.dma16("sync", lambda e, tt=tt, gc=gc: e.dma_start(
                    out=FB.s1[:, :],
                    in_=FB.d_h[128 * tt:128 * (tt + 1),
                               2048 * gc:2048 * (gc + 1)]), "fx")
                B.wait("vector", "fx", nfx)
                B.drain("vector")
                B.em("vector", lambda e, tt=tt: e.tensor_scalar(
                    FB.s1[:, :], FB.s1[:, :], FB.s2q[:, tt:tt + 1], None,
                    op0=ALU.mult))
                B.drain("vector")
                B.em("vector", lambda e: e.tensor_copy(
                    FB.s2[:, :].bitcast(mybir.dt.int32), FB.s1[:, :]))
                B.drain("vector")
                vq = B.tick("vector", lambda e: e.tensor_copy(
                    FB.qbf[:, :], FB.s2[:, :].bitcast(mybir.dt.int32)), "cV")
                self._f4_s1r = vq
                B.wait("tensor", "cV", vq)
                for k in range(KD):
                    fj = 16 * gc + k
                    pb = (B.P4 if k % 2 == 0 else B.P5)[:, 0:64].bitcast(BF16)
                    if k >= 2 and self.fev[k - 2]:
                        B.wait("tensor", "cV", self.fev[k - 2])
                    tk = B.tick("tensor", lambda e, pb=pb, k=k: e.transpose(
                        pb, FB.qbf[:, 128 * k:128 * (k + 1)],
                        B.identb[:, :]), "cT")
                    B.wait("vector", "cT", tk)
                    self.fev[k] = B.tick(
                        "vector",
                        lambda e, pb=pb, fj=fj, tt=tt: e.tensor_copy(
                            FB.hqT[:, TB * fj + 128 * tt:
                                   TB * fj + 128 * (tt + 1)], pb), "cV")
        self.f4_done = B.val("cV")

    def phaseF5(self):
        B = self.B
        FB = self

        bt = B.val("cV")
        B.wait("tensor", "cV", bt)
        tb = B.tick("tensor", lambda e: e.transpose(
            B.P4[0:NT, 0:128], FB.b2neg[:, 0:NT], B.ident[:, :]), "cT")
        B.wait("vector", "cT", tb)
        bv = B.tick("vector", lambda e: e.tensor_copy(
            FB.row8[0:NT, :], B.P4[0:NT, 0:128]), "cV")
        B.wait("sync", "cV", bv)
        nbr = B.dma16("sync", lambda e: e.dma_start(
            out=FB.b2row[0:1, :].rearrange("o (t p) -> o t p", p=128),
            in_=FB.row8[0:NT, :].rearrange("t p -> o t p", o=1)
            if False else FB.row8[0:NT, :]), "fr")
        B.wait("tensor", "fr", nbr)
        prevc = 0
        for hh in range(2):
            if prevc:
                B.wait("tensor", "cV", prevc)
            th = B.t_mm(B.P7[:, 0:512], B.onesr[0:1, :],
                        FB.b2row[0:1, 512 * hh:512 * (hh + 1)], True, True,
                        tick=True)
            B.wait("vector", "cT", th)
            prevc = B.tick("vector", lambda e, hh=hh: e.tensor_copy(
                FB.bet2b[:, 512 * hh:512 * (hh + 1)], B.P7[:, 0:512]), "cV")
        nfx = B.dma16("sync", lambda e: e.dma_start(
            out=FB.b2c[:, :],
            in_=FB.d_b2.rearrange("o (j p) -> (o p) j", p=128),
            allow_slow_non_contiguous=True), "fx")
        bet_done = B.val("cV")

        q_tick = {0: 0, 1: 0}
        f5mm = {0: 0, 1: 0}
        nld = dict(self.f1_nld)
        yout_tick = {}
        nfy = 0
        x2tr_done = 0
        for dd in range(KD):
            ppair = [B.P1, B.P2] if dd % 2 == 0 else [B.P3, B.P0]
            B.wait("gpsimd", "cT", x2tr_done)
            if dd == 0:
                B.wait("gpsimd", "cV", self._f4_s1r)
                B.wait("gpsimd", "fx", B.val("fx"))
            for tt in range(NT):
                nfx = B.dma16(
                    "gpsimd", lambda e, dd=dd, tt=tt: e.indirect_dma_start(
                        out=FB.x2c[:, 128 * tt:128 * (tt + 1)],
                        out_offset=None,
                        in_=B.d_x2.rearrange("t (j m) -> (t j) m", m=128),
                        element_offset=128 * dd,
                        in_offset=bass_rust.IndirectOffsetOnAxis(
                            ap=FB.xrows16[:, tt:tt + 1], axis=0)), "fg")
            for fg in range(4):
                sl = (4 * dd + fg) % 2
                sem = "fw%d" % sl
                if q_tick[sl]:
                    B.wait("sync", "cV", q_tick[sl])
                elif self.f1_q[sl]:
                    B.wait("sync", "cV", self.f1_q[sl])
                nld[sl] += 1
                B.dma16("sync", lambda e, dd=dd, fg=fg, sl=sl: e.dma_start(
                    out=FB.wf1[sl][:, :].rearrange("p (j m) -> p j m", m=128),
                    in_=FB.d_w2T.rearrange("(j p) m -> p j m", p=128)
                    [:, 16 * fg:16 * (fg + 1), 128 * dd:128 * (dd + 1)]),
                    sem)
                B.wait("vector", sem, 16 * nld[sl])
                if dd == 0 and fg == 0:
                    B.wait("tensor", "cV", self.f4_done)
                if dd <= 1 and fg <= 1:
                    B.wait("vector", "cT", self.f1_mm[sl])
                if f5mm[sl]:
                    B.wait("vector", "cT", f5mm[sl])
                B.drain("vector")
                B.em("vector", lambda e, sl=sl: e.tensor_scalar(
                    FB.qa[:, :], FB.wf1[sl][:, :], FB.thc[:, 1:2], None,
                    op0=ALU.is_gt))
                B.drain("vector")
                qt = B.tick("vector", lambda e, sl=sl: e.scalar_tensor_tensor(
                    FB.w1q[sl][:, :], FB.wf1[sl][:, :], FB.thc[:, 3:4],
                    FB.qa[:, :], op0=ALU.is_lt, op1=ALU.subtract), "cV")
                q_tick[sl] = qt
                B.wait("tensor", "cV", qt)
                if dd >= 2 and fg == 0:
                    B.wait("tensor", "cV", yout_tick[dd - 2])
                for k in range(KD):
                    fj = 16 * fg + k
                    for hh in range(2):
                        t5 = B.t_mm(ppair[hh][:, :],
                                    FB.w1q[sl][:, 128 * k:128 * (k + 1)],
                                    FB.hqT[:, TB * fj + 512 * hh:
                                           TB * fj + 512 * (hh + 1)],
                                    (fg == 0 and k == 0),
                                    fg == 3 and k == KD - 1,
                                    tick=(k == KD - 1 and hh == 1))
                        if k == KD - 1 and hh == 1:
                            f5mm[sl] = t5
            mt = B.val("cT")
            B.wait("vector", "cT", mt)
            if dd >= 1:
                B.wait("vector", "fy", nfy)
            B.drain("vector")
            for hh in range(2):
                if hh == 1:
                    ye = B.tick("vector",
                                lambda e, ppair=ppair, hh=hh: e.tensor_tensor(
                                    FB.ytT[:, 512 * hh:512 * (hh + 1)],
                                    ppair[hh][:, :],
                                    FB.bet2b[:, 512 * hh:512 * (hh + 1)],
                                    op=ALU.mult), "cV")
                else:
                    B.em("vector",
                         lambda e, ppair=ppair, hh=hh: e.tensor_tensor(
                             FB.ytT[:, 512 * hh:512 * (hh + 1)],
                             ppair[hh][:, :],
                             FB.bet2b[:, 512 * hh:512 * (hh + 1)],
                             op=ALU.mult))
            B.drain("vector")
            B.wait("tensor", "cV", ye)
            B.wait("tensor", "fg", nfx)
            sts = []
            for tt in range(NT):
                pb = (B.P4 if tt % 2 == 0 else B.P5)[:, 128:256]
                if tt >= 2:
                    B.wait("tensor", "cV", sts[tt - 2])
                tk = B.tick("tensor", lambda e, pb=pb, tt=tt: e.transpose(
                    pb, FB.x2c[:, 128 * tt:128 * (tt + 1)], B.ident[:, :]),
                    "cT")
                B.wait("vector", "cT", tk)
                st = B.tick("vector", lambda e, pb=pb, tt=tt, dd=dd:
                            e.scalar_tensor_tensor(
                                FB.ytT[:, 128 * tt:128 * (tt + 1)],
                                FB.ytT[:, 128 * tt:128 * (tt + 1)],
                                FB.b2c[:, dd:dd + 1], pb,
                                op0=ALU.add, op1=ALU.add), "cV")
                B.drain("vector")
                sts.append(st)
            yout_tick[dd] = ye
            x2tr_done = B.val("cT")
            B.wait("scalar", "cV", sts[-1])
            nfy = B.dma16("scalar", lambda e, dd=dd: e.dma_start(
                out=FB.d_yT[128 * dd:128 * (dd + 1), :],
                in_=FB.ytT[:, :]), "fy")
        B.wait("gpsimd", "fy", nfy)


def ffn_host_inputs(w1, w2, b1, gamma, beta, b2):
    return {
        "ffn_w1T": np.ascontiguousarray(w1.T),
        "ffn_w2T": np.ascontiguousarray(w2.T),
        "ffn_b1": np.ascontiguousarray(b1.reshape(1, FF)),
        "ffn_gamma": np.ascontiguousarray(gamma.reshape(1, FF)),
        "ffn_beta": np.ascontiguousarray(beta.reshape(1, FF)),
        "ffn_b2": np.ascontiguousarray(b2.reshape(1, D)),
    }


import concourse.bass as bass
from concourse.bass_utils import run_bass_kernel_spmd

N_CORES = 8
SEQ = NCH * CK
TOK_PER_CORE = SEQ // N_CORES


def _xrows_input(core):
    idx = (TOK_PER_CORE * core + np.arange(TOK_PER_CORE, dtype=np.int32)
           ).reshape(NT, 128)
    return {"ffn_xrows": np.ascontiguousarray(idx.T),
            "ffn_xrows16": np.ascontiguousarray(16 * idx.T)}


def _build_program():
    nc = bass.Bass("TRN2", target_bir_lowering=False)
    T = TTTBuilder(nc)
    PW = PWStream(T)
    T.declare(early=lambda core: PW.declare())
    qs = [PW.npairs // NCH + (1 if i < PW.npairs % NCH else 0)
          for i in range(NCH)]

    def hookfn(s, sub):
        if sub == 0:
            PW.hook(qs[s])

    T.emit(step_hook=hookfn)
    barrier = {s: T.val(s) for s in
               ("cV", "cT", "cS", "sx2", "cc", "sbo", "sbi",
                "sx", "sxc0", "sxc1", "sxc2")}
    ttt_names = [it[0] for it in T.ctx[T._sbuf_mark:]
                 if isinstance(it, tuple)]
    T.free_sbuf(ttt_names)
    FB = FFNBuilder(T)
    FB.declare(pw=PW)
    FB.emit(barrier=barrier, pw=PW)
    with nc.Block() as block:
        T.replay(block)
    return nc


def kernel(**inputs):
    x = np.ascontiguousarray(inputs["L0_attn_x"][0], dtype=np.float32)
    W0 = np.asarray(inputs["L0_attn_W0"], np.float32)
    b0 = np.asarray(inputs["L0_attn_b0"], np.float32)
    W1 = np.asarray(inputs["L0_attn_W1"], np.float32)
    b1a = np.asarray(inputs["L0_attn_b1"], np.float32)
    lr_w = np.asarray(inputs["L0_attn_lr_w"], np.float32)
    lr_b = np.asarray(inputs["L0_attn_lr_b"], np.float32)
    fg_w = np.asarray(inputs["L0_attn_fg_w"], np.float32)
    fg_b = np.asarray(inputs["L0_attn_fg_b"], np.float32)
    fw1 = np.asarray(inputs["L0_ffn_w1"], np.float32)
    fb1 = np.asarray(inputs["L0_ffn_b1"], np.float32)
    gamma = np.asarray(inputs["L0_ffn_gamma"], np.float32)
    beta = np.asarray(inputs["L0_ffn_beta"], np.float32)
    fw2 = np.asarray(inputs["L0_ffn_w2"], np.float32)
    fb2 = np.asarray(inputs["L0_ffn_b2"], np.float32)

    nc = _build_program()
    ffn_common = ffn_host_inputs(fw1, fw2, fb1, gamma, beta, fb2)
    in_maps = []
    for r in range(N_CORES):
        im = host_inputs(x, W0, b0, W1, b1a, lr_w, lr_b, fg_w, fg_b, r)
        im.update(ffn_common)
        im.update(_xrows_input(r))
        in_maps.append(im)
    res = run_bass_kernel_spmd(nc, in_maps, core_ids=list(range(N_CORES)))
    out = np.concatenate(
        [np.ascontiguousarray(res.results[r]["ffn_yT"]).T
         for r in range(N_CORES)], axis=0)
    return out.reshape(1, SEQ, D).astype(np.float32)


# version 04
# speedup vs baseline: 1.1939x; 1.1939x over previous
import sys
if "/opt/trn_rl_repo" not in sys.path:
    sys.path.insert(0, "/opt/trn_rl_repo")

import os
import numpy as np
import concourse.mybir as mybir

SKIP_BIASMM = os.environ.get("TTT_SKIP_BIASMM", "0") == "1"
SKIP_GRADS = os.environ.get("TTT_SKIP_GRADS", "0") == "1"
SKIP_GATES = os.environ.get("TTT_SKIP_GATES", "0") == "1"
TR_F32 = os.environ.get("TTT_TR_F32", "0") == "1"
SKIP_AR = os.environ.get("TTT_SKIP_AR", "0") == "1"
PHASE = int(os.environ.get("TTT_PHASE", "99"))
SUB = int(os.environ.get("TTT_SUB", "99"))

F32 = mybir.dt.float32
F32R = mybir.dt.float32r
AF = mybir.ActivationFunctionType
ALU = mybir.AluOpType

NCH = 64
D = 2048
MC = 256
CK = 128
KD = 16
LR_MEM = 0.01
GFAC = 2.0 / (CK * D)


class Core:
    def __init__(self, nc, n_cores=8):
        self.nc = nc
        self.ops = {e: [] for e in ("tensor", "vector", "scalar", "sync", "gpsimd")}
        self.cnt = {}
        self.n_cores = n_cores
        self.ctx = []
        self.sems = {}


    def em(self, eng, fn):
        self.ops[eng].append(fn)

    def drain(self, eng):
        self.ops[eng].append(lambda e: e.drain())

    def tick(self, eng, fn, sem_name):
        sem = self.sems[sem_name]
        self.ops[eng].append(lambda e, fn=fn, sem=sem: fn(e).then_inc(sem, 1))
        return self.inc(sem_name, 1)

    def dma16(self, eng, fn, sem_name):
        sem = self.sems[sem_name]
        self.ops[eng].append(lambda e, fn=fn, sem=sem: fn(e).then_inc(sem, 16))
        return self.inc(sem_name, 16)

    def wait(self, eng, sem_name, val):
        if val <= 0:
            return
        sem = self.sems[sem_name]
        self.ops[eng].append(lambda e, sem=sem, val=val: e.wait_ge(sem, val))

    def inc(self, name, by):
        self.cnt[name] = self.cnt.get(name, 0) + by
        return self.cnt[name]

    def val(self, name):
        return self.cnt.get(name, 0)

    def t_mm(self, psum_ap, lhsT, rhs, start, stop, tick=False):
        def fn(e, a=psum_ap, l=lhsT, r=rhs, st=start, sp=stop):
            return e.matmul(a, l, r, start=st, stop=sp)
        if tick:
            return self.tick("tensor", fn, "cT")
        self.em("tensor", fn)

    def t_tr(self, psum_ap, in_ap, f32r):
        if TR_F32 and f32r:
            in_ap = in_ap.bitcast(F32)
            f32r = False
        ident = self.identr[:, :] if f32r else self.ident[:, :]
        if f32r:
            psum_ap = psum_ap.bitcast(F32R)

        def fn(e, a=psum_ap, i=in_ap, idt=ident):
            return e.transpose(a, i, idt)
        return self.tick("tensor", fn, "cT")

    def new_sem(self, name):
        cm = self.nc.semaphore(name)
        self.sems[name] = cm.__enter__()
        self.ctx.append(cm)

    def sbuf(self, name, shape, dtyp=F32):
        cm = self.nc.sbuf_tensor(name, shape, dtyp)
        h = cm.__enter__()
        self.ctx.append((name, cm))
        return h

    def free_sbuf(self, names):
        kept = []
        for item in reversed(self.ctx):
            if isinstance(item, tuple) and item[0] in names:
                item[1].__exit__(None, None, None)
            else:
                kept.append(item)
        self.ctx = kept[::-1]

    def declare_common(self):
        nc = self.nc
        dt = nc.dram_tensor
        self.d_ident = dt("ttt_ident", [128, 128], F32, kind="ExternalInput").ap()
        self.d_onesr = dt("ttt_onesr", [1, 128], F32, kind="ExternalInput").ap()
        self.d_onesc = dt("ttt_onesc", [128, 1], F32, kind="ExternalInput").ap()
        self.d_epscol = dt("ttt_epscol", [128, 1], F32, kind="ExternalInput").ap()
        self.d_inv128 = dt("ttt_inv128", [1, 1], F32, kind="ExternalInput").ap()
        self.d_identr = dt("ttt_identr", [128, 128], F32R, kind="ExternalInput").ap()
        self.d_onesrr = dt("ttt_onesrr", [1, 128], F32R, kind="ExternalInput").ap()
        self.d_identb = dt("ttt_identb", [128, 128], mybir.dt.bfloat16,
                           kind="ExternalInput").ap()
        self.ident = self.sbuf("identsb", [128, 128])
        self.identb = self.sbuf("identbsb", [128, 128], mybir.dt.bfloat16)
        self.identr = self.sbuf("identrsb", [128, 128], F32R)
        self.onesrr = self.sbuf("onesrrsb", [1, 128], F32R)
        self.onesr = self.sbuf("onesrsb", [1, 128])
        self.onesc = self.sbuf("onescsb", [128, 1])
        self.epscol = self.sbuf("epscol", [128, 1])
        self.inv128 = self.sbuf("inv128", [1, 1])
        ps = nc.psum_tensor
        self.P0 = ps("PS0", [128, 512], F32).__enter__()
        self.P1 = ps("PS1", [128, 512], F32).__enter__()
        self.P2 = ps("PS2", [128, 512], F32).__enter__()
        self.P3 = ps("PS3", [128, 512], F32).__enter__()
        self.P4 = ps("PS4", [128, 512], F32).__enter__()
        self.P5 = ps("PS5", [128, 512], F32).__enter__()
        self.P6 = ps("PS6", [128, 512], F32).__enter__()
        self.P7 = ps("PS7", [128, 512], F32).__enter__()
        for s in ("cT", "cV", "cS"):
            self.new_sem(s)

    def load_common(self):
        self.new_sem("sconst")
        B = self

        def ld(e, dst, src):
            return e.dma_start(out=dst, in_=src)
        B.dma16("sync", lambda e: ld(e, B.ident[:, :], B.d_ident), "sconst")
        B.dma16("sync", lambda e: ld(e, B.identr[:, :], B.d_identr), "sconst")
        B.dma16("sync", lambda e: ld(e, B.identb[:, :], B.d_identb), "sconst")
        B.dma16("sync", lambda e: ld(e, B.onesrr[:, :], B.d_onesrr), "sconst")
        B.dma16("sync", lambda e: ld(e, B.onesr[:, :], B.d_onesr), "sconst")
        B.dma16("sync", lambda e: ld(e, B.onesc[:, :], B.d_onesc), "sconst")
        B.dma16("sync", lambda e: ld(e, B.epscol[:, :], B.d_epscol), "sconst")
        B.dma16("sync", lambda e: ld(e, B.inv128[:, :], B.d_inv128), "sconst")
        return B.val("sconst")

    def replay(self, block):
        ops = self.ops

        @block.tensor
        def _(e):
            for f in ops["tensor"]:
                f(e)

        @block.vector
        def _(e):
            for f in ops["vector"]:
                f(e)

        @block.scalar
        def _(e):
            for f in ops["scalar"]:
                f(e)

        @block.sync
        def _(e):
            for f in ops["sync"]:
                f(e)

        @block.gpsimd
        def _(e):
            for f in ops["gpsimd"]:
                f(e)


def common_inputs():
    import ml_dtypes
    return {
        "ttt_identb": np.eye(128, dtype=np.float32).astype(ml_dtypes.bfloat16),
        "ttt_ident": np.eye(128, dtype=np.float32),
        "ttt_identr": np.eye(128, dtype=np.float32),
        "ttt_onesr": np.ones((1, 128), np.float32),
        "ttt_onesrr": np.ones((1, 128), np.float32),
        "ttt_onesc": np.ones((128, 1), np.float32),
        "ttt_epscol": np.full((128, 1), 1e-5, np.float32),
        "ttt_inv128": np.full((1, 1), 1.0 / 128, np.float32),
    }



class TTTBuilder(Core):
    def __init__(self, nc, n_cores=8):
        super().__init__(nc, n_cores)
        self.chunk_sx = {}
        self.fwd_z_tick = {}
        self.pred_free = 0
        self.square_tick = {}
        self.tdh_tick = {}
        self.x2_tick = 0
        self.dsb_free = 0
        self.prep_ct = {}

    def declare(self, x2_dst=None, early=None):
        nc = self.nc
        dt = nc.dram_tensor
        self.d_x = dt("ttt_x", [NCH * CK, D], F32, kind="ExternalInput").ap()
        self.d_W0T = dt("ttt_W0T", [D, MC], F32, kind="ExternalInput").ap()
        self.d_b0 = dt("ttt_b0", [1, MC], F32, kind="ExternalInput").ap()
        self.d_W1T = dt("ttt_W1T", [MC, D], F32, kind="ExternalInput").ap()
        self.d_W1n = dt("ttt_W1n", [D, MC], F32R, kind="ExternalInput").ap()
        self.d_b1 = dt("ttt_b1", [1, D], F32R, kind="ExternalInput").ap()
        self.d_lrfgT = dt("ttt_lrfgT", [D, 2], F32, kind="ExternalInput").ap()
        self.d_lrb = dt("ttt_lrb", [1, 1], F32, kind="ExternalInput").ap()
        self.d_fgb = dt("ttt_fgb", [1, 1], F32, kind="ExternalInput").ap()
        if x2_dst is None:
            x2_dst = dt("ttt_x2", [NCH * CK, D], F32, kind="ExternalOutput").ap()
        self.d_x2 = x2_dst
        self.bin = [dt(f"ttt_arin{i}", [2 * CK, D], F32) for i in range(2)]
        self.bout = [dt(f"ttt_arout{i}", [2 * CK, D], F32) for i in range(2)]

        self.declare_common()
        if early is not None:
            early(self)
        self._sbuf_mark = len(self.ctx)
        S = self.sbuf
        self.W0T = S("W0T", [128, KD * MC])
        self.W1T = S("W1T", [128, 2 * D])
        self.W1n = S("W1n", [128, KD * MC], F32R)
        self.b0 = S("b0sb", [128, 2])
        self.b1 = S("b1sb", [1, D], F32R)
        self.lrfgT = S("lrfgT", [128, KD * 2])
        self.lrbcol = S("lrbcol", [128, 1])
        self.fgb11 = S("fgb11", [1, 1])
        self.lrb11 = S("lrb11", [1, 1])
        self.xraw = [S(f"xraw{i}", [128, D]) for i in range(3)]
        self.ctok = [S(f"ctok{i}", [128, D]) for i in range(2)]
        self.cT = S("cTs", [128, KD * 256])
        self.zsb = [S(f"zsb{i}", [128, 2 * MC]) for i in range(2)]
        self.H2 = [S(f"H2sb{i}", [128, 2 * MC]) for i in range(2)]
        self.pred = S("predsb", [128, 2 * D])
        self.part = S("partsb", [128, 2 * D])
        self.dsb = S("dsb", [128, D], F32R)
        self.dT = S("dTsb", [128, KD * 128], F32R)
        self.hs = S("hssb", [128, MC], F32R)
        self.dzf = S("dzfsb", [128, MC], F32R)
        self.dzt = S("dztsb", [128, MC], F32R)
        self.x2sb = S("x2sb", [128, D])
        self.sg = S("sgsb", [128, 2 * 128])
        self.sp1 = S("sp1sb", [128, 2 * 128])
        self.sp2 = S("sp2sb", [128, 2 * 128])
        self.stat = S("statsb", [128, 8])
        self.gred = S("gredsb", [128, 2])
        self.scal = [S(f"scal{i}", [1, 4]) for i in range(3)]
        self.fk = [S(f"fkcol{i}", [128, 2]) for i in range(3)]
        self.k8m = [S(f"k8mat{i}", [128, 128], F32R) for i in range(3)]
        self.k8r = S("k8row", [1, 128])
        self.gb0c = S("gb0c", [128, 2])
        self.ctokr = S("ctokr", [128, D], F32R)
        self.k2sb = S("k2sb", [128, 256], F32R)

        for s in ("sx", "sxc0", "sxc1", "sxc2", "sbo", "sbi", "sx2", "cc"):
            self.new_sem(s)

    TTT_BIG = ("W0T", "W1T", "W1n", "xraw0", "xraw1", "xraw2", "ctok0",
               "ctok1", "cTs", "zsb0", "zsb1", "H2sb0", "H2sb1",
               "predsb", "partsb", "dsb", "dTsb", "hssb", "dzfsb", "dztsb",
               "x2sb", "sgsb", "sp1sb", "sp2sb", "k8mat0", "k8mat1",
               "k8mat2", "ctokr", "k2sb", "lrfgT")

    def emit(self, step_hook=None):
        self.step_hook = step_hook
        if PHASE == 0:
            self.prologue_noflood()
            self.debug_exit()
            return
        self.prologue()
        if PHASE == 1:
            self.debug_exit()
            return
        for s in range(min(NCH, PHASE - 1)):
            self.step(s)
        if PHASE - 1 < NCH:
            self.debug_exit()
            return
        self.epilogue()

    def prologue_noflood(self):
        real_fwd = self.fwd
        self.fwd = lambda s: None
        try:
            self.prologue()
        finally:
            self.fwd = real_fwd

    def debug_exit(self):
        B = self
        B.wait("gpsimd", "cV", B.val("cV"))
        B.wait("gpsimd", "cT", B.val("cT"))
        B.wait("gpsimd", "cS", B.val("cS"))
        B.dma16("gpsimd", lambda e: e.dma_start(
            out=B.d_x2[0:128, :], in_=B.ctok[0][:, :]), "sx2")
        B.wait("gpsimd", "sx2", B.val("sx2"))

    def prologue(self):
        B = self

        def ld(e, dst, src, allow_slow_non_contiguous=False):
            return e.dma_start(out=dst, in_=src,
                               allow_slow_non_contiguous=allow_slow_non_contiguous)

        B.dma16("sync", lambda e: ld(
            e, B.W0T[:, :].rearrange("p (j m) -> p j m", m=MC),
            B.d_W0T.rearrange("(j p) m -> p j m", p=128)), "sx")
        B.dma16("sync", lambda e: ld(
            e, B.W1T[:, :].rearrange("p (t m) -> p t m", m=D),
            B.d_W1T.rearrange("(t p) m -> p t m", p=128)), "sx")
        B.dma16("sync", lambda e: ld(
            e, B.W1n[:, :].rearrange("p (j m) -> p j m", m=MC),
            B.d_W1n.rearrange("(j p) m -> p j m", p=128)), "sx")
        B.dma16("sync", lambda e: ld(
            e, B.b0[:, :], B.d_b0.rearrange("o (t p) -> (o p) t", p=128),
        allow_slow_non_contiguous=True), "sx")
        B.dma16("sync", lambda e: ld(e, B.b1[:, :], B.d_b1), "sx")
        B.dma16("sync", lambda e: ld(
            e, B.lrfgT[:, :].rearrange("p (j m) -> p j m", m=2),
            B.d_lrfgT.rearrange("(j p) m -> p j m", p=128)), "sx")
        B.dma16("sync", lambda e: ld(e, B.fgb11[:, :], B.d_fgb), "sx")
        B.dma16("sync", lambda e: ld(e, B.lrb11[:, :], B.d_lrb), "sx")
        ncst = B.load_common()
        nload = B.val("sx")
        B.wait("vector", "sconst", ncst)
        B.wait("tensor", "sconst", ncst)
        B.wait("scalar", "sconst", ncst)

        for c in range(2):
            B.chunk_sx[c] = ("sxc%d" % (c % 3), B.dma16("sync", lambda e, c=c: ld(
                e, B.xraw[c % 3][:, :], B.d_x[128 * c:128 * (c + 1), :]),
                "sxc%d" % (c % 3)))

        B.wait("vector", "sx", nload)
        B.em("vector", lambda e: e.tensor_scalar(
            B.b1[:, :], B.b1[:, :], 0.125, None, op0=ALU.mult))
        B.part_zero = B.tick(
            "vector", lambda e: e.memset(B.part[:, 0:D], 0.0), "cV")
        B.wait("tensor", "sx", nload)
        tb = B.t_mm(B.P6[:, 0:1], B.onesr[0:1, :], B.lrb11[0:1, :], True, True,
                    tick=True)
        B.wait("vector", "cT", tb)
        B.tick("vector", lambda e: e.tensor_copy(B.lrbcol[:, :], B.P6[:, 0:1]),
               "cV")

        self.prep(0)
        self.prep(1)
        self.fwd(0)

    def zwin(self, s):
        B = self
        css, csn = s % 2, (s + 1) % 2
        new_ch = s + 1 <= NCH - 1
        B.wait("vector", "sbi", 32 * (s + 1))
        B.wait("tensor", "cV",
               self.prep_ct.get(s + 1 if new_ch else s, 0))
        if new_ch:
            first = True
            for t2 in range(2):
                for jj in range(KD):
                    B.t_mm(B.P0[:, 128 * t2:128 * (t2 + 1)],
                           B.W0T[:, MC * jj + 128 * t2:
                                 MC * jj + 128 * t2 + 128],
                           B.cT[:, 256 * jj + 128 * csn:
                                256 * jj + 128 * csn + 128],
                           first, t2 == 1 and jj == KD - 1,
                           tick=(t2 == 1 and jj == KD - 1))
                    first = False
        first = True
        if new_ch:
            for jj in range(KD):
                B.t_mm(B.P3[:, 0:128],
                       B.cT[:, 256 * jj + 128 * css:
                            256 * jj + 128 * css + 128],
                       B.cT[:, 256 * jj + 128 * csn:
                            256 * jj + 128 * csn + 128],
                       first, False)
                first = False
        for jj in range(KD):
            B.t_mm(B.P3[:, 128:256],
                   B.cT[:, 256 * jj + 128 * css:256 * jj + 128 * css + 128],
                   B.cT[:, 256 * jj + 128 * css:256 * jj + 128 * css + 128],
                   first, jj == KD - 1, tick=(jj == KD - 1))
            first = False
        tks = B.val("cT")
        B.wait("vector", "cT", tks)
        if new_ch:
            B.em("vector", lambda e: e.tensor_copy(
                B.part[:, 0:256], B.P0[:, 0:256]))
            B.em("vector", lambda e: e.tensor_copy(
                B.k2sb[:, 128:256], B.P3[:, 0:128]))
        self.zwin_ev = B.tick("vector", lambda e: e.tensor_copy(
            B.k2sb[:, 0:128], B.P3[:, 128:256]), "cV")
        self.fwd_z_tick[s + 1] = tks

    def prep(self, c):
        B = self
        sl = c % 3
        sl2 = c % 2
        xr = B.xraw[sl]
        B.wait("scalar", B.chunk_sx[c][0], B.chunk_sx[c][1])
        if c >= 3:
            B.wait("scalar", "sx2", 16 * (c - 2))
        B.wait("scalar", "cV", B.x2_tick)
        ssq = B.tick("scalar", lambda e: e.activation(
            B.x2sb[:, :], xr[:, :], AF.Square), "cS")
        B.square_tick[c] = ssq
        B.wait("vector", "cS", ssq)
        B.wait("vector", "cT", B.fwd_z_tick.get(c - 1, 0))
        B.em("vector", lambda e: e.tensor_reduce(
            B.stat[:, 2:3], xr[:, :], axis=mybir.AxisListType.X, op=ALU.add))
        B.em("vector", lambda e: e.tensor_reduce(
            B.stat[:, 3:4], B.x2sb[:, :], axis=mybir.AxisListType.X, op=ALU.add))
        B.drain("vector")
        B.em("vector", lambda e: e.tensor_scalar(
            B.stat[:, 0:1], B.stat[:, 2:3], 1.0 / D, None, op0=ALU.mult))
        B.em("vector", lambda e: e.tensor_scalar(
            B.stat[:, 4:5], B.stat[:, 3:4], 1.0 / D, None, op0=ALU.mult))
        B.drain("vector")
        B.em("vector", lambda e: e.tensor_tensor(
            B.stat[:, 5:6], B.stat[:, 0:1], B.stat[:, 0:1], op=ALU.mult))
        B.drain("vector")
        vv = B.tick("vector", lambda e: e.tensor_tensor(
            B.stat[:, 6:7], B.stat[:, 4:5], B.stat[:, 5:6], op=ALU.subtract),
            "cV")
        B.wait("scalar", "cV", vv)
        ssd = B.tick("scalar", lambda e: e.activation(
            B.stat[:, 5:6], B.stat[:, 6:7], AF.Sqrt,
            bias=B.epscol[:, 0:1]), "cS")
        B.wait("vector", "cS", ssd)
        B.em("vector", lambda e: e.reciprocal(B.stat[:, 1:2], B.stat[:, 5:6]))
        B.drain("vector")
        vc = B.tick("vector", lambda e, sl2=sl2: e.tensor_scalar(
            B.ctok[sl2][:, :], xr[:, :], B.stat[:, 0:1], B.stat[:, 1:2],
            op0=ALU.subtract, op1=ALU.mult), "cV")

        B.wait("tensor", "cV", vc)
        evac_t = []
        for jj in range(KD):
            if jj >= 2:
                B.wait("tensor", "cV", evac_t[jj - 2])
            pb = (B.P4 if jj % 2 == 0 else B.P5)[:, 0:128]
            tt = B.t_tr(pb, B.ctok[sl2][:, 128 * jj:128 * (jj + 1)], f32r=False)
            B.wait("vector", "cT", tt)
            ev = B.tick("vector", lambda e, pb=pb, jj=jj, sl2=sl2: e.tensor_copy(
                B.cT[:, 256 * jj + 128 * sl2:256 * jj + 128 * sl2 + 128],
                pb), "cV")
            evac_t.append(ev)
        self.prep_ct[c] = evac_t[-1]

        B.wait("tensor", "cV", evac_t[-1])
        for jj in range(KD):
            B.t_mm(B.P6[:, 0:2],
                   B.cT[:, 256 * jj + 128 * sl2:256 * jj + 128 * sl2 + 128],
                   B.lrfgT[:, 2 * jj:2 * (jj + 1)], jj == 0, jj == KD - 1,
                   tick=(jj == KD - 1))
        tg = B.val("cT")
        B.wait("scalar", "cT", tg)
        B.em("scalar", lambda e: e.activation(
            B.gred[:, 0:1], B.P6[:, 0:1], AF.Sigmoid, bias=B.lrbcol[:, 0:1]))
        sg2 = B.tick("scalar", lambda e: e.copy(B.gred[:, 1:2], B.P6[:, 1:2]),
                     "cS")
        B.wait("tensor", "cS", sg2)
        tr2 = B.t_mm(B.P6[0:1, 4:6], B.onesc[:, 0:1], B.gred[:, :], True, True,
                     tick=True)
        sc = B.scal[sl]
        B.wait("scalar", "cT", tr2)
        sf = B.tick("scalar", lambda e, sc=sc: e.activation(
            sc[0:1, 0:1], B.P6[0:1, 5:6], AF.Sigmoid,
            bias=B.fgb11[0:1, 0:1], scale=B.inv128[0:1, 0:1]), "cS")
        B.wait("vector", "cS", sf)
        B.em("vector", lambda e, sc=sc: e.tensor_scalar(
            sc[0:1, 1:2], B.P6[0:1, 4:5], LR_MEM / 128.0 * GFAC, None,
            op0=ALU.mult))
        B.drain("vector")
        B.em("vector", lambda e, sc=sc: e.tensor_scalar(
            sc[0:1, 2:3], sc[0:1, 1:2], 0.125, None, op0=ALU.mult))
        B.drain("vector")
        vk = B.tick("vector", lambda e, sc=sc: e.tensor_scalar(
            B.k8r[0:1, :], B.onesr[0:1, :], sc[0:1, 2:3], None, op0=ALU.mult),
            "cV")
        B.wait("tensor", "cV", vk)
        B.t_mm(B.P6[:, 8:10], B.onesr[0:1, :], sc[0:1, 0:2], True, True)
        tbc = B.t_mm(B.P7[:, 0:128], B.onesr[0:1, :], B.k8r[0:1, :], True, True,
                     tick=True)
        B.wait("vector", "cT", tbc)
        B.em("vector", lambda e, sl=sl: e.tensor_copy(
            B.fk[sl][:, 0:2], B.P6[:, 8:10]))
        B.tick("vector", lambda e, sl=sl: e.tensor_copy(
            B.k8m[sl][:, :], B.P7[:, 0:128]), "cV")

    def fwd(self, s):
        B = self
        halves = []
        if s >= 1:
            halves.append((0, (s - 1) % 2))
        if s <= NCH - 1:
            halves.append((1, s % 2))
        slot = s % 2

        B.wait("tensor", "cV", B.val("cV"))
        if len(halves) == 2:
            lo, hi = 0, 256
        else:
            lo = 128 * halves[0][1]
            hi = lo + 128
        if s == 0:
            first = True
            for jj in range(KD):
                for t2 in range(2):
                    last = (jj == KD - 1 and t2 == 1)
                    csl = halves[0][1]
                    off = 256 * jj + 128 * csl
                    B.t_mm(B.P0[:, 256 * t2 + 128 * csl:
                                256 * t2 + 128 * csl + 128],
                           B.W0T[:, MC * jj + 128 * t2:
                                 MC * jj + 128 * t2 + 128],
                           B.cT[:, off:off + 128],
                           first, last, tick=last)
                    first = False
            tz = B.val("cT")
            B.fwd_z_tick[s] = tz
            B.wait("scalar", "cT", tz)
            B.wait("vector", "cT", tz)
            B.drain("vector")
            for t2 in range(2):
                def sil(e, t2=t2, slot=slot, lo=lo, hi=hi):
                    return e.activation(
                        B.H2[slot][:, 256 * t2 + lo:256 * t2 + hi],
                        B.P0[:, 256 * t2 + lo:256 * t2 + hi], AF.Silu,
                        bias=B.b0[:, t2:t2 + 1])
                if t2 == 1:
                    sh = B.tick("scalar", sil, "cS")
                else:
                    B.em("scalar", sil)

                def zc(e, t2=t2, slot=slot, lo=lo, hi=hi):
                    return e.tensor_scalar(
                        B.zsb[slot][:, 256 * t2 + lo:256 * t2 + hi],
                        B.P0[:, 256 * t2 + lo:256 * t2 + hi],
                        B.b0[:, t2:t2 + 1], None, op0=ALU.add)
                if t2 == 1:
                    shv = B.tick("vector", zc, "cV")
                else:
                    B.em("vector", zc)
        else:
            fprev = B.fk[(s - 1) % 3][:, 0:1]
            pslot = (s - 1) % 2
            ncorr = 2 * len(halves)
            ic = 0
            for h, csl in halves:
                k2ap = (B.k2sb[:, 0:128] if h == 0
                        else B.k2sb[:, 128:256])
                for t2 in range(2):
                    B.t_mm(B.P0[:, 256 * csl + 128 * t2:
                                256 * csl + 128 * t2 + 128],
                           B.dzt[:, 128 * t2:128 * (t2 + 1)],
                           k2ap, ic == 0, ic == ncorr - 1,
                           tick=(ic == ncorr - 1))
                    ic += 1
            tz = B.val("cT")
            B.wait("vector", "cT", tz)
            B.drain("vector")
            cticks = []
            for h, csl in halves:
                for t2 in range(2):
                    if h == 0:
                        zsrc = B.zsb[pslot][:, 256 * t2 + 128 * csl:
                                            256 * t2 + 128 * csl + 128]
                    else:
                        zsrc = B.part[:, 128 * t2:128 * (t2 + 1)]
                    dst = B.zsb[slot][:, 256 * t2 + 128 * csl:
                                      256 * t2 + 128 * csl + 128]
                    B.em("vector", lambda e, dst=dst, zsrc=zsrc,
                         fprev=fprev, csl=csl, t2=t2:
                         e.scalar_tensor_tensor(
                             dst, zsrc, fprev,
                             B.P0[:, 256 * csl + 128 * t2:
                                  256 * csl + 128 * t2 + 128],
                             op0=ALU.mult, op1=ALU.subtract))
                    B.drain("vector")
                    ct = B.tick("vector", lambda e, dst=dst, t2=t2:
                                e.tensor_scalar(
                                    dst, dst, B.b0[:, t2:t2 + 1], None,
                                    op0=ALU.add), "cV")
                    B.drain("vector")
                    cticks.append(ct)
            shv = cticks[-1]
            B.wait("scalar", "cV", shv)
            for t2 in range(2):
                def sil(e, t2=t2, slot=slot, lo=lo, hi=hi):
                    return e.activation(
                        B.H2[slot][:, 256 * t2 + lo:256 * t2 + hi],
                        B.zsb[slot][:, 256 * t2 + lo:256 * t2 + hi],
                        AF.Silu)
                if t2 == 1:
                    sh = B.tick("scalar", sil, "cS")
                else:
                    B.em("scalar", sil)

        B.wait("tensor", "cS", sh)
        B.wait("tensor", "cV", shv)
        if s >= 1:
            B.wait("vector", "sbi", 32 * s)
        pv = []
        for hi, (h, csl) in enumerate(halves):
            for ch in range(4):
                pb = B.P1 if (hi * 4 + ch) % 2 == 0 else B.P2
                if len(pv) >= 2:
                    B.wait("tensor", "cV", pv[-2])
                for t2 in range(2):
                    B.t_mm(pb[:, :],
                           B.H2[slot][:, 256 * t2 + 128 * csl:
                                      256 * t2 + 128 * csl + 128],
                           B.W1T[:, D * t2 + 512 * ch:D * t2 + 512 * (ch + 1)],
                           t2 == 0, False)
                if SKIP_BIASMM:
                    tpm = B.t_mm(pb[:, :],
                                 B.H2[slot][:, 128 * h:128 * h + 128],
                                 B.W1T[:, 512 * ch:512 * (ch + 1)],
                                 False, True, tick=True)
                else:
                    tpm = B.tick("tensor", lambda e, pb=pb, ch=ch: e.matmul(
                        pb[:, :], B.onesrr[0:1, :],
                        B.b1[0:1, 512 * ch:512 * (ch + 1)],
                        start=False, stop=True), "cT")
                B.wait("vector", "cT", tpm)
                ev = B.tick("vector",
                            lambda e, pb=pb, h=h, ch=ch: e.tensor_copy(
                                B.part[:, D * h + 512 * ch:
                                       D * h + 512 * (ch + 1)], pb[:, :]),
                            "cV")
                pv.append(ev)
        B.wait("scalar", "cV", pv[-1])
        if s == 0:
            B.wait("scalar", "cV", B.part_zero)
        if s >= 2:
            B.wait("scalar", "cc", s - 1)
        for h in range(2):
            B.dma16("scalar", lambda e, h=h, slot=slot: e.dma_start(
                out=B.bin[slot][128 * h:128 * (h + 1), :],
                in_=B.part[:, D * h:D * (h + 1)]), "sbi")
        B.wait("gpsimd", "sbi", B.val("sbi"))
        if s >= 2:
            B.wait("gpsimd", "sbo", 32 * (s - 1))
        B.em("gpsimd", lambda e, slot=slot, sem=B.sems["cc"]: e.collective_compute(
            "AllReduce", ALU.add, replica_groups=[list(range(B.n_cores))],
            ins=[B.bin[slot].ap().opt()],
            outs=[B.bout[slot].ap().opt()]).then_inc(sem, 1))
        B.inc("cc", 1)

        B.wait("sync", "cc", B.val("cc"))
        B.wait("sync", "cV", B.pred_free)
        for h in range(2):
            B.dma16("sync", lambda e, h=h, slot=slot: e.dma_start(
                out=B.pred[:, D * h:D * (h + 1)],
                in_=B.bout[slot][128 * h:128 * (h + 1), :]), "sbo")

    def step(self, s):
        B = self
        slot = s % 2
        csl = s % 3
        csl2 = s % 2
        kap_ap = B.fk[csl][:, 1:2]
        f_ap = B.fk[csl][:, 0:1]
        f11 = B.scal[csl][0:1, 0:1]

        if self.step_hook is not None:
            self.step_hook(s, 0)
        self.zwin(s)

        B.drain("vector")
        B.wait("vector", "sbo", 32 * (s + 1))
        B.wait("vector", "cT", B.dsb_free)
        vd = B.tick("vector", lambda e, csl2=csl2: e.tensor_tensor(
            B.dsb[:, :], B.pred[:, D:2 * D], B.ctok[csl2][:, :],
            op=ALU.subtract), "cV")
        if s >= 1:
            if s >= 2:
                B.wait("vector", "sx2", 16 * (s - 1))
            B.wait("vector", "cS", B.square_tick.get(s + 1, 0))
            vx2 = B.tick("vector", lambda e, s=s: e.tensor_tensor(
                B.x2sb[:, :], B.xraw[(s - 1) % 3][:, :],
                B.pred[:, 0:D], op=ALU.add), "cV")
            B.wait("gpsimd", "cV", vx2)
            B.dma16("gpsimd", lambda e, s=s: e.dma_start(
                out=B.d_x2[128 * (s - 1):128 * s, :], in_=B.x2sb[:, :]), "sx2")
            B.pred_free = vx2
            B.x2_tick = vx2
        else:
            B.pred_free = vd

        if s + 2 <= NCH - 1:
            if s >= 1:
                B.wait("sync", "sx2", 16 * s)
            B.chunk_sx[s + 2] = ("sxc%d" % ((s + 2) % 3),
                                 B.dma16("sync", lambda e, s=s: e.dma_start(
                                     out=B.xraw[(s + 2) % 3][:, :],
                                     in_=B.d_x[128 * (s + 2):128 * (s + 3), :]),
                                     "sxc%d" % ((s + 2) % 3)))

        B.wait("tensor", "cV", vd)
        evt = []
        for jj in range(KD):
            if jj >= 2:
                B.wait("tensor", "cV", evt[jj - 2])
            pb = (B.P4 if jj % 2 == 0 else B.P5)[:, 0:128]
            tt = B.t_tr(pb, B.dsb[:, 128 * jj:128 * (jj + 1)], f32r=True)
            B.wait("vector", "cT", tt)
            ev = B.tick("vector", lambda e, pb=pb, jj=jj: e.tensor_copy(
                B.dT[:, 128 * jj:128 * (jj + 1)], pb), "cV")
            evt.append(ev)

        if SUB <= 2:
            return
        B.tick("vector", lambda e, csl2=csl2: e.tensor_copy(
            B.ctokr[:, :], B.ctok[csl2][:, :]), "cV")

        B.wait("tensor", "cV", evt[-1])
        for t2 in range(2):
            pb = (B.P4 if t2 % 2 == 0 else B.P5)[:, 128:256]
            th = B.t_tr(pb, B.H2[slot][:, 256 * t2 + 128 * slot:
                                       256 * t2 + 128 * slot + 128],
                        f32r=False)
            B.wait("vector", "cT", th)
            B.tick("vector", lambda e, pb=pb, t2=t2, kap_ap=kap_ap:
                   e.tensor_scalar(B.hs[:, 128 * t2:128 * (t2 + 1)], pb,
                                   kap_ap, None, op0=ALU.mult), "cV")
        sh = B.val("cV")

        if SUB <= 3:
            return
        B.wait("tensor", "cV", evt[-1])
        first = True
        for jj in range(KD):
            for mc in range(2):
                last = (jj == KD - 1 and mc == 1)
                B.t_mm(B.P3[:, 128 * mc:128 * (mc + 1)],
                       B.W1n[:, MC * jj + 128 * mc:MC * jj + 128 * mc + 128],
                       B.dT[:, 128 * jj:128 * (jj + 1)],
                       first, last, tick=last)
                first = False
        tdh = B.val("cT")
        B.tdh_tick[s] = tdh

        if SUB <= 4:
            return
        B.wait("tensor", "cV", sh)
        gwv = []
        for t2 in range(2):
            for ch in range(4):
                pb = B.P1 if (t2 * 4 + ch) % 2 == 0 else B.P2
                if len(gwv) >= 2:
                    B.wait("tensor", "cV", gwv[-2])
                tm = B.t_mm(pb[:, :],
                            B.hs[:, 128 * t2:128 * (t2 + 1)],
                            B.dsb[:, 512 * ch:512 * (ch + 1)],
                            True, True, tick=True)
                B.wait("vector", "cT", tm)
                ev = B.tick(
                    "vector",
                    lambda e, pb=pb, t2=t2, ch=ch, f_ap=f_ap:
                    e.scalar_tensor_tensor(
                        B.W1T[:, D * t2 + 512 * ch:D * t2 + 512 * (ch + 1)],
                        B.W1T[:, D * t2 + 512 * ch:D * t2 + 512 * (ch + 1)],
                        f_ap, pb[:, :], op0=ALU.mult, op1=ALU.subtract), "cV")
                gwv.append(ev)

        if SUB <= 5:
            return
        if self.step_hook is not None:
            self.step_hook(s, 1)
        B.wait("scalar", "cT", tdh)
        for t2 in range(2):
            zh = B.zsb[slot][:, 256 * t2 + 128 * slot:
                             256 * t2 + 128 * slot + 128]
            B.tick("scalar", lambda e, t2=t2, zh=zh: e.activation(
                B.sg[:, 128 * t2:128 * (t2 + 1)], zh, AF.Sigmoid), "cS")
        ssg = B.val("cS")
        B.wait("vector", "cS", ssg)
        for t2 in range(2):
            zh = B.zsb[slot][:, 256 * t2 + 128 * slot:
                             256 * t2 + 128 * slot + 128]
            sgh = B.sg[:, 128 * t2:128 * (t2 + 1)]
            s1 = B.sp1[:, 128 * t2:128 * (t2 + 1)]
            s2 = B.sp2[:, 128 * t2:128 * (t2 + 1)]
            dzh = B.dzf[:, 128 * t2:128 * (t2 + 1)]
            B.em("vector", lambda e, zh=zh, sgh=sgh, s1=s1: e.tensor_tensor(
                s1, zh, sgh, op=ALU.mult))
            B.drain("vector")
            B.em("vector", lambda e, sgh=sgh, s1=s1, s2=s2: e.tensor_tensor(
                s2, s1, sgh, op=ALU.mult))
            B.drain("vector")
            B.em("vector", lambda e, s1=s1, s2=s2: e.tensor_tensor(
                s1, s1, s2, op=ALU.subtract))
            B.drain("vector")
            B.em("vector", lambda e, sgh=sgh, s1=s1: e.tensor_tensor(
                s1, sgh, s1, op=ALU.add))
            B.drain("vector")
            B.tick("vector", lambda e, t2=t2, s1=s1, dzh=dzh: e.tensor_tensor(
                dzh, B.P3[:, 128 * t2:128 * (t2 + 1)], s1, op=ALU.mult), "cV")
        vdz = B.val("cV")

        if SUB <= 6:
            return
        B.wait("tensor", "cV", vdz)
        for t2 in range(2):
            pb = (B.P4 if t2 % 2 == 0 else B.P5)[:, 256:384]
            tt = B.t_tr(pb, B.dzf[:, 128 * t2:128 * (t2 + 1)], f32r=True)
            B.wait("vector", "cT", tt)
            B.tick("vector", lambda e, pb=pb, t2=t2, kap_ap=kap_ap:
                   e.tensor_scalar(B.dzt[:, 128 * t2:128 * (t2 + 1)], pb,
                                   kap_ap, None, op0=ALU.mult), "cV")
        sdz = B.val("cV")

        B.wait("tensor", "cV", sdz)
        g0v = []
        for jj in range(KD):
            pb = B.P1 if jj % 2 == 0 else B.P2
            if len(g0v) >= 2:
                B.wait("tensor", "cV", g0v[-2])
            tm = B.t_mm(pb[:, 0:256],
                        B.ctokr[:, 128 * jj:128 * (jj + 1)],
                        B.dzt[:, :], True, True, tick=True)
            B.wait("vector", "cT", tm)
            ev = B.tick("vector", lambda e, pb=pb, jj=jj, f_ap=f_ap:
                        e.scalar_tensor_tensor(
                            B.W0T[:, MC * jj:MC * (jj + 1)],
                            B.W0T[:, MC * jj:MC * (jj + 1)],
                            f_ap, pb[:, 0:256],
                            op0=ALU.mult, op1=ALU.subtract), "cV")
            g0v.append(ev)
        w0done = g0v[-1]

        if SUB <= 7:
            return
        gb1v = []
        for ch in range(4):
            if len(gb1v) >= 1:
                B.wait("tensor", "cV", gb1v[-1])
            tm = B.t_mm(B.P7[:, 0:512], B.k8m[csl][:, :],
                        B.dsb[:, 512 * ch:512 * (ch + 1)],
                        True, True, tick=True)
            B.wait("vector", "cT", tm)
            ev = B.tick("vector", lambda e, ch=ch, f11=f11:
                        e.scalar_tensor_tensor(
                            B.b1[0:1, 512 * ch:512 * (ch + 1)],
                            B.b1[0:1, 512 * ch:512 * (ch + 1)],
                            f11, B.P7[0:1, 0:512],
                            op0=ALU.mult, op1=ALU.subtract), "cV")
            gb1v.append(ev)
        B.dsb_free = B.val("cT")

        B.drain("vector")
        for t2 in range(2):
            B.em("vector", lambda e, t2=t2: e.tensor_reduce(
                B.gb0c[:, t2:t2 + 1], B.dzf[:, 128 * t2:128 * (t2 + 1)],
                axis=mybir.AxisListType.X, op=ALU.add))
        B.drain("vector")
        B.em("vector", lambda e, kap_ap=kap_ap: e.tensor_scalar(
            B.gb0c[:, 0:2], B.gb0c[:, 0:2], kap_ap, None, op0=ALU.mult))
        B.drain("vector")
        B.tick("vector", lambda e, f_ap=f_ap: e.scalar_tensor_tensor(
            B.b0[:, 0:2], B.b0[:, 0:2], f_ap, B.gb0c[:, 0:2],
            op0=ALU.mult, op1=ALU.subtract), "cV")
        bdone = B.val("cV")

        if SUB <= 8:
            return
        if self.step_hook is not None:
            self.step_hook(s, 2)
        B.wait("tensor", "cV", bdone)
        self.fwd(s + 1)
        if s + 2 <= NCH - 1:
            self.prep(s + 2)

    def epilogue(self):
        B = self
        s = NCH
        slot = s % 2
        B.wait("vector", "sbo", 32 * (s + 1))
        B.wait("vector", "sx2", 16 * (s - 1))
        vx2 = B.tick("vector", lambda e: e.tensor_tensor(
            B.x2sb[:, :], B.xraw[(s - 1) % 3][:, :],
            B.pred[:, 0:D], op=ALU.add), "cV")
        B.wait("gpsimd", "cV", vx2)
        B.dma16("gpsimd", lambda e: e.dma_start(
            out=B.d_x2[128 * (NCH - 1):128 * NCH, :], in_=B.x2sb[:, :]), "sx2")
        B.wait("gpsimd", "sx2", B.val("sx2"))


def host_inputs(x, W0, b0, W1, b1, lr_w, lr_b, fg_w, fg_b, core):
    r = core
    msl = slice(MC * r, MC * (r + 1))
    return {
        "ttt_x": np.ascontiguousarray(x.reshape(NCH * CK, D)),
        "ttt_W0T": np.ascontiguousarray(W0[msl, :].T),
        "ttt_b0": np.ascontiguousarray(b0[msl].reshape(1, MC)),
        "ttt_W1T": np.ascontiguousarray(W1[:, msl].T),
        "ttt_W1n": np.ascontiguousarray(W1[:, msl]),
        "ttt_b1": np.ascontiguousarray(b1.reshape(1, D)),
        "ttt_lrfgT": np.ascontiguousarray(
            np.stack([lr_w.reshape(D), fg_w.reshape(D)], axis=1)),
        "ttt_lrb": np.asarray(lr_b, np.float32).reshape(1, 1),
        "ttt_fgb": np.asarray(fg_b, np.float32).reshape(1, 1),
        **common_inputs(),
    }


import numpy as np
import bass_rust
import concourse.mybir as mybir

F32 = mybir.dt.float32
BF16 = mybir.dt.bfloat16
AF = mybir.ActivationFunctionType
ALU = mybir.AluOpType

D = 2048
FF = 8192
TB = 1024
NT = 8
KD = 16
KF = 64
NW = 1.0 / (FF * D)


class PWStream:
    NSLOT = 4

    def __init__(self, core):
        self.B = core
        self.nc = core.nc

    def declare(self):
        nc = self.nc
        B = self.B
        dt = nc.dram_tensor
        self.d_w1T = dt("ffn_w1T", [D, FF], F32, kind="ExternalInput").ap()
        self.d_w2T = dt("ffn_w2T", [FF, D], F32, kind="ExternalInput").ap()
        self.pwbuf = B.sbuf("pw_buf", [128, 1024])
        self.pacc = B.sbuf("pw_acc", [128, 4])
        for i in range(2):
            B.new_sem("pw%d" % i)
        self.chunks = []
        for j in range(D // 128):
            for k in range(FF // 256):
                self.chunks.append(
                    (self.d_w1T[128 * j:128 * (j + 1), 256 * k:256 * (k + 1)],
                     0))
        for j in range(FF // 128):
            for k in range(D // 256):
                self.chunks.append(
                    (self.d_w2T[128 * j:128 * (j + 1), 256 * k:256 * (k + 1)],
                     1))
        self.npairs = len(self.chunks) // 2
        self.ci = 0
        self.ri = 0
        self.red_tick = {}
        self._init_done = False

    def _trigger_pair(self):
        B = self.B
        p = self.ci
        q = p % 2
        if p >= 2:
            B.wait("scalar", "cV", self.red_tick.pop(p - 2))
        for h in range(2):
            src, _ = self.chunks[2 * p + h]
            B.dma16("scalar", lambda e, src=src, q=q, h=h: e.dma_start(
                out=self.pwbuf[:, 512 * q + 256 * h:512 * q + 256 * (h + 1)],
                in_=src), "pw%d" % q)
        self.ci += 1

    def _reduce_pair(self):
        B = self.B
        p = self.ri
        _, col = self.chunks[2 * p]
        q = p % 2
        B.wait("vector", "pw%d" % q, 32 * (p // 2 + 1))
        B.drain("vector")
        B.em("vector", lambda e, q=q: e.tensor_reduce(
            self.pacc[:, 2:3], self.pwbuf[:, 512 * q:512 * (q + 1)],
            axis=mybir.AxisListType.X, op=ALU.add,
            apply_absolute_value=True))
        B.drain("vector")
        t = B.tick("vector", lambda e, col=col: e.tensor_tensor(
            self.pacc[:, col:col + 1], self.pacc[:, col:col + 1],
            self.pacc[:, 2:3], op=ALU.add), "cV")
        self.red_tick[p] = t
        self.ri += 1

    def hook(self, quota):
        B = self.B
        if not self._init_done:
            self._init_done = True
            B.tick("vector", lambda e: e.memset(self.pacc[:, :], 0.0), "cV")
            B.drain("vector")
            for _ in range(min(2, self.npairs)):
                self._trigger_pair()
        n = 0
        while n < quota and self.ri < self.npairs:
            self._reduce_pair()
            if self.ci < self.npairs:
                self._trigger_pair()
            n += 1

    def finale(self):
        while self.ri < self.npairs:
            self._reduce_pair()


class FFNBuilder:
    def __init__(self, core_builder):
        self.B = core_builder
        self.nc = core_builder.nc

    def declare(self, pw=None):
        nc = self.nc
        B = self.B
        dt = nc.dram_tensor
        if pw is not None:
            self.d_w1T = pw.d_w1T
            self.d_w2T = pw.d_w2T
        else:
            self.d_w1T = dt("ffn_w1T", [D, FF], F32,
                            kind="ExternalInput").ap()
            self.d_w2T = dt("ffn_w2T", [FF, D], F32,
                            kind="ExternalInput").ap()
        self.d_b1 = dt("ffn_b1", [1, FF], F32, kind="ExternalInput").ap()
        self.d_gamma = dt("ffn_gamma", [1, FF], F32, kind="ExternalInput").ap()
        self.d_beta = dt("ffn_beta", [1, FF], F32, kind="ExternalInput").ap()
        self.d_b2 = dt("ffn_b2", [1, D], F32, kind="ExternalInput").ap()
        self.d_xrows = dt("ffn_xrows", [128, NT], mybir.dt.int32,
                          kind="ExternalInput").ap()
        self.d_xrows16 = dt("ffn_xrows16", [128, NT], mybir.dt.int32,
                            kind="ExternalInput").ap()
        self.d_yT = dt("ffn_yT", [D, TB], F32, kind="ExternalOutput").ap()
        self.d_h = dt("ffn_hscr", [TB, FF], F32)

        S = B.sbuf
        self.hqT = S("f_hqT", [128, KF * TB], BF16)
        self.wf1 = [S(f"f_wf1_{i}", [128, KD * 128]) for i in range(2)]
        self.w1q = [S(f"f_w1q_{i}", [128, KD * 128], BF16) for i in range(2)]
        self.qa = S("f_qa", [128, KD * 128], BF16)
        self.s1 = S("f_s1", [128, D])
        self.s2 = S("f_s2", [128, D])
        self.qbf = S("f_qbf", [128, D], BF16)
        self.bnb = S("f_bnb", [128, NT * KF * 6])
        _qaf = self.qa.bitcast(F32)
        self.gbc = _qaf[:, 0:512]
        self.bbc = _qaf[:, 512:1024]
        self.htile = [S(f"f_ht{i}", [128, 128]) for i in range(2)]
        self.hgl = [S(f"f_hg{i}", [128, 128]) for i in range(2)]
        self.b1bc = S("f_b1bc", [128, 128])
        self.fst = S("f_stat", [128, 16])
        self.bx = S("f_bx", [128, NT])
        self.mv = S("f_mv", [128, NT * 2])
        self.mu = S("f_mu", [128, NT])
        self.rho = S("f_rho", [128, NT])
        self.mx2 = S("f_mx2", [128, NT])
        self.s2q = S("f_s2q", [128, NT])
        self.b2neg = S("f_b2neg", [128, NT])
        self.thc = S("f_thc", [128, 4])
        self.swc = S("f_swc", [128, 2])
        self.scl = S("f_scl", [1, 8])
        self.b2row = S("f_b2row", [1, TB])
        self.row8 = S("f_row8", [8, 128])
        self.b2c = S("f_b2c", [128, KD])
        self.bet2b = self.s2[:, TB:2 * TB]
        self.ytT = self.s2[:, 0:TB]
        self.x2c = self.s1[:, 0:NT * 128]
        self.rs1 = S("f_rs1", [1, 512])
        self.rs2 = S("f_rs2", [1, 512])
        self.xrows = S("f_xrows_sb", [128, NT], mybir.dt.int32)
        self.xrows16 = S("f_xrows16_sb", [128, NT], mybir.dt.int32)
        self.xqT = self.hqT

        for s in ("fw0", "fw1", "fh0", "fh1", "fo", "fo0", "fo1", "fx",
                  "fy", "fr", "fxr", "fg"):
            B.new_sem(s)

    def emit(self, barrier=None, pw=None):
        B = self.B
        if barrier:
            for eng in ("sync", "gpsimd", "vector", "scalar", "tensor"):
                for sem_name, v in barrier.items():
                    B.wait(eng, sem_name, v)
        B.dma16("sync", lambda e: e.dma_start(
            out=self.xrows[:, :], in_=self.d_xrows), "fxr")
        B.dma16("sync", lambda e: e.dma_start(
            out=self.xrows16[:, :], in_=self.d_xrows16), "fxr")
        self._xrows_fr = B.val("fxr")
        if pw is not None:
            self.phaseW_finale(pw)
        else:
            self.phaseW()
        self.phaseX()
        self.phaseF1()
        self.phaseF3()
        self.phaseF4()
        self.phaseF5()

    def phaseW(self):
        B = self.B
        FB = self
        ms = B.tick("vector", lambda e: e.memset(FB.fst[:, :], 0.0), "cV")
        B.wait("vector", "cV", ms)
        self.rd_tick = {0: 0, 1: 0}
        nld = {0: 0, 1: 0}

        def stream(src_ap, acc_col, i, total_i):
            sl = i % 2
            sem = "fw%d" % sl
            if self.rd_tick[sl]:
                B.wait("sync", "cV", self.rd_tick[sl])
            nld[sl] += 1
            B.dma16("sync", lambda e, src_ap=src_ap, sl=sl: e.dma_start(
                out=FB.wf1[sl][:, :], in_=src_ap), sem)
            B.wait("vector", sem, 16 * nld[sl])
            B.em("vector", lambda e, sl=sl: e.tensor_reduce(
                FB.fst[:, 2:3], FB.wf1[sl][:, :], axis=mybir.AxisListType.X,
                op=ALU.add, apply_absolute_value=True))
            B.drain("vector")
            t = B.tick("vector", lambda e, acc_col=acc_col: e.tensor_tensor(
                FB.fst[:, acc_col:acc_col + 1], FB.fst[:, acc_col:acc_col + 1],
                FB.fst[:, 2:3], op=ALU.add), "cV")
            B.drain("vector")
            self.rd_tick[sl] = t

        i = 0
        for dd in range(KD):
            for cc in range(4):
                stream(self.d_w1T[128 * dd:128 * (dd + 1),
                                  2048 * cc:2048 * (cc + 1)], 0, i, 128)
                i += 1
        for fj in range(KF):
            stream(self.d_w2T[128 * fj:128 * (fj + 1), :], 1, i, 128)
            i += 1
        self.w_nld = dict(nld)
        self._w_tail(FB.fst[:, 0:2])

    def phaseW_finale(self, pw):
        pw.finale()
        self.rd_tick = {0: 0, 1: 0}
        self.w_nld = {0: 0, 1: 0}
        self._w_tail(pw.pacc[:, 0:2])

    def _w_tail(self, acc_ap):
        B = self.B
        FB = self
        B.wait("tensor", "cV", B.val("cV"))
        tm = B.t_mm(B.P6[0:1, 0:2], B.onesc[:, 0:1], acc_ap,
                    True, True, tick=True)
        B.wait("vector", "cT", tm)
        B.em("vector", lambda e: e.tensor_scalar(
            FB.scl[0:1, 0:2], B.P6[0:1, 0:2], NW, 1e-5,
            op0=ALU.mult, op1=ALU.max))
        B.drain("vector")
        B.em("vector", lambda e: e.tensor_scalar(
            FB.scl[0:1, 2:4], FB.scl[0:1, 0:2], 0.5, None, op0=ALU.mult))
        B.drain("vector")
        t = B.tick("vector", lambda e: e.tensor_scalar(
            FB.scl[0:1, 4:6], FB.scl[0:1, 2:4], -1.0, None, op0=ALU.mult),
            "cV")
        B.wait("tensor", "cV", t)
        B.t_mm(B.P6[:, 4:8], B.onesr[0:1, :], FB.scl[0:1, 2:6], True, True)
        tb = B.t_mm(B.P6[:, 8:10], B.onesr[0:1, :], FB.scl[0:1, 0:2],
                    True, True, tick=True)
        B.wait("vector", "cT", tb)
        B.em("vector", lambda e: e.tensor_copy(FB.thc[:, 0:4], B.P6[:, 4:8]))
        self.w_done = B.tick("vector", lambda e: e.tensor_copy(
            FB.swc[:, 0:2], B.P6[:, 8:10]), "cV")

    def phaseX(self):
        B = self.B
        FB = self
        self.xev = [0] * KD
        nfx = B.val("fx")
        B.wait("gpsimd", "fxr", self._xrows_fr)
        for tt in range(NT):
            B.wait("gpsimd", "cV", getattr(self, "_x_s1r", 0))
            nfx = B.dma16("gpsimd", lambda e, tt=tt: e.indirect_dma_start(
                out=FB.s1[:, :], out_offset=None,
                in_=B.d_x2,
                in_offset=bass_rust.IndirectOffsetOnAxis(
                    ap=FB.xrows[:, tt:tt + 1], axis=0)), "fg")
            B.wait("vector", "fg", nfx)
            B.drain("vector")
            for q in range(4):
                B.em("vector", lambda e, q=q, tt=tt: e.bn_stats(
                    FB.bnb[:, 24 * tt + 6 * q:24 * tt + 6 * (q + 1)],
                    FB.s1[:, 512 * q:512 * (q + 1)]))
            B.drain("vector")
            vb = B.tick("vector", lambda e, tt=tt: e.bn_aggr(
                FB.mv[:, 2 * tt:2 * (tt + 1)],
                FB.bnb[:, 24 * tt:24 * tt + 24]), "cV")
            B.wait("scalar", "cV", vb)
            sq = B.tick("scalar", lambda e, tt=tt: e.activation(
                FB.rho[:, tt:tt + 1], FB.mv[:, 2 * tt + 1:2 * tt + 2],
                AF.Sqrt, bias=B.epscol[:, 0:1]), "cS")
            B.wait("vector", "cS", sq)
            B.em("vector", lambda e, tt=tt: e.reciprocal(
                FB.rho[:, tt:tt + 1], FB.rho[:, tt:tt + 1]))
            B.drain("vector")
            B.em("vector", lambda e, tt=tt: e.tensor_scalar(
                FB.s2[:, :], FB.s1[:, :], FB.mv[:, 2 * tt:2 * tt + 1],
                FB.rho[:, tt:tt + 1], op0=ALU.subtract, op1=ALU.mult))
            B.drain("vector")
            B.em("vector", lambda e: e.tensor_reduce(
                FB.fst[:, 4:5], FB.s2[:, :], axis=mybir.AxisListType.X,
                op=ALU.max, apply_absolute_value=True))
            B.drain("vector")
            B.em("vector", lambda e: e.tensor_scalar(
                FB.fst[:, 5:6], FB.fst[:, 4:5], 1e-5, None, op0=ALU.max))
            B.drain("vector")
            if tt == 0:
                B.wait("vector", "cV", self.w_done)
            B.em("vector", lambda e, tt=tt: e.scalar_tensor_tensor(
                FB.bx[:, tt:tt + 1], FB.fst[:, 5:6], -1.0 / 127.0,
                FB.swc[:, 0:1], op0=ALU.mult, op1=ALU.mult))
            B.em("vector", lambda e: e.reciprocal(
                FB.fst[:, 6:7], FB.fst[:, 5:6]))
            B.drain("vector")
            B.em("vector", lambda e: e.tensor_scalar(
                FB.fst[:, 7:8], FB.fst[:, 6:7], 127.0, None, op0=ALU.mult))
            B.drain("vector")
            B.em("vector", lambda e: e.tensor_scalar(
                FB.s1[:, :], FB.s2[:, :], FB.fst[:, 7:8], None,
                op0=ALU.mult))
            B.drain("vector")
            B.em("vector", lambda e: e.tensor_copy(
                FB.s2[:, :].bitcast(mybir.dt.int32), FB.s1[:, :]))
            B.drain("vector")
            vq = B.tick("vector", lambda e: e.tensor_copy(
                FB.qbf[:, :], FB.s2[:, :].bitcast(mybir.dt.int32)), "cV")
            self._x_s1r = vq
            B.wait("tensor", "cV", vq)
            for dd in range(KD):
                pb = (B.P4 if dd % 2 == 0 else B.P5)[:, 0:64].bitcast(BF16)
                if self.xev[dd - 2] if dd >= 2 else 0:
                    B.wait("tensor", "cV", self.xev[dd - 2])
                tk = B.tick("tensor", lambda e, pb=pb, dd=dd: e.transpose(
                    pb, FB.qbf[:, 128 * dd:128 * (dd + 1)],
                    B.identb[:, :]), "cT")
                B.wait("vector", "cT", tk)
                self.xev[dd] = B.tick(
                    "vector", lambda e, pb=pb, dd=dd, tt=tt: e.tensor_copy(
                        FB.xqT[:, TB * dd + 128 * tt:TB * dd + 128 * (tt + 1)],
                        pb), "cV")
        self.x_done = B.val("cV")

    def phaseF1(self):
        B = self.B
        FB = self
        q_tick = {0: 0, 1: 0}
        mm_tick = {0: 0, 1: 0}
        self.f1_nld = {0: self.w_nld[0], 1: self.w_nld[1]}
        h_ev = [0, 0]
        pv_ev = [0, 0]
        fo_at = [0, 0]
        hst = 0
        self._rs1_free = 0
        for fc in range(KF):
            sl = fc % 2
            sem = "fw%d" % sl
            if q_tick[sl]:
                B.wait("sync", "cV", q_tick[sl])
            elif self.rd_tick[sl]:
                B.wait("sync", "cV", self.rd_tick[sl])
            self.f1_nld[sl] += 1
            B.dma16("sync", lambda e, fc=fc, sl=sl: e.dma_start(
                out=FB.wf1[sl][:, :].rearrange("p (j m) -> p j m", m=128),
                in_=FB.d_w1T.rearrange("(j p) m -> p j m", p=128)
                [:, :, 128 * fc:128 * (fc + 1)]), sem)
            B.wait("vector", sem, 16 * self.f1_nld[sl])
            if fc == 0:
                B.wait("vector", "cV", self.x_done)
            if mm_tick[sl]:
                B.wait("vector", "cT", mm_tick[sl])
            B.drain("vector")
            B.em("vector", lambda e, sl=sl: e.tensor_scalar(
                FB.qa[:, :], FB.wf1[sl][:, :], FB.thc[:, 0:1], None,
                op0=ALU.is_gt))
            B.drain("vector")
            qt = B.tick("vector", lambda e, sl=sl: e.scalar_tensor_tensor(
                FB.w1q[sl][:, :], FB.wf1[sl][:, :], FB.thc[:, 2:3],
                FB.qa[:, :], op0=ALU.is_lt, op1=ALU.subtract), "cV")
            q_tick[sl] = qt
            B.wait("sync", "cT", self._rs1_free)
            nfr = B.dma16("sync", lambda e, fc=fc: e.dma_start(
                out=FB.rs1[0:1, 0:128],
                in_=FB.d_b1[0:1, 128 * fc:128 * (fc + 1)]), "fr")
            B.wait("tensor", "cV", qt)
            B.wait("tensor", "fr", nfr)
            bb = B.t_mm(B.P6[:, 0:128], B.onesr[0:1, :],
                        FB.rs1[0:1, 0:128], True, True, tick=True)
            self._rs1_free = bb
            B.wait("vector", "cT", bb)
            bbv = B.tick("vector", lambda e: e.tensor_copy(
                FB.b1bc[:, :], B.P6[:, 0:128]), "cV")
            B.wait("tensor", "cV", bbv)
            for tt in range(NT):
                pb = B.P1 if tt % 2 == 0 else B.P2
                if hst >= 2:
                    B.wait("tensor", "cV", pv_ev[hst % 2])
                for dd in range(KD):
                    B.t_mm(pb[:, 0:128],
                           FB.xqT[:, TB * dd + 128 * tt:
                                  TB * dd + 128 * (tt + 1)],
                           FB.w1q[sl][:, 128 * dd:128 * (dd + 1)],
                           dd == 0, dd == KD - 1, tick=(dd == KD - 1))
                mt = B.val("cT")
                if tt == NT - 1:
                    mm_tick[sl] = mt
                hb = hst % 2
                B.wait("vector", "cT", mt)
                if hst >= 2:
                    B.wait("vector", "cS", h_ev[hb])
                B.drain("vector")
                ve = B.tick("vector", lambda e, pb=pb, hb=hb, tt=tt:
                            e.scalar_tensor_tensor(
                                FB.htile[hb][:, :], pb[:, 0:128],
                                FB.bx[:, tt:tt + 1], FB.b1bc[:, :],
                                op0=ALU.mult, op1=ALU.add), "cV")
                pv_ev[hb] = ve
                B.wait("scalar", "cV", ve)
                if hst >= 2:
                    B.wait("scalar", "fo%d" % hb, fo_at[hb])
                sg = B.tick("scalar", lambda e, hb=hb: e.activation(
                    FB.hgl[hb][:, :], FB.htile[hb][:, :], AF.Gelu), "cS")
                h_ev[hb] = sg
                B.wait("vector", "cS", sg)
                B.drain("vector")
                B.em("vector", lambda e, hb=hb, tt=tt, fc=fc: e.bn_stats(
                    FB.bnb[:, (tt * KF + fc) * 6:(tt * KF + fc) * 6 + 6],
                    FB.hgl[hb][:, :]))
                B.wait("scalar", "cS", sg)
                fo_at[hb] = B.dma16(
                    "scalar", lambda e, hb=hb, tt=tt, fc=fc: e.dma_start(
                        out=FB.d_h[128 * tt:128 * (tt + 1),
                                   128 * fc:128 * (fc + 1)],
                        in_=FB.hgl[hb][:, :]), "fo%d" % hb)
                hst += 1
        self.f1_fo = (B.val("fo0"), B.val("fo1"))
        self.f1_q = dict(q_tick)
        self.f1_mm = dict(mm_tick)

    def phaseF3(self):
        B = self.B
        FB = self
        B.drain("vector")
        for tt in range(NT):
            vb = B.tick("vector", lambda e, tt=tt: e.bn_aggr(
                FB.mv[:, 2 * tt:2 * (tt + 1)],
                FB.bnb[:, tt * KF * 6:(tt + 1) * KF * 6]), "cV")
            B.drain("vector")
            B.em("vector", lambda e, tt=tt: e.tensor_copy(
                FB.mu[:, tt:tt + 1], FB.mv[:, 2 * tt:2 * tt + 1]))
            B.wait("scalar", "cV", B.val("cV"))
            sq = B.tick("scalar", lambda e, tt=tt: e.activation(
                FB.rho[:, tt:tt + 1], FB.mv[:, 2 * tt + 1:2 * tt + 2],
                AF.Sqrt, bias=B.epscol[:, 0:1]), "cS")
            B.wait("vector", "cS", sq)
            B.em("vector", lambda e, tt=tt: e.reciprocal(
                FB.rho[:, tt:tt + 1], FB.rho[:, tt:tt + 1]))
            B.drain("vector")
        mz = B.tick("vector", lambda e: e.memset(FB.mx2[:, :], 0.0), "cV")
        B.wait("vector", "cV", mz)
        B.wait("vector", "fo0", self.f1_fo[0])
        B.wait("vector", "fo1", self.f1_fo[1])
        nfx = B.val("fx")
        nfo = B.val("fo")
        for gc in range(KF // 4):
            B.wait("sync", "cT", getattr(self, "_rs1_free", 0))
            nfr = B.dma16("sync", lambda e, gc=gc: e.dma_start(
                out=FB.rs1[0:1, :],
                in_=FB.d_gamma[0:1, 512 * gc:512 * (gc + 1)]), "fr")
            nfr = B.dma16("sync", lambda e, gc=gc: e.dma_start(
                out=FB.rs2[0:1, :],
                in_=FB.d_beta[0:1, 512 * gc:512 * (gc + 1)]), "fr")
            B.wait("tensor", "fr", nfr)
            gb = B.t_mm(B.P6[:, 0:512], B.onesr[0:1, :],
                        FB.rs1[0:1, :], True, True, tick=True)
            B.wait("vector", "cT", gb)
            B.tick("vector", lambda e: e.tensor_copy(
                FB.gbc[:, :], B.P6[:, 0:512]), "cV")
            bb = B.t_mm(B.P7[:, 0:512], B.onesr[0:1, :],
                        FB.rs2[0:1, :], True, True, tick=True)
            self._rs1_free = bb
            B.wait("vector", "cT", bb)
            B.tick("vector", lambda e: e.tensor_copy(
                FB.bbc[:, :], B.P7[:, 0:512]), "cV")
            B.drain("vector")
            for tt in range(NT):
                B.wait("sync", "cV", getattr(self, "_f3_s1r", 0))
                nfx = B.dma16("sync", lambda e, tt=tt, gc=gc: e.dma_start(
                    out=FB.s1[:, 0:512],
                    in_=FB.d_h[128 * tt:128 * (tt + 1),
                               512 * gc:512 * (gc + 1)]), "fx")
                B.wait("vector", "fx", nfx)
                B.wait("vector", "fo", nfo)
                B.drain("vector")
                s1r = B.tick("vector", lambda e, tt=tt: e.tensor_scalar(
                    FB.s2[:, 0:512], FB.s1[:, 0:512], FB.mu[:, tt:tt + 1],
                    FB.rho[:, tt:tt + 1], op0=ALU.subtract, op1=ALU.mult),
                    "cV")
                self._f3_s1r = s1r
                B.drain("vector")
                B.em("vector", lambda e: e.tensor_tensor(
                    FB.s2[:, 0:512], FB.s2[:, 0:512], FB.gbc[:, :],
                    op=ALU.mult))
                B.drain("vector")
                B.em("vector", lambda e: e.tensor_tensor(
                    FB.s2[:, 0:512], FB.s2[:, 0:512], FB.bbc[:, :],
                    op=ALU.add))
                B.drain("vector")
                B.em("vector", lambda e: e.tensor_reduce(
                    FB.fst[:, 8:9], FB.s2[:, 0:512],
                    axis=mybir.AxisListType.X, op=ALU.max,
                    apply_absolute_value=True))
                B.drain("vector")
                B.em("vector", lambda e, tt=tt: e.tensor_tensor(
                    FB.mx2[:, tt:tt + 1], FB.mx2[:, tt:tt + 1],
                    FB.fst[:, 8:9], op=ALU.max))
                B.drain("vector")
                hv = B.tick("vector", lambda e: e.tensor_copy(
                    FB.fst[:, 10:11], FB.fst[:, 10:11]), "cV")
                B.wait("scalar", "cV", hv)
                nfo = B.dma16("scalar",
                              lambda e, tt=tt, gc=gc: e.dma_start(
                                  out=FB.d_h[128 * tt:128 * (tt + 1),
                                             512 * gc:512 * (gc + 1)],
                                  in_=FB.s2[:, 0:512]), "fo")
        self.f3_fo = B.val("fo")

    def phaseF4(self):
        B = self.B
        FB = self
        B.drain("vector")
        for tt in range(NT):
            B.em("vector", lambda e, tt=tt: e.tensor_scalar(
                FB.fst[:, 8:9], FB.mx2[:, tt:tt + 1], 1e-5, None,
                op0=ALU.max))
            B.drain("vector")
            B.em("vector", lambda e, tt=tt: e.scalar_tensor_tensor(
                FB.b2neg[:, tt:tt + 1], FB.fst[:, 8:9], -1.0 / 127.0,
                FB.swc[:, 1:2], op0=ALU.mult, op1=ALU.mult))
            B.em("vector", lambda e: e.reciprocal(
                FB.fst[:, 9:10], FB.fst[:, 8:9]))
            B.drain("vector")
            B.em("vector", lambda e, tt=tt: e.tensor_scalar(
                FB.s2q[:, tt:tt + 1], FB.fst[:, 9:10], 127.0, None,
                op0=ALU.mult))
            B.drain("vector")
        B.wait("vector", "fo", self.f3_fo)
        self.fev = [0] * KD
        self._f4_s1r = getattr(self, "_f3_s1r", 0)
        B.wait("sync", "fx", B.val("fx"))
        nfx = B.val("fx")
        for tt in range(NT):
            for gc in range(4):
                B.wait("sync", "cV", getattr(self, "_f4_s1r", 0))
                nfx = B.dma16("sync", lambda e, tt=tt, gc=gc: e.dma_start(
                    out=FB.s1[:, :],
                    in_=FB.d_h[128 * tt:128 * (tt + 1),
                               2048 * gc:2048 * (gc + 1)]), "fx")
                B.wait("vector", "fx", nfx)
                B.drain("vector")
                B.em("vector", lambda e, tt=tt: e.tensor_scalar(
                    FB.s1[:, :], FB.s1[:, :], FB.s2q[:, tt:tt + 1], None,
                    op0=ALU.mult))
                B.drain("vector")
                B.em("vector", lambda e: e.tensor_copy(
                    FB.s2[:, :].bitcast(mybir.dt.int32), FB.s1[:, :]))
                B.drain("vector")
                vq = B.tick("vector", lambda e: e.tensor_copy(
                    FB.qbf[:, :], FB.s2[:, :].bitcast(mybir.dt.int32)), "cV")
                self._f4_s1r = vq
                B.wait("tensor", "cV", vq)
                for k in range(KD):
                    fj = 16 * gc + k
                    pb = (B.P4 if k % 2 == 0 else B.P5)[:, 0:64].bitcast(BF16)
                    if k >= 2 and self.fev[k - 2]:
                        B.wait("tensor", "cV", self.fev[k - 2])
                    tk = B.tick("tensor", lambda e, pb=pb, k=k: e.transpose(
                        pb, FB.qbf[:, 128 * k:128 * (k + 1)],
                        B.identb[:, :]), "cT")
                    B.wait("vector", "cT", tk)
                    self.fev[k] = B.tick(
                        "vector",
                        lambda e, pb=pb, fj=fj, tt=tt: e.tensor_copy(
                            FB.hqT[:, TB * fj + 128 * tt:
                                   TB * fj + 128 * (tt + 1)], pb), "cV")
        self.f4_done = B.val("cV")

    def phaseF5(self):
        B = self.B
        FB = self

        bt = B.val("cV")
        B.wait("tensor", "cV", bt)
        tb = B.tick("tensor", lambda e: e.transpose(
            B.P4[0:NT, 0:128], FB.b2neg[:, 0:NT], B.ident[:, :]), "cT")
        B.wait("vector", "cT", tb)
        bv = B.tick("vector", lambda e: e.tensor_copy(
            FB.row8[0:NT, :], B.P4[0:NT, 0:128]), "cV")
        B.wait("sync", "cV", bv)
        nbr = B.dma16("sync", lambda e: e.dma_start(
            out=FB.b2row[0:1, :].rearrange("o (t p) -> o t p", p=128),
            in_=FB.row8[0:NT, :].rearrange("t p -> o t p", o=1)
            if False else FB.row8[0:NT, :]), "fr")
        B.wait("tensor", "fr", nbr)
        prevc = 0
        for hh in range(2):
            if prevc:
                B.wait("tensor", "cV", prevc)
            th = B.t_mm(B.P7[:, 0:512], B.onesr[0:1, :],
                        FB.b2row[0:1, 512 * hh:512 * (hh + 1)], True, True,
                        tick=True)
            B.wait("vector", "cT", th)
            prevc = B.tick("vector", lambda e, hh=hh: e.tensor_copy(
                FB.bet2b[:, 512 * hh:512 * (hh + 1)], B.P7[:, 0:512]), "cV")
        nfx = B.dma16("sync", lambda e: e.dma_start(
            out=FB.b2c[:, :],
            in_=FB.d_b2.rearrange("o (j p) -> (o p) j", p=128),
            allow_slow_non_contiguous=True), "fx")
        bet_done = B.val("cV")

        q_tick = {0: 0, 1: 0}
        f5mm = {0: 0, 1: 0}
        nld = dict(self.f1_nld)
        yout_tick = {}
        nfy = 0
        x2tr_done = 0
        for dd in range(KD):
            ppair = [B.P1, B.P2] if dd % 2 == 0 else [B.P3, B.P0]
            B.wait("gpsimd", "cT", x2tr_done)
            if dd == 0:
                B.wait("gpsimd", "cV", self._f4_s1r)
                B.wait("gpsimd", "fx", B.val("fx"))
            for tt in range(NT):
                nfx = B.dma16(
                    "gpsimd", lambda e, dd=dd, tt=tt: e.indirect_dma_start(
                        out=FB.x2c[:, 128 * tt:128 * (tt + 1)],
                        out_offset=None,
                        in_=B.d_x2.rearrange("t (j m) -> (t j) m", m=128),
                        element_offset=128 * dd,
                        in_offset=bass_rust.IndirectOffsetOnAxis(
                            ap=FB.xrows16[:, tt:tt + 1], axis=0)), "fg")
            for fg in range(4):
                sl = (4 * dd + fg) % 2
                sem = "fw%d" % sl
                if q_tick[sl]:
                    B.wait("sync", "cV", q_tick[sl])
                elif self.f1_q[sl]:
                    B.wait("sync", "cV", self.f1_q[sl])
                nld[sl] += 1
                B.dma16("sync", lambda e, dd=dd, fg=fg, sl=sl: e.dma_start(
                    out=FB.wf1[sl][:, :].rearrange("p (j m) -> p j m", m=128),
                    in_=FB.d_w2T.rearrange("(j p) m -> p j m", p=128)
                    [:, 16 * fg:16 * (fg + 1), 128 * dd:128 * (dd + 1)]),
                    sem)
                B.wait("vector", sem, 16 * nld[sl])
                if dd == 0 and fg == 0:
                    B.wait("tensor", "cV", self.f4_done)
                if dd <= 1 and fg <= 1:
                    B.wait("vector", "cT", self.f1_mm[sl])
                if f5mm[sl]:
                    B.wait("vector", "cT", f5mm[sl])
                B.drain("vector")
                B.em("vector", lambda e, sl=sl: e.tensor_scalar(
                    FB.qa[:, :], FB.wf1[sl][:, :], FB.thc[:, 1:2], None,
                    op0=ALU.is_gt))
                B.drain("vector")
                qt = B.tick("vector", lambda e, sl=sl: e.scalar_tensor_tensor(
                    FB.w1q[sl][:, :], FB.wf1[sl][:, :], FB.thc[:, 3:4],
                    FB.qa[:, :], op0=ALU.is_lt, op1=ALU.subtract), "cV")
                q_tick[sl] = qt
                B.wait("tensor", "cV", qt)
                if dd >= 2 and fg == 0:
                    B.wait("tensor", "cV", yout_tick[dd - 2])
                for k in range(KD):
                    fj = 16 * fg + k
                    for hh in range(2):
                        t5 = B.t_mm(ppair[hh][:, :],
                                    FB.w1q[sl][:, 128 * k:128 * (k + 1)],
                                    FB.hqT[:, TB * fj + 512 * hh:
                                           TB * fj + 512 * (hh + 1)],
                                    (fg == 0 and k == 0),
                                    fg == 3 and k == KD - 1,
                                    tick=(k == KD - 1 and hh == 1))
                        if k == KD - 1 and hh == 1:
                            f5mm[sl] = t5
            mt = B.val("cT")
            B.wait("vector", "cT", mt)
            if dd >= 1:
                B.wait("vector", "fy", nfy)
            B.drain("vector")
            for hh in range(2):
                if hh == 1:
                    ye = B.tick("vector",
                                lambda e, ppair=ppair, hh=hh: e.tensor_tensor(
                                    FB.ytT[:, 512 * hh:512 * (hh + 1)],
                                    ppair[hh][:, :],
                                    FB.bet2b[:, 512 * hh:512 * (hh + 1)],
                                    op=ALU.mult), "cV")
                else:
                    B.em("vector",
                         lambda e, ppair=ppair, hh=hh: e.tensor_tensor(
                             FB.ytT[:, 512 * hh:512 * (hh + 1)],
                             ppair[hh][:, :],
                             FB.bet2b[:, 512 * hh:512 * (hh + 1)],
                             op=ALU.mult))
            B.drain("vector")
            B.wait("tensor", "cV", ye)
            B.wait("tensor", "fg", nfx)
            sts = []
            for tt in range(NT):
                pb = (B.P4 if tt % 2 == 0 else B.P5)[:, 128:256]
                if tt >= 2:
                    B.wait("tensor", "cV", sts[tt - 2])
                tk = B.tick("tensor", lambda e, pb=pb, tt=tt: e.transpose(
                    pb, FB.x2c[:, 128 * tt:128 * (tt + 1)], B.ident[:, :]),
                    "cT")
                B.wait("vector", "cT", tk)
                st = B.tick("vector", lambda e, pb=pb, tt=tt, dd=dd:
                            e.scalar_tensor_tensor(
                                FB.ytT[:, 128 * tt:128 * (tt + 1)],
                                FB.ytT[:, 128 * tt:128 * (tt + 1)],
                                FB.b2c[:, dd:dd + 1], pb,
                                op0=ALU.add, op1=ALU.add), "cV")
                B.drain("vector")
                sts.append(st)
            yout_tick[dd] = ye
            x2tr_done = B.val("cT")
            B.wait("scalar", "cV", sts[-1])
            nfy = B.dma16("scalar", lambda e, dd=dd: e.dma_start(
                out=FB.d_yT[128 * dd:128 * (dd + 1), :],
                in_=FB.ytT[:, :]), "fy")
        B.wait("gpsimd", "fy", nfy)


def ffn_host_inputs(w1, w2, b1, gamma, beta, b2):
    return {
        "ffn_w1T": np.ascontiguousarray(w1.T),
        "ffn_w2T": np.ascontiguousarray(w2.T),
        "ffn_b1": np.ascontiguousarray(b1.reshape(1, FF)),
        "ffn_gamma": np.ascontiguousarray(gamma.reshape(1, FF)),
        "ffn_beta": np.ascontiguousarray(beta.reshape(1, FF)),
        "ffn_b2": np.ascontiguousarray(b2.reshape(1, D)),
    }


import concourse.bass as bass
from concourse.bass_utils import run_bass_kernel_spmd

N_CORES = 8
SEQ = NCH * CK
TOK_PER_CORE = SEQ // N_CORES


def _xrows_input(core):
    idx = (TOK_PER_CORE * core + np.arange(TOK_PER_CORE, dtype=np.int32)
           ).reshape(NT, 128)
    return {"ffn_xrows": np.ascontiguousarray(idx.T),
            "ffn_xrows16": np.ascontiguousarray(16 * idx.T)}


def _build_program():
    nc = bass.Bass("TRN2", target_bir_lowering=False)
    T = TTTBuilder(nc)
    PW = PWStream(T)
    T.declare(early=lambda core: PW.declare())
    qs = [PW.npairs // NCH + (1 if i < PW.npairs % NCH else 0)
          for i in range(NCH)]

    def hookfn(s, sub):
        if sub == 0:
            PW.hook(qs[s])

    T.emit(step_hook=hookfn)
    barrier = {s: T.val(s) for s in
               ("cV", "cT", "cS", "sx2", "cc", "sbo", "sbi",
                "sx", "sxc0", "sxc1", "sxc2")}
    ttt_names = [it[0] for it in T.ctx[T._sbuf_mark:]
                 if isinstance(it, tuple)]
    T.free_sbuf(ttt_names)
    FB = FFNBuilder(T)
    FB.declare(pw=PW)
    FB.emit(barrier=barrier, pw=PW)
    with nc.Block() as block:
        T.replay(block)
    return nc


def kernel(**inputs):
    x = np.ascontiguousarray(inputs["L0_attn_x"][0], dtype=np.float32)
    W0 = np.asarray(inputs["L0_attn_W0"], np.float32)
    b0 = np.asarray(inputs["L0_attn_b0"], np.float32)
    W1 = np.asarray(inputs["L0_attn_W1"], np.float32)
    b1a = np.asarray(inputs["L0_attn_b1"], np.float32)
    lr_w = np.asarray(inputs["L0_attn_lr_w"], np.float32)
    lr_b = np.asarray(inputs["L0_attn_lr_b"], np.float32)
    fg_w = np.asarray(inputs["L0_attn_fg_w"], np.float32)
    fg_b = np.asarray(inputs["L0_attn_fg_b"], np.float32)
    fw1 = np.asarray(inputs["L0_ffn_w1"], np.float32)
    fb1 = np.asarray(inputs["L0_ffn_b1"], np.float32)
    gamma = np.asarray(inputs["L0_ffn_gamma"], np.float32)
    beta = np.asarray(inputs["L0_ffn_beta"], np.float32)
    fw2 = np.asarray(inputs["L0_ffn_w2"], np.float32)
    fb2 = np.asarray(inputs["L0_ffn_b2"], np.float32)

    nc = _build_program()
    ffn_common = ffn_host_inputs(fw1, fw2, fb1, gamma, beta, fb2)
    in_maps = []
    for r in range(N_CORES):
        im = host_inputs(x, W0, b0, W1, b1a, lr_w, lr_b, fg_w, fg_b, r)
        im.update(ffn_common)
        im.update(_xrows_input(r))
        in_maps.append(im)
    res = run_bass_kernel_spmd(nc, in_maps, core_ids=list(range(N_CORES)))
    out = np.concatenate(
        [np.ascontiguousarray(res.results[r]["ffn_yT"]).T
         for r in range(N_CORES)], axis=0)
    return out.reshape(1, SEQ, D).astype(np.float32)


# version 05
# speedup vs baseline: 1.2205x; 1.0222x over previous
import sys
if "/opt/trn_rl_repo" not in sys.path:
    sys.path.insert(0, "/opt/trn_rl_repo")

import os
import numpy as np
import concourse.mybir as mybir

SKIP_BIASMM = os.environ.get("TTT_SKIP_BIASMM", "0") == "1"
SKIP_GRADS = os.environ.get("TTT_SKIP_GRADS", "0") == "1"
SKIP_GATES = os.environ.get("TTT_SKIP_GATES", "0") == "1"
TR_F32 = os.environ.get("TTT_TR_F32", "0") == "1"
SKIP_AR = os.environ.get("TTT_SKIP_AR", "0") == "1"
PHASE = int(os.environ.get("TTT_PHASE", "99"))
SUB = int(os.environ.get("TTT_SUB", "99"))

F32 = mybir.dt.float32
F32R = mybir.dt.float32r
AF = mybir.ActivationFunctionType
ALU = mybir.AluOpType

NCH = 64
D = 2048
MC = 256
CK = 128
KD = 16
LR_MEM = 0.01
GFAC = 2.0 / (CK * D)


class Core:
    def __init__(self, nc, n_cores=8):
        self.nc = nc
        self.ops = {e: [] for e in ("tensor", "vector", "scalar", "sync", "gpsimd")}
        self.cnt = {}
        self.n_cores = n_cores
        self.ctx = []
        self.sems = {}


    def em(self, eng, fn):
        self.ops[eng].append(fn)

    def drain(self, eng):
        self.ops[eng].append(lambda e: e.drain())

    def tick(self, eng, fn, sem_name):
        sem = self.sems[sem_name]
        self.ops[eng].append(lambda e, fn=fn, sem=sem: fn(e).then_inc(sem, 1))
        return self.inc(sem_name, 1)

    def dma16(self, eng, fn, sem_name):
        sem = self.sems[sem_name]
        self.ops[eng].append(lambda e, fn=fn, sem=sem: fn(e).then_inc(sem, 16))
        return self.inc(sem_name, 16)

    def wait(self, eng, sem_name, val):
        if val <= 0:
            return
        sem = self.sems[sem_name]
        self.ops[eng].append(lambda e, sem=sem, val=val: e.wait_ge(sem, val))

    def inc(self, name, by):
        self.cnt[name] = self.cnt.get(name, 0) + by
        return self.cnt[name]

    def val(self, name):
        return self.cnt.get(name, 0)

    def t_mm(self, psum_ap, lhsT, rhs, start, stop, tick=False):
        def fn(e, a=psum_ap, l=lhsT, r=rhs, st=start, sp=stop):
            return e.matmul(a, l, r, start=st, stop=sp)
        if tick:
            return self.tick("tensor", fn, "cT")
        self.em("tensor", fn)

    def t_tr(self, psum_ap, in_ap, f32r):
        if TR_F32 and f32r:
            in_ap = in_ap.bitcast(F32)
            f32r = False
        ident = self.identr[:, :] if f32r else self.ident[:, :]
        if f32r:
            psum_ap = psum_ap.bitcast(F32R)

        def fn(e, a=psum_ap, i=in_ap, idt=ident):
            return e.transpose(a, i, idt)
        return self.tick("tensor", fn, "cT")

    def new_sem(self, name):
        cm = self.nc.semaphore(name)
        self.sems[name] = cm.__enter__()
        self.ctx.append(cm)

    def sbuf(self, name, shape, dtyp=F32):
        cm = self.nc.sbuf_tensor(name, shape, dtyp)
        h = cm.__enter__()
        self.ctx.append((name, cm))
        return h

    def free_sbuf(self, names):
        kept = []
        for item in reversed(self.ctx):
            if isinstance(item, tuple) and item[0] in names:
                item[1].__exit__(None, None, None)
            else:
                kept.append(item)
        self.ctx = kept[::-1]

    def declare_common(self):
        nc = self.nc
        dt = nc.dram_tensor
        self.d_ident = dt("ttt_ident", [128, 128], F32, kind="ExternalInput").ap()
        self.d_onesr = dt("ttt_onesr", [1, 128], F32, kind="ExternalInput").ap()
        self.d_onesc = dt("ttt_onesc", [128, 1], F32, kind="ExternalInput").ap()
        self.d_epscol = dt("ttt_epscol", [128, 1], F32, kind="ExternalInput").ap()
        self.d_inv128 = dt("ttt_inv128", [1, 1], F32, kind="ExternalInput").ap()
        self.d_identr = dt("ttt_identr", [128, 128], F32R, kind="ExternalInput").ap()
        self.d_onesrr = dt("ttt_onesrr", [1, 128], F32R, kind="ExternalInput").ap()
        self.d_identb = dt("ttt_identb", [128, 128], mybir.dt.bfloat16,
                           kind="ExternalInput").ap()
        self.ident = self.sbuf("identsb", [128, 128])
        self.identb = self.sbuf("identbsb", [128, 128], mybir.dt.bfloat16)
        self.identr = self.sbuf("identrsb", [128, 128], F32R)
        self.onesrr = self.sbuf("onesrrsb", [1, 128], F32R)
        self.onesr = self.sbuf("onesrsb", [1, 128])
        self.onesc = self.sbuf("onescsb", [128, 1])
        self.epscol = self.sbuf("epscol", [128, 1])
        self.inv128 = self.sbuf("inv128", [1, 1])
        ps = nc.psum_tensor
        self.P0 = ps("PS0", [128, 512], F32).__enter__()
        self.P1 = ps("PS1", [128, 512], F32).__enter__()
        self.P2 = ps("PS2", [128, 512], F32).__enter__()
        self.P3 = ps("PS3", [128, 512], F32).__enter__()
        self.P4 = ps("PS4", [128, 512], F32).__enter__()
        self.P5 = ps("PS5", [128, 512], F32).__enter__()
        self.P6 = ps("PS6", [128, 512], F32).__enter__()
        self.P7 = ps("PS7", [128, 512], F32).__enter__()
        for s in ("cT", "cV", "cS"):
            self.new_sem(s)

    def load_common(self):
        self.new_sem("sconst")
        B = self

        def ld(e, dst, src):
            return e.dma_start(out=dst, in_=src)
        B.dma16("sync", lambda e: ld(e, B.ident[:, :], B.d_ident), "sconst")
        B.dma16("sync", lambda e: ld(e, B.identr[:, :], B.d_identr), "sconst")
        B.dma16("sync", lambda e: ld(e, B.identb[:, :], B.d_identb), "sconst")
        B.dma16("sync", lambda e: ld(e, B.onesrr[:, :], B.d_onesrr), "sconst")
        B.dma16("sync", lambda e: ld(e, B.onesr[:, :], B.d_onesr), "sconst")
        B.dma16("sync", lambda e: ld(e, B.onesc[:, :], B.d_onesc), "sconst")
        B.dma16("sync", lambda e: ld(e, B.epscol[:, :], B.d_epscol), "sconst")
        B.dma16("sync", lambda e: ld(e, B.inv128[:, :], B.d_inv128), "sconst")
        return B.val("sconst")

    def replay(self, block):
        ops = self.ops

        @block.tensor
        def _(e):
            for f in ops["tensor"]:
                f(e)

        @block.vector
        def _(e):
            for f in ops["vector"]:
                f(e)

        @block.scalar
        def _(e):
            for f in ops["scalar"]:
                f(e)

        @block.sync
        def _(e):
            for f in ops["sync"]:
                f(e)

        @block.gpsimd
        def _(e):
            for f in ops["gpsimd"]:
                f(e)


def common_inputs():
    import ml_dtypes
    return {
        "ttt_identb": np.eye(128, dtype=np.float32).astype(ml_dtypes.bfloat16),
        "ttt_ident": np.eye(128, dtype=np.float32),
        "ttt_identr": np.eye(128, dtype=np.float32),
        "ttt_onesr": np.ones((1, 128), np.float32),
        "ttt_onesrr": np.ones((1, 128), np.float32),
        "ttt_onesc": np.ones((128, 1), np.float32),
        "ttt_epscol": np.full((128, 1), 1e-5, np.float32),
        "ttt_inv128": np.full((1, 1), 1.0 / 128, np.float32),
    }



class TTTBuilder(Core):
    def __init__(self, nc, n_cores=8):
        super().__init__(nc, n_cores)
        self.chunk_sx = {}
        self.fwd_z_tick = {}
        self.pred_free = 0
        self.square_tick = {}
        self.tdh_tick = {}
        self.x2_tick = 0
        self.dsb_free = 0
        self.prep_ct = {}

    def declare(self, x2_dst=None, early=None):
        nc = self.nc
        dt = nc.dram_tensor
        self.d_x = dt("ttt_x", [NCH * CK, D], F32, kind="ExternalInput").ap()
        self.d_W0T = dt("ttt_W0T", [D, MC], F32, kind="ExternalInput").ap()
        self.d_b0 = dt("ttt_b0", [1, MC], F32, kind="ExternalInput").ap()
        self.d_W1T = dt("ttt_W1T", [MC, D], F32, kind="ExternalInput").ap()
        self.d_W1n = dt("ttt_W1n", [D, MC], F32R, kind="ExternalInput").ap()
        self.d_b1 = dt("ttt_b1", [1, D], F32R, kind="ExternalInput").ap()
        self.d_lrfgT = dt("ttt_lrfgT", [D, 2], F32, kind="ExternalInput").ap()
        self.d_lrb = dt("ttt_lrb", [1, 1], F32, kind="ExternalInput").ap()
        self.d_fgb = dt("ttt_fgb", [1, 1], F32, kind="ExternalInput").ap()
        if x2_dst is None:
            x2_dst = dt("ttt_x2", [NCH * CK, D], F32, kind="ExternalOutput").ap()
        self.d_x2 = x2_dst
        self.bin = [dt(f"ttt_arin{i}", [2 * CK, D], F32) for i in range(2)]
        self.bout = [dt(f"ttt_arout{i}", [2 * CK, D], F32) for i in range(2)]

        self.declare_common()
        if early is not None:
            early(self)
        self._sbuf_mark = len(self.ctx)
        S = self.sbuf
        self.W0T = S("W0T", [128, KD * MC])
        self.W1T = S("W1T", [128, 2 * D])
        self.W1n = S("W1n", [128, KD * MC], F32R)
        self.b0 = S("b0sb", [128, 2])
        self.b1 = S("b1sb", [1, D], F32R)
        self.lrfgT = S("lrfgT", [128, KD * 2])
        self.lrbcol = S("lrbcol", [128, 1])
        self.fgb11 = S("fgb11", [1, 1])
        self.lrb11 = S("lrb11", [1, 1])
        self.xraw = [S(f"xraw{i}", [128, D]) for i in range(3)]
        self.ctok = [S(f"ctok{i}", [128, D]) for i in range(2)]
        self.cT = S("cTs", [128, KD * 256])
        self.zsb = [S(f"zsb{i}", [128, 2 * MC]) for i in range(2)]
        self.H2 = [S(f"H2sb{i}", [128, 2 * MC]) for i in range(2)]
        self.pred = S("predsb", [128, 2 * D])
        self.part = S("partsb", [128, 2 * D])
        self.dsb = S("dsb", [128, D], F32R)
        self.dT = S("dTsb", [128, KD * 128], F32R)
        self.hs = S("hssb", [128, MC], F32R)
        self.dzf = S("dzfsb", [128, MC], F32R)
        self.dzt = S("dztsb", [128, MC], F32R)
        self.x2sb = S("x2sb", [128, D])
        self.sg = S("sgsb", [128, 2 * 128])
        self.sp1 = S("sp1sb", [128, 2 * 128])
        self.sp2 = S("sp2sb", [128, 2 * 128])
        self.stat = S("statsb", [128, 8])
        self.gred = S("gredsb", [128, 2])
        self.scal = [S(f"scal{i}", [1, 4]) for i in range(3)]
        self.fk = [S(f"fkcol{i}", [128, 2]) for i in range(3)]
        self.k8m = [S(f"k8mat{i}", [128, 128], F32R) for i in range(3)]
        self.k8r = S("k8row", [1, 128])
        self.gb0c = S("gb0c", [128, 2])
        self.ctokr = S("ctokr", [128, D], F32R)
        self.k2sb = S("k2sb", [128, 256], F32R)

        for s in ("sx", "sxc0", "sxc1", "sxc2", "sbo", "sbi", "sx2", "cc"):
            self.new_sem(s)

    TTT_BIG = ("W0T", "W1T", "W1n", "xraw0", "xraw1", "xraw2", "ctok0",
               "ctok1", "cTs", "zsb0", "zsb1", "H2sb0", "H2sb1",
               "predsb", "partsb", "dsb", "dTsb", "hssb", "dzfsb", "dztsb",
               "x2sb", "sgsb", "sp1sb", "sp2sb", "k8mat0", "k8mat1",
               "k8mat2", "ctokr", "k2sb", "lrfgT")

    def emit(self, step_hook=None):
        self.step_hook = step_hook
        if PHASE == 0:
            self.prologue_noflood()
            self.debug_exit()
            return
        self.prologue()
        if PHASE == 1:
            self.debug_exit()
            return
        for s in range(min(NCH, PHASE - 1)):
            self.step(s)
        if PHASE - 1 < NCH:
            self.debug_exit()
            return
        self.epilogue()

    def prologue_noflood(self):
        real_fwd = self.fwd
        self.fwd = lambda s: None
        try:
            self.prologue()
        finally:
            self.fwd = real_fwd

    def debug_exit(self):
        B = self
        B.wait("gpsimd", "cV", B.val("cV"))
        B.wait("gpsimd", "cT", B.val("cT"))
        B.wait("gpsimd", "cS", B.val("cS"))
        B.dma16("gpsimd", lambda e: e.dma_start(
            out=B.d_x2[0:128, :], in_=B.ctok[0][:, :]), "sx2")
        B.wait("gpsimd", "sx2", B.val("sx2"))

    def prologue(self):
        B = self

        def ld(e, dst, src, allow_slow_non_contiguous=False):
            return e.dma_start(out=dst, in_=src,
                               allow_slow_non_contiguous=allow_slow_non_contiguous)

        B.dma16("sync", lambda e: ld(
            e, B.W0T[:, :].rearrange("p (j m) -> p j m", m=MC),
            B.d_W0T.rearrange("(j p) m -> p j m", p=128)), "sx")
        B.dma16("sync", lambda e: ld(
            e, B.W1T[:, :].rearrange("p (t m) -> p t m", m=D),
            B.d_W1T.rearrange("(t p) m -> p t m", p=128)), "sx")
        B.dma16("sync", lambda e: ld(
            e, B.W1n[:, :].rearrange("p (j m) -> p j m", m=MC),
            B.d_W1n.rearrange("(j p) m -> p j m", p=128)), "sx")
        B.dma16("sync", lambda e: ld(
            e, B.b0[:, :], B.d_b0.rearrange("o (t p) -> (o p) t", p=128),
        allow_slow_non_contiguous=True), "sx")
        B.dma16("sync", lambda e: ld(e, B.b1[:, :], B.d_b1), "sx")
        B.dma16("sync", lambda e: ld(
            e, B.lrfgT[:, :].rearrange("p (j m) -> p j m", m=2),
            B.d_lrfgT.rearrange("(j p) m -> p j m", p=128)), "sx")
        B.dma16("sync", lambda e: ld(e, B.fgb11[:, :], B.d_fgb), "sx")
        B.dma16("sync", lambda e: ld(e, B.lrb11[:, :], B.d_lrb), "sx")
        ncst = B.load_common()
        nload = B.val("sx")
        B.wait("vector", "sconst", ncst)
        B.wait("tensor", "sconst", ncst)
        B.wait("scalar", "sconst", ncst)

        for c in range(2):
            B.chunk_sx[c] = ("sxc%d" % (c % 3), B.dma16("sync", lambda e, c=c: ld(
                e, B.xraw[c % 3][:, :], B.d_x[128 * c:128 * (c + 1), :]),
                "sxc%d" % (c % 3)))

        B.wait("vector", "sx", nload)
        B.em("vector", lambda e: e.tensor_scalar(
            B.b1[:, :], B.b1[:, :], 0.125, None, op0=ALU.mult))
        B.part_zero = B.tick(
            "vector", lambda e: e.memset(B.part[:, 0:D], 0.0), "cV")
        B.wait("tensor", "sx", nload)
        tb = B.t_mm(B.P6[:, 0:1], B.onesr[0:1, :], B.lrb11[0:1, :], True, True,
                    tick=True)
        B.wait("vector", "cT", tb)
        B.tick("vector", lambda e: e.tensor_copy(B.lrbcol[:, :], B.P6[:, 0:1]),
               "cV")

        self.prep(0)
        self.prep(1)
        self.fwd(0)

    def zwin(self, s):
        B = self
        css, csn = s % 2, (s + 1) % 2
        new_ch = s + 1 <= NCH - 1
        B.wait("vector", "sbi", 32 * (s + 1))
        B.wait("tensor", "cV",
               self.prep_ct.get(s + 1 if new_ch else s, 0))
        if new_ch:
            first = True
            for t2 in range(2):
                for jj in range(KD):
                    B.t_mm(B.P0[:, 128 * t2:128 * (t2 + 1)],
                           B.W0T[:, MC * jj + 128 * t2:
                                 MC * jj + 128 * t2 + 128],
                           B.cT[:, 256 * jj + 128 * csn:
                                256 * jj + 128 * csn + 128],
                           first, t2 == 1 and jj == KD - 1,
                           tick=(t2 == 1 and jj == KD - 1))
                    first = False
        first = True
        if new_ch:
            for jj in range(KD):
                B.t_mm(B.P3[:, 0:128],
                       B.cT[:, 256 * jj + 128 * css:
                            256 * jj + 128 * css + 128],
                       B.cT[:, 256 * jj + 128 * csn:
                            256 * jj + 128 * csn + 128],
                       first, False)
                first = False
        for jj in range(KD):
            B.t_mm(B.P3[:, 128:256],
                   B.cT[:, 256 * jj + 128 * css:256 * jj + 128 * css + 128],
                   B.cT[:, 256 * jj + 128 * css:256 * jj + 128 * css + 128],
                   first, jj == KD - 1, tick=(jj == KD - 1))
            first = False
        tks = B.val("cT")
        B.wait("vector", "cT", tks)
        if new_ch:
            B.em("vector", lambda e: e.tensor_copy(
                B.part[:, 0:256], B.P0[:, 0:256]))
            B.em("vector", lambda e: e.tensor_copy(
                B.k2sb[:, 128:256], B.P3[:, 0:128]))
        self.zwin_ev = B.tick("vector", lambda e: e.tensor_copy(
            B.k2sb[:, 0:128], B.P3[:, 128:256]), "cV")
        self.fwd_z_tick[s + 1] = tks

    def prep(self, c):
        B = self
        sl = c % 3
        sl2 = c % 2
        xr = B.xraw[sl]
        B.wait("scalar", B.chunk_sx[c][0], B.chunk_sx[c][1])
        if c >= 3:
            B.wait("scalar", "sx2", 16 * (c - 2))
        B.wait("scalar", "cV", B.x2_tick)
        ssq = B.tick("scalar", lambda e: e.activation(
            B.x2sb[:, :], xr[:, :], AF.Square), "cS")
        B.square_tick[c] = ssq
        B.wait("vector", "cS", ssq)
        B.wait("vector", "cT", B.fwd_z_tick.get(c - 1, 0))
        B.em("vector", lambda e: e.tensor_reduce(
            B.stat[:, 2:3], xr[:, :], axis=mybir.AxisListType.X, op=ALU.add))
        B.em("vector", lambda e: e.tensor_reduce(
            B.stat[:, 3:4], B.x2sb[:, :], axis=mybir.AxisListType.X, op=ALU.add))
        B.drain("vector")
        B.em("vector", lambda e: e.tensor_scalar(
            B.stat[:, 0:1], B.stat[:, 2:3], 1.0 / D, None, op0=ALU.mult))
        B.em("vector", lambda e: e.tensor_scalar(
            B.stat[:, 4:5], B.stat[:, 3:4], 1.0 / D, None, op0=ALU.mult))
        B.drain("vector")
        B.em("vector", lambda e: e.tensor_tensor(
            B.stat[:, 5:6], B.stat[:, 0:1], B.stat[:, 0:1], op=ALU.mult))
        B.drain("vector")
        vv = B.tick("vector", lambda e: e.tensor_tensor(
            B.stat[:, 6:7], B.stat[:, 4:5], B.stat[:, 5:6], op=ALU.subtract),
            "cV")
        B.wait("scalar", "cV", vv)
        ssd = B.tick("scalar", lambda e: e.activation(
            B.stat[:, 5:6], B.stat[:, 6:7], AF.Sqrt,
            bias=B.epscol[:, 0:1]), "cS")
        B.wait("vector", "cS", ssd)
        B.em("vector", lambda e: e.reciprocal(B.stat[:, 1:2], B.stat[:, 5:6]))
        B.drain("vector")
        vc = B.tick("vector", lambda e, sl2=sl2: e.tensor_scalar(
            B.ctok[sl2][:, :], xr[:, :], B.stat[:, 0:1], B.stat[:, 1:2],
            op0=ALU.subtract, op1=ALU.mult), "cV")

        B.wait("tensor", "cV", vc)
        evac_t = []
        for jj in range(KD):
            if jj >= 2:
                B.wait("tensor", "cV", evac_t[jj - 2])
            pb = (B.P4 if jj % 2 == 0 else B.P5)[:, 0:128]
            tt = B.t_tr(pb, B.ctok[sl2][:, 128 * jj:128 * (jj + 1)], f32r=False)
            B.wait("vector", "cT", tt)
            ev = B.tick("vector", lambda e, pb=pb, jj=jj, sl2=sl2: e.tensor_copy(
                B.cT[:, 256 * jj + 128 * sl2:256 * jj + 128 * sl2 + 128],
                pb), "cV")
            evac_t.append(ev)
        self.prep_ct[c] = evac_t[-1]

        B.wait("tensor", "cV", evac_t[-1])
        for jj in range(KD):
            B.t_mm(B.P6[:, 0:2],
                   B.cT[:, 256 * jj + 128 * sl2:256 * jj + 128 * sl2 + 128],
                   B.lrfgT[:, 2 * jj:2 * (jj + 1)], jj == 0, jj == KD - 1,
                   tick=(jj == KD - 1))
        tg = B.val("cT")
        B.wait("scalar", "cT", tg)
        B.em("scalar", lambda e: e.activation(
            B.gred[:, 0:1], B.P6[:, 0:1], AF.Sigmoid, bias=B.lrbcol[:, 0:1]))
        sg2 = B.tick("scalar", lambda e: e.copy(B.gred[:, 1:2], B.P6[:, 1:2]),
                     "cS")
        B.wait("tensor", "cS", sg2)
        tr2 = B.t_mm(B.P6[0:1, 4:6], B.onesc[:, 0:1], B.gred[:, :], True, True,
                     tick=True)
        sc = B.scal[sl]
        B.wait("scalar", "cT", tr2)
        sf = B.tick("scalar", lambda e, sc=sc: e.activation(
            sc[0:1, 0:1], B.P6[0:1, 5:6], AF.Sigmoid,
            bias=B.fgb11[0:1, 0:1], scale=B.inv128[0:1, 0:1]), "cS")
        B.wait("vector", "cS", sf)
        B.em("vector", lambda e, sc=sc: e.tensor_scalar(
            sc[0:1, 1:2], B.P6[0:1, 4:5], LR_MEM / 128.0 * GFAC, None,
            op0=ALU.mult))
        B.drain("vector")
        B.em("vector", lambda e, sc=sc: e.tensor_scalar(
            sc[0:1, 2:3], sc[0:1, 1:2], 0.125, None, op0=ALU.mult))
        B.drain("vector")
        vk = B.tick("vector", lambda e, sc=sc: e.tensor_scalar(
            B.k8r[0:1, :], B.onesr[0:1, :], sc[0:1, 2:3], None, op0=ALU.mult),
            "cV")
        B.wait("tensor", "cV", vk)
        B.t_mm(B.P6[:, 8:10], B.onesr[0:1, :], sc[0:1, 0:2], True, True)
        tbc = B.t_mm(B.P7[:, 0:128], B.onesr[0:1, :], B.k8r[0:1, :], True, True,
                     tick=True)
        B.wait("vector", "cT", tbc)
        B.em("vector", lambda e, sl=sl: e.tensor_copy(
            B.fk[sl][:, 0:2], B.P6[:, 8:10]))
        B.tick("vector", lambda e, sl=sl: e.tensor_copy(
            B.k8m[sl][:, :], B.P7[:, 0:128]), "cV")

    def fwd(self, s):
        B = self
        halves = []
        if s >= 1:
            halves.append((0, (s - 1) % 2))
        if s <= NCH - 1:
            halves.append((1, s % 2))
        slot = s % 2

        B.wait("tensor", "cV", B.val("cV"))
        if len(halves) == 2:
            lo, hi = 0, 256
        else:
            lo = 128 * halves[0][1]
            hi = lo + 128
        if s == 0:
            first = True
            for jj in range(KD):
                for t2 in range(2):
                    last = (jj == KD - 1 and t2 == 1)
                    csl = halves[0][1]
                    off = 256 * jj + 128 * csl
                    B.t_mm(B.P0[:, 256 * t2 + 128 * csl:
                                256 * t2 + 128 * csl + 128],
                           B.W0T[:, MC * jj + 128 * t2:
                                 MC * jj + 128 * t2 + 128],
                           B.cT[:, off:off + 128],
                           first, last, tick=last)
                    first = False
            tz = B.val("cT")
            B.fwd_z_tick[s] = tz
            B.wait("scalar", "cT", tz)
            B.wait("vector", "cT", tz)
            B.drain("vector")
            for t2 in range(2):
                def sil(e, t2=t2, slot=slot, lo=lo, hi=hi):
                    return e.activation(
                        B.H2[slot][:, 256 * t2 + lo:256 * t2 + hi],
                        B.P0[:, 256 * t2 + lo:256 * t2 + hi], AF.Silu,
                        bias=B.b0[:, t2:t2 + 1])
                if t2 == 1:
                    sh = B.tick("scalar", sil, "cS")
                else:
                    B.em("scalar", sil)

                def zc(e, t2=t2, slot=slot, lo=lo, hi=hi):
                    return e.tensor_scalar(
                        B.zsb[slot][:, 256 * t2 + lo:256 * t2 + hi],
                        B.P0[:, 256 * t2 + lo:256 * t2 + hi],
                        B.b0[:, t2:t2 + 1], None, op0=ALU.add)
                if t2 == 1:
                    shv = B.tick("vector", zc, "cV")
                else:
                    B.em("vector", zc)
        else:
            fprev = B.fk[(s - 1) % 3][:, 0:1]
            pslot = (s - 1) % 2
            ncorr = 2 * len(halves)
            ic = 0
            for h, csl in halves:
                k2ap = (B.k2sb[:, 0:128] if h == 0
                        else B.k2sb[:, 128:256])
                for t2 in range(2):
                    B.t_mm(B.P0[:, 256 * csl + 128 * t2:
                                256 * csl + 128 * t2 + 128],
                           B.dzt[:, 128 * t2:128 * (t2 + 1)],
                           k2ap, ic == 0, ic == ncorr - 1,
                           tick=(ic == ncorr - 1))
                    ic += 1
            tz = B.val("cT")
            B.wait("vector", "cT", tz)
            B.drain("vector")
            cticks = []
            for h, csl in halves:
                for t2 in range(2):
                    if h == 0:
                        zsrc = B.zsb[pslot][:, 256 * t2 + 128 * csl:
                                            256 * t2 + 128 * csl + 128]
                    else:
                        zsrc = B.part[:, 128 * t2:128 * (t2 + 1)]
                    dst = B.zsb[slot][:, 256 * t2 + 128 * csl:
                                      256 * t2 + 128 * csl + 128]
                    B.em("vector", lambda e, dst=dst, zsrc=zsrc,
                         fprev=fprev, csl=csl, t2=t2:
                         e.scalar_tensor_tensor(
                             dst, zsrc, fprev,
                             B.P0[:, 256 * csl + 128 * t2:
                                  256 * csl + 128 * t2 + 128],
                             op0=ALU.mult, op1=ALU.subtract))
                    B.drain("vector")
                    ct = B.tick("vector", lambda e, dst=dst, t2=t2:
                                e.tensor_scalar(
                                    dst, dst, B.b0[:, t2:t2 + 1], None,
                                    op0=ALU.add), "cV")
                    B.drain("vector")
                    cticks.append(ct)
            shv = cticks[-1]
            B.wait("scalar", "cV", shv)
            for t2 in range(2):
                def sil(e, t2=t2, slot=slot, lo=lo, hi=hi):
                    return e.activation(
                        B.H2[slot][:, 256 * t2 + lo:256 * t2 + hi],
                        B.zsb[slot][:, 256 * t2 + lo:256 * t2 + hi],
                        AF.Silu)
                if t2 == 1:
                    sh = B.tick("scalar", sil, "cS")
                else:
                    B.em("scalar", sil)

        B.wait("tensor", "cS", sh)
        B.wait("tensor", "cV", shv)
        if s >= 1:
            B.wait("vector", "sbi", 32 * s)
        pv = []
        for hi, (h, csl) in enumerate(halves):
            for ch in range(4):
                pb = B.P1 if (hi * 4 + ch) % 2 == 0 else B.P2
                if len(pv) >= 2:
                    B.wait("tensor", "cV", pv[-2])
                for t2 in range(2):
                    B.t_mm(pb[:, :],
                           B.H2[slot][:, 256 * t2 + 128 * csl:
                                      256 * t2 + 128 * csl + 128],
                           B.W1T[:, D * t2 + 512 * ch:D * t2 + 512 * (ch + 1)],
                           t2 == 0, False)
                if SKIP_BIASMM:
                    tpm = B.t_mm(pb[:, :],
                                 B.H2[slot][:, 128 * h:128 * h + 128],
                                 B.W1T[:, 512 * ch:512 * (ch + 1)],
                                 False, True, tick=True)
                else:
                    tpm = B.tick("tensor", lambda e, pb=pb, ch=ch: e.matmul(
                        pb[:, :], B.onesrr[0:1, :],
                        B.b1[0:1, 512 * ch:512 * (ch + 1)],
                        start=False, stop=True), "cT")
                B.wait("vector", "cT", tpm)
                ev = B.tick("vector",
                            lambda e, pb=pb, h=h, ch=ch: e.tensor_copy(
                                B.part[:, D * h + 512 * ch:
                                       D * h + 512 * (ch + 1)], pb[:, :]),
                            "cV")
                pv.append(ev)
        B.wait("scalar", "cV", pv[-1])
        if s == 0:
            B.wait("scalar", "cV", B.part_zero)
        if s >= 2:
            B.wait("scalar", "cc", s - 1)
        for h in range(2):
            B.dma16("scalar", lambda e, h=h, slot=slot: e.dma_start(
                out=B.bin[slot][128 * h:128 * (h + 1), :],
                in_=B.part[:, D * h:D * (h + 1)]), "sbi")
        B.wait("gpsimd", "sbi", B.val("sbi"))
        if s >= 2:
            B.wait("gpsimd", "sbo", 32 * (s - 1))
        B.em("gpsimd", lambda e, slot=slot, sem=B.sems["cc"]: e.collective_compute(
            "AllReduce", ALU.add, replica_groups=[list(range(B.n_cores))],
            ins=[B.bin[slot].ap().opt()],
            outs=[B.bout[slot].ap().opt()]).then_inc(sem, 1))
        B.inc("cc", 1)

        B.wait("sync", "cc", B.val("cc"))
        B.wait("sync", "cV", B.pred_free)
        for h in range(2):
            B.dma16("sync", lambda e, h=h, slot=slot: e.dma_start(
                out=B.pred[:, D * h:D * (h + 1)],
                in_=B.bout[slot][128 * h:128 * (h + 1), :]), "sbo")

    def step(self, s):
        B = self
        slot = s % 2
        csl = s % 3
        csl2 = s % 2
        kap_ap = B.fk[csl][:, 1:2]
        f_ap = B.fk[csl][:, 0:1]
        f11 = B.scal[csl][0:1, 0:1]

        if self.step_hook is not None:
            self.step_hook(s, 0)
        self.zwin(s)

        B.drain("vector")
        B.wait("vector", "sbo", 32 * (s + 1))
        B.wait("vector", "cT", B.dsb_free)
        vd = B.tick("vector", lambda e, csl2=csl2: e.tensor_tensor(
            B.dsb[:, :], B.pred[:, D:2 * D], B.ctok[csl2][:, :],
            op=ALU.subtract), "cV")
        if s >= 1:
            if s >= 2:
                B.wait("vector", "sx2", 16 * (s - 1))
            B.wait("vector", "cS", B.square_tick.get(s + 1, 0))
            vx2 = B.tick("vector", lambda e, s=s: e.tensor_tensor(
                B.x2sb[:, :], B.xraw[(s - 1) % 3][:, :],
                B.pred[:, 0:D], op=ALU.add), "cV")
            B.wait("gpsimd", "cV", vx2)
            B.dma16("gpsimd", lambda e, s=s: e.dma_start(
                out=B.d_x2[128 * (s - 1):128 * s, :], in_=B.x2sb[:, :]), "sx2")
            B.pred_free = vx2
            B.x2_tick = vx2
        else:
            B.pred_free = vd

        if s + 2 <= NCH - 1:
            if s >= 1:
                B.wait("sync", "sx2", 16 * s)
            B.chunk_sx[s + 2] = ("sxc%d" % ((s + 2) % 3),
                                 B.dma16("sync", lambda e, s=s: e.dma_start(
                                     out=B.xraw[(s + 2) % 3][:, :],
                                     in_=B.d_x[128 * (s + 2):128 * (s + 3), :]),
                                     "sxc%d" % ((s + 2) % 3)))

        B.wait("tensor", "cV", vd)
        evt = []
        for jj in range(KD):
            if jj >= 2:
                B.wait("tensor", "cV", evt[jj - 2])
            pb = (B.P4 if jj % 2 == 0 else B.P5)[:, 0:128]
            tt = B.t_tr(pb, B.dsb[:, 128 * jj:128 * (jj + 1)], f32r=True)
            B.wait("vector", "cT", tt)
            ev = B.tick("vector", lambda e, pb=pb, jj=jj: e.tensor_copy(
                B.dT[:, 128 * jj:128 * (jj + 1)], pb), "cV")
            evt.append(ev)

        if SUB <= 2:
            return
        B.tick("vector", lambda e, csl2=csl2: e.tensor_copy(
            B.ctokr[:, :], B.ctok[csl2][:, :]), "cV")

        B.wait("tensor", "cV", evt[-1])
        for t2 in range(2):
            pb = (B.P4 if t2 % 2 == 0 else B.P5)[:, 128:256]
            th = B.t_tr(pb, B.H2[slot][:, 256 * t2 + 128 * slot:
                                       256 * t2 + 128 * slot + 128],
                        f32r=False)
            B.wait("vector", "cT", th)
            B.tick("vector", lambda e, pb=pb, t2=t2, kap_ap=kap_ap:
                   e.tensor_scalar(B.hs[:, 128 * t2:128 * (t2 + 1)], pb,
                                   kap_ap, None, op0=ALU.mult), "cV")
        sh = B.val("cV")

        if SUB <= 3:
            return
        B.wait("tensor", "cV", evt[-1])
        first = True
        for jj in range(KD):
            for mc in range(2):
                last = (jj == KD - 1 and mc == 1)
                B.t_mm(B.P3[:, 128 * mc:128 * (mc + 1)],
                       B.W1n[:, MC * jj + 128 * mc:MC * jj + 128 * mc + 128],
                       B.dT[:, 128 * jj:128 * (jj + 1)],
                       first, last, tick=last)
                first = False
        tdh = B.val("cT")
        B.tdh_tick[s] = tdh

        if SUB <= 4:
            return
        B.wait("tensor", "cV", sh)
        gwv = []
        for t2 in range(2):
            for ch in range(4):
                pb = B.P1 if (t2 * 4 + ch) % 2 == 0 else B.P2
                if len(gwv) >= 2:
                    B.wait("tensor", "cV", gwv[-2])
                tm = B.t_mm(pb[:, :],
                            B.hs[:, 128 * t2:128 * (t2 + 1)],
                            B.dsb[:, 512 * ch:512 * (ch + 1)],
                            True, True, tick=True)
                B.wait("vector", "cT", tm)
                ev = B.tick(
                    "vector",
                    lambda e, pb=pb, t2=t2, ch=ch, f_ap=f_ap:
                    e.scalar_tensor_tensor(
                        B.W1T[:, D * t2 + 512 * ch:D * t2 + 512 * (ch + 1)],
                        B.W1T[:, D * t2 + 512 * ch:D * t2 + 512 * (ch + 1)],
                        f_ap, pb[:, :], op0=ALU.mult, op1=ALU.subtract), "cV")
                gwv.append(ev)

        if SUB <= 5:
            return
        if self.step_hook is not None:
            self.step_hook(s, 1)
        B.wait("scalar", "cT", tdh)
        for t2 in range(2):
            zh = B.zsb[slot][:, 256 * t2 + 128 * slot:
                             256 * t2 + 128 * slot + 128]
            B.tick("scalar", lambda e, t2=t2, zh=zh: e.activation(
                B.sg[:, 128 * t2:128 * (t2 + 1)], zh, AF.Sigmoid), "cS")
        ssg = B.val("cS")
        B.wait("vector", "cS", ssg)
        for t2 in range(2):
            zh = B.zsb[slot][:, 256 * t2 + 128 * slot:
                             256 * t2 + 128 * slot + 128]
            sgh = B.sg[:, 128 * t2:128 * (t2 + 1)]
            s1 = B.sp1[:, 128 * t2:128 * (t2 + 1)]
            s2 = B.sp2[:, 128 * t2:128 * (t2 + 1)]
            dzh = B.dzf[:, 128 * t2:128 * (t2 + 1)]
            B.em("vector", lambda e, zh=zh, sgh=sgh, s1=s1: e.tensor_tensor(
                s1, zh, sgh, op=ALU.mult))
            B.drain("vector")
            B.em("vector", lambda e, sgh=sgh, s1=s1, s2=s2: e.tensor_tensor(
                s2, s1, sgh, op=ALU.mult))
            B.drain("vector")
            B.em("vector", lambda e, s1=s1, s2=s2: e.tensor_tensor(
                s1, s1, s2, op=ALU.subtract))
            B.drain("vector")
            B.em("vector", lambda e, sgh=sgh, s1=s1: e.tensor_tensor(
                s1, sgh, s1, op=ALU.add))
            B.drain("vector")
            B.tick("vector", lambda e, t2=t2, s1=s1, dzh=dzh: e.tensor_tensor(
                dzh, B.P3[:, 128 * t2:128 * (t2 + 1)], s1, op=ALU.mult), "cV")
        vdz = B.val("cV")

        if SUB <= 6:
            return
        B.wait("tensor", "cV", vdz)
        for t2 in range(2):
            pb = (B.P4 if t2 % 2 == 0 else B.P5)[:, 256:384]
            tt = B.t_tr(pb, B.dzf[:, 128 * t2:128 * (t2 + 1)], f32r=True)
            B.wait("vector", "cT", tt)
            B.tick("vector", lambda e, pb=pb, t2=t2, kap_ap=kap_ap:
                   e.tensor_scalar(B.dzt[:, 128 * t2:128 * (t2 + 1)], pb,
                                   kap_ap, None, op0=ALU.mult), "cV")
        sdz = B.val("cV")

        B.wait("tensor", "cV", sdz)
        g0v = []
        for jj in range(KD):
            pb = B.P1 if jj % 2 == 0 else B.P2
            if len(g0v) >= 2:
                B.wait("tensor", "cV", g0v[-2])
            tm = B.t_mm(pb[:, 0:256],
                        B.ctokr[:, 128 * jj:128 * (jj + 1)],
                        B.dzt[:, :], True, True, tick=True)
            B.wait("vector", "cT", tm)
            ev = B.tick("vector", lambda e, pb=pb, jj=jj, f_ap=f_ap:
                        e.scalar_tensor_tensor(
                            B.W0T[:, MC * jj:MC * (jj + 1)],
                            B.W0T[:, MC * jj:MC * (jj + 1)],
                            f_ap, pb[:, 0:256],
                            op0=ALU.mult, op1=ALU.subtract), "cV")
            g0v.append(ev)
        w0done = g0v[-1]

        if SUB <= 7:
            return
        gb1v = []
        for ch in range(4):
            if len(gb1v) >= 1:
                B.wait("tensor", "cV", gb1v[-1])
            tm = B.t_mm(B.P7[:, 0:512], B.k8m[csl][:, :],
                        B.dsb[:, 512 * ch:512 * (ch + 1)],
                        True, True, tick=True)
            B.wait("vector", "cT", tm)
            ev = B.tick("vector", lambda e, ch=ch, f11=f11:
                        e.scalar_tensor_tensor(
                            B.b1[0:1, 512 * ch:512 * (ch + 1)],
                            B.b1[0:1, 512 * ch:512 * (ch + 1)],
                            f11, B.P7[0:1, 0:512],
                            op0=ALU.mult, op1=ALU.subtract), "cV")
            gb1v.append(ev)
        B.dsb_free = B.val("cT")

        B.drain("vector")
        for t2 in range(2):
            B.em("vector", lambda e, t2=t2: e.tensor_reduce(
                B.gb0c[:, t2:t2 + 1], B.dzf[:, 128 * t2:128 * (t2 + 1)],
                axis=mybir.AxisListType.X, op=ALU.add))
        B.drain("vector")
        B.em("vector", lambda e, kap_ap=kap_ap: e.tensor_scalar(
            B.gb0c[:, 0:2], B.gb0c[:, 0:2], kap_ap, None, op0=ALU.mult))
        B.drain("vector")
        B.tick("vector", lambda e, f_ap=f_ap: e.scalar_tensor_tensor(
            B.b0[:, 0:2], B.b0[:, 0:2], f_ap, B.gb0c[:, 0:2],
            op0=ALU.mult, op1=ALU.subtract), "cV")
        bdone = B.val("cV")

        if SUB <= 8:
            return
        if self.step_hook is not None:
            self.step_hook(s, 2)
        B.wait("tensor", "cV", bdone)
        self.fwd(s + 1)
        if s + 2 <= NCH - 1:
            self.prep(s + 2)

    def epilogue(self):
        B = self
        s = NCH
        slot = s % 2
        B.wait("vector", "sbo", 32 * (s + 1))
        B.wait("vector", "sx2", 16 * (s - 1))
        vx2 = B.tick("vector", lambda e: e.tensor_tensor(
            B.x2sb[:, :], B.xraw[(s - 1) % 3][:, :],
            B.pred[:, 0:D], op=ALU.add), "cV")
        B.wait("gpsimd", "cV", vx2)
        B.dma16("gpsimd", lambda e: e.dma_start(
            out=B.d_x2[128 * (NCH - 1):128 * NCH, :], in_=B.x2sb[:, :]), "sx2")
        B.wait("gpsimd", "sx2", B.val("sx2"))


def host_inputs(x, W0, b0, W1, b1, lr_w, lr_b, fg_w, fg_b, core):
    r = core
    msl = slice(MC * r, MC * (r + 1))
    return {
        "ttt_x": np.ascontiguousarray(x.reshape(NCH * CK, D)),
        "ttt_W0T": np.ascontiguousarray(W0[msl, :].T),
        "ttt_b0": np.ascontiguousarray(b0[msl].reshape(1, MC)),
        "ttt_W1T": np.ascontiguousarray(W1[:, msl].T),
        "ttt_W1n": np.ascontiguousarray(W1[:, msl]),
        "ttt_b1": np.ascontiguousarray(b1.reshape(1, D)),
        "ttt_lrfgT": np.ascontiguousarray(
            np.stack([lr_w.reshape(D), fg_w.reshape(D)], axis=1)),
        "ttt_lrb": np.asarray(lr_b, np.float32).reshape(1, 1),
        "ttt_fgb": np.asarray(fg_b, np.float32).reshape(1, 1),
        **common_inputs(),
    }


import numpy as np
import bass_rust
import concourse.mybir as mybir

F32 = mybir.dt.float32
BF16 = mybir.dt.bfloat16
AF = mybir.ActivationFunctionType
ALU = mybir.AluOpType

D = 2048
FF = 8192
TB = 1024
NT = 8
KD = 16
KF = 64
NW = 1.0 / (FF * D)


class PWStream:
    NSLOT = 4

    def __init__(self, core):
        self.B = core
        self.nc = core.nc

    def declare(self):
        nc = self.nc
        B = self.B
        dt = nc.dram_tensor
        self.d_w1T = dt("ffn_w1T", [D, FF], F32, kind="ExternalInput").ap()
        self.d_w2T = dt("ffn_w2T", [FF, D], F32, kind="ExternalInput").ap()
        self.pwbuf = B.sbuf("pw_buf", [128, 1024])
        self.pacc = B.sbuf("pw_acc", [128, 4])
        for i in range(2):
            B.new_sem("pw%d" % i)
        self.chunks = []
        for j in range(D // 128):
            for k in range(FF // 256):
                self.chunks.append(
                    (self.d_w1T[128 * j:128 * (j + 1), 256 * k:256 * (k + 1)],
                     0))
        for j in range(FF // 128):
            for k in range(D // 256):
                self.chunks.append(
                    (self.d_w2T[128 * j:128 * (j + 1), 256 * k:256 * (k + 1)],
                     1))
        self.npairs = len(self.chunks) // 2
        self.ci = 0
        self.ri = 0
        self.red_tick = {}
        self._init_done = False

    def _trigger_pair(self):
        B = self.B
        p = self.ci
        q = p % 2
        if p >= 2:
            B.wait("scalar", "cV", self.red_tick.pop(p - 2))
        for h in range(2):
            src, _ = self.chunks[2 * p + h]
            B.dma16("scalar", lambda e, src=src, q=q, h=h: e.dma_start(
                out=self.pwbuf[:, 512 * q + 256 * h:512 * q + 256 * (h + 1)],
                in_=src), "pw%d" % q)
        self.ci += 1

    def _reduce_pair(self):
        B = self.B
        p = self.ri
        _, col = self.chunks[2 * p]
        q = p % 2
        B.wait("vector", "pw%d" % q, 32 * (p // 2 + 1))
        B.drain("vector")
        B.em("vector", lambda e, q=q: e.tensor_reduce(
            self.pacc[:, 2:3], self.pwbuf[:, 512 * q:512 * (q + 1)],
            axis=mybir.AxisListType.X, op=ALU.add,
            apply_absolute_value=True))
        B.drain("vector")
        t = B.tick("vector", lambda e, col=col: e.tensor_tensor(
            self.pacc[:, col:col + 1], self.pacc[:, col:col + 1],
            self.pacc[:, 2:3], op=ALU.add), "cV")
        self.red_tick[p] = t
        self.ri += 1

    def hook(self, quota):
        B = self.B
        if not self._init_done:
            self._init_done = True
            B.tick("vector", lambda e: e.memset(self.pacc[:, :], 0.0), "cV")
            B.drain("vector")
            for _ in range(min(2, self.npairs)):
                self._trigger_pair()
        n = 0
        while n < quota and self.ri < self.npairs:
            self._reduce_pair()
            if self.ci < self.npairs:
                self._trigger_pair()
            n += 1

    def finale(self):
        while self.ri < self.npairs:
            self._reduce_pair()


class FFNBuilder:
    def __init__(self, core_builder):
        self.B = core_builder
        self.nc = core_builder.nc

    def declare(self, pw=None):
        nc = self.nc
        B = self.B
        dt = nc.dram_tensor
        if pw is not None:
            self.d_w1T = pw.d_w1T
            self.d_w2T = pw.d_w2T
        else:
            self.d_w1T = dt("ffn_w1T", [D, FF], F32,
                            kind="ExternalInput").ap()
            self.d_w2T = dt("ffn_w2T", [FF, D], F32,
                            kind="ExternalInput").ap()
        self.d_b1 = dt("ffn_b1", [1, FF], F32, kind="ExternalInput").ap()
        self.d_gamma = dt("ffn_gamma", [1, FF], F32, kind="ExternalInput").ap()
        self.d_beta = dt("ffn_beta", [1, FF], F32, kind="ExternalInput").ap()
        self.d_b2 = dt("ffn_b2", [1, D], F32, kind="ExternalInput").ap()
        self.d_xrows = dt("ffn_xrows", [128, NT], mybir.dt.int32,
                          kind="ExternalInput").ap()
        self.d_xrows16 = dt("ffn_xrows16", [128, NT], mybir.dt.int32,
                            kind="ExternalInput").ap()
        self.d_yT = dt("ffn_yT", [D, TB], F32, kind="ExternalOutput").ap()
        self.d_h = dt("ffn_hscr", [TB, FF], F32)

        S = B.sbuf
        self.hqT = S("f_hqT", [128, KF * TB], BF16)
        self.wf1 = [S(f"f_wf1_{i}", [128, KD * 128]) for i in range(2)]
        self.w1q = [S(f"f_w1q_{i}", [128, KD * 128], BF16) for i in range(2)]
        self.qa = S("f_qa", [128, KD * 128], BF16)
        self.s1 = S("f_s1", [128, D])
        self.s2 = S("f_s2", [128, D])
        self.qbf = S("f_qbf", [128, D], BF16)
        self.bnb = S("f_bnb", [128, NT * KF * 6])
        _qaf = self.qa.bitcast(F32)
        self.gbc = _qaf[:, 0:512]
        self.bbc = _qaf[:, 512:1024]
        self.htile = [S(f"f_ht{i}", [128, 128]) for i in range(2)]
        self.hgl = [S(f"f_hg{i}", [128, 128]) for i in range(2)]
        self.b1bc = S("f_b1bc", [128, 128])
        self.fst = S("f_stat", [128, 16])
        self.bx = S("f_bx", [128, NT])
        self.mv = S("f_mv", [128, NT * 2])
        self.mu = S("f_mu", [128, NT])
        self.rho = S("f_rho", [128, NT])
        self.mx2 = S("f_mx2", [128, NT])
        self.s2q = S("f_s2q", [128, NT])
        self.b2neg = S("f_b2neg", [128, NT])
        self.thc = S("f_thc", [128, 4])
        self.swc = S("f_swc", [128, 2])
        self.scl = S("f_scl", [1, 8])
        self.b2row = S("f_b2row", [1, TB])
        self.row8 = S("f_row8", [8, 128])
        self.b2c = S("f_b2c", [128, KD])
        self.bet2b = self.s2[:, TB:2 * TB]
        self.ytT = self.s2[:, 0:TB]
        self.x2c = self.s1[:, 0:NT * 128]
        self.rs1 = S("f_rs1", [1, 512])
        self.rs2 = S("f_rs2", [1, 512])
        self.xrows = S("f_xrows_sb", [128, NT], mybir.dt.int32)
        self.xrows16 = S("f_xrows16_sb", [128, NT], mybir.dt.int32)
        self.xqT = self.hqT

        for s in ("fw0", "fw1", "fh0", "fh1", "fo", "fo0", "fo1", "fx",
                  "fy", "fr", "fxr", "fg", "fxB", "foB"):
            B.new_sem(s)

    def emit(self, barrier=None, pw=None):
        B = self.B
        if barrier:
            for eng in ("sync", "gpsimd", "vector", "scalar", "tensor"):
                for sem_name, v in barrier.items():
                    B.wait(eng, sem_name, v)
        B.dma16("sync", lambda e: e.dma_start(
            out=self.xrows[:, :], in_=self.d_xrows), "fxr")
        B.dma16("sync", lambda e: e.dma_start(
            out=self.xrows16[:, :], in_=self.d_xrows16), "fxr")
        self._xrows_fr = B.val("fxr")
        if pw is not None:
            self.phaseW_finale(pw)
        else:
            self.phaseW()
        self.phaseX()
        self.phaseF1()
        self.phaseF3()
        self.phaseF4()
        self.phaseF5()

    def phaseW(self):
        B = self.B
        FB = self
        ms = B.tick("vector", lambda e: e.memset(FB.fst[:, :], 0.0), "cV")
        B.wait("vector", "cV", ms)
        self.rd_tick = {0: 0, 1: 0}
        nld = {0: 0, 1: 0}

        def stream(src_ap, acc_col, i, total_i):
            sl = i % 2
            sem = "fw%d" % sl
            if self.rd_tick[sl]:
                B.wait("sync", "cV", self.rd_tick[sl])
            nld[sl] += 1
            B.dma16("sync", lambda e, src_ap=src_ap, sl=sl: e.dma_start(
                out=FB.wf1[sl][:, :], in_=src_ap), sem)
            B.wait("vector", sem, 16 * nld[sl])
            B.em("vector", lambda e, sl=sl: e.tensor_reduce(
                FB.fst[:, 2:3], FB.wf1[sl][:, :], axis=mybir.AxisListType.X,
                op=ALU.add, apply_absolute_value=True))
            B.drain("vector")
            t = B.tick("vector", lambda e, acc_col=acc_col: e.tensor_tensor(
                FB.fst[:, acc_col:acc_col + 1], FB.fst[:, acc_col:acc_col + 1],
                FB.fst[:, 2:3], op=ALU.add), "cV")
            B.drain("vector")
            self.rd_tick[sl] = t

        i = 0
        for dd in range(KD):
            for cc in range(4):
                stream(self.d_w1T[128 * dd:128 * (dd + 1),
                                  2048 * cc:2048 * (cc + 1)], 0, i, 128)
                i += 1
        for fj in range(KF):
            stream(self.d_w2T[128 * fj:128 * (fj + 1), :], 1, i, 128)
            i += 1
        self.w_nld = dict(nld)
        self._w_tail(FB.fst[:, 0:2])

    def phaseW_finale(self, pw):
        pw.finale()
        self.rd_tick = {0: 0, 1: 0}
        self.w_nld = {0: 0, 1: 0}
        self._w_tail(pw.pacc[:, 0:2])

    def _w_tail(self, acc_ap):
        B = self.B
        FB = self
        B.wait("tensor", "cV", B.val("cV"))
        tm = B.t_mm(B.P6[0:1, 0:2], B.onesc[:, 0:1], acc_ap,
                    True, True, tick=True)
        B.wait("vector", "cT", tm)
        B.em("vector", lambda e: e.tensor_scalar(
            FB.scl[0:1, 0:2], B.P6[0:1, 0:2], NW, 1e-5,
            op0=ALU.mult, op1=ALU.max))
        B.drain("vector")
        B.em("vector", lambda e: e.tensor_scalar(
            FB.scl[0:1, 2:4], FB.scl[0:1, 0:2], 0.5, None, op0=ALU.mult))
        B.drain("vector")
        t = B.tick("vector", lambda e: e.tensor_scalar(
            FB.scl[0:1, 4:6], FB.scl[0:1, 2:4], -1.0, None, op0=ALU.mult),
            "cV")
        B.wait("tensor", "cV", t)
        B.t_mm(B.P6[:, 4:8], B.onesr[0:1, :], FB.scl[0:1, 2:6], True, True)
        tb = B.t_mm(B.P6[:, 8:10], B.onesr[0:1, :], FB.scl[0:1, 0:2],
                    True, True, tick=True)
        B.wait("vector", "cT", tb)
        B.em("vector", lambda e: e.tensor_copy(FB.thc[:, 0:4], B.P6[:, 4:8]))
        self.w_done = B.tick("vector", lambda e: e.tensor_copy(
            FB.swc[:, 0:2], B.P6[:, 8:10]), "cV")

    def phaseX(self):
        B = self.B
        FB = self
        self.xev = [0] * KD
        nfx = B.val("fx")
        B.wait("gpsimd", "fxr", self._xrows_fr)
        for tt in range(NT):
            B.wait("gpsimd", "cV", getattr(self, "_x_s1r", 0))
            nfx = B.dma16("gpsimd", lambda e, tt=tt: e.indirect_dma_start(
                out=FB.s1[:, :], out_offset=None,
                in_=B.d_x2,
                in_offset=bass_rust.IndirectOffsetOnAxis(
                    ap=FB.xrows[:, tt:tt + 1], axis=0)), "fg")
            B.wait("vector", "fg", nfx)
            B.drain("vector")
            for q in range(4):
                B.em("vector", lambda e, q=q, tt=tt: e.bn_stats(
                    FB.bnb[:, 24 * tt + 6 * q:24 * tt + 6 * (q + 1)],
                    FB.s1[:, 512 * q:512 * (q + 1)]))
            B.drain("vector")
            vb = B.tick("vector", lambda e, tt=tt: e.bn_aggr(
                FB.mv[:, 2 * tt:2 * (tt + 1)],
                FB.bnb[:, 24 * tt:24 * tt + 24]), "cV")
            B.wait("scalar", "cV", vb)
            sq = B.tick("scalar", lambda e, tt=tt: e.activation(
                FB.rho[:, tt:tt + 1], FB.mv[:, 2 * tt + 1:2 * tt + 2],
                AF.Sqrt, bias=B.epscol[:, 0:1]), "cS")
            B.wait("vector", "cS", sq)
            B.em("vector", lambda e, tt=tt: e.reciprocal(
                FB.rho[:, tt:tt + 1], FB.rho[:, tt:tt + 1]))
            B.drain("vector")
            B.em("vector", lambda e, tt=tt: e.tensor_scalar(
                FB.s2[:, :], FB.s1[:, :], FB.mv[:, 2 * tt:2 * tt + 1],
                FB.rho[:, tt:tt + 1], op0=ALU.subtract, op1=ALU.mult))
            B.drain("vector")
            B.em("vector", lambda e: e.tensor_reduce(
                FB.fst[:, 4:5], FB.s2[:, :], axis=mybir.AxisListType.X,
                op=ALU.max, apply_absolute_value=True))
            B.drain("vector")
            B.em("vector", lambda e: e.tensor_scalar(
                FB.fst[:, 5:6], FB.fst[:, 4:5], 1e-5, None, op0=ALU.max))
            B.drain("vector")
            if tt == 0:
                B.wait("vector", "cV", self.w_done)
            B.em("vector", lambda e, tt=tt: e.scalar_tensor_tensor(
                FB.bx[:, tt:tt + 1], FB.fst[:, 5:6], -1.0 / 127.0,
                FB.swc[:, 0:1], op0=ALU.mult, op1=ALU.mult))
            B.em("vector", lambda e: e.reciprocal(
                FB.fst[:, 6:7], FB.fst[:, 5:6]))
            B.drain("vector")
            B.em("vector", lambda e: e.tensor_scalar(
                FB.fst[:, 7:8], FB.fst[:, 6:7], 127.0, None, op0=ALU.mult))
            B.drain("vector")
            B.em("vector", lambda e: e.tensor_scalar(
                FB.s1[:, :], FB.s2[:, :], FB.fst[:, 7:8], None,
                op0=ALU.mult))
            B.drain("vector")
            B.em("vector", lambda e: e.tensor_copy(
                FB.s2[:, :].bitcast(mybir.dt.int32), FB.s1[:, :]))
            B.drain("vector")
            vq = B.tick("vector", lambda e: e.tensor_copy(
                FB.qbf[:, :], FB.s2[:, :].bitcast(mybir.dt.int32)), "cV")
            self._x_s1r = vq
            B.wait("tensor", "cV", vq)
            for dd in range(KD):
                pb = (B.P4 if dd % 2 == 0 else B.P5)[:, 0:64].bitcast(BF16)
                if self.xev[dd - 2] if dd >= 2 else 0:
                    B.wait("tensor", "cV", self.xev[dd - 2])
                tk = B.tick("tensor", lambda e, pb=pb, dd=dd: e.transpose(
                    pb, FB.qbf[:, 128 * dd:128 * (dd + 1)],
                    B.identb[:, :]), "cT")
                B.wait("vector", "cT", tk)
                self.xev[dd] = B.tick(
                    "vector", lambda e, pb=pb, dd=dd, tt=tt: e.tensor_copy(
                        FB.xqT[:, TB * dd + 128 * tt:TB * dd + 128 * (tt + 1)],
                        pb), "cV")
        self.x_done = B.val("cV")

    def phaseF1(self):
        B = self.B
        FB = self
        q_tick = {0: 0, 1: 0}
        mm_tick = {0: 0, 1: 0}
        self.f1_nld = {0: self.w_nld[0], 1: self.w_nld[1]}
        h_ev = [0, 0]
        pv_ev = [0, 0]
        fo_at = [0, 0]
        hst = 0
        self._rs1_free = 0
        for fc in range(KF):
            sl = fc % 2
            sem = "fw%d" % sl
            if q_tick[sl]:
                B.wait("sync", "cV", q_tick[sl])
            elif self.rd_tick[sl]:
                B.wait("sync", "cV", self.rd_tick[sl])
            self.f1_nld[sl] += 1
            B.dma16("sync", lambda e, fc=fc, sl=sl: e.dma_start(
                out=FB.wf1[sl][:, :].rearrange("p (j m) -> p j m", m=128),
                in_=FB.d_w1T.rearrange("(j p) m -> p j m", p=128)
                [:, :, 128 * fc:128 * (fc + 1)]), sem)
            B.wait("vector", sem, 16 * self.f1_nld[sl])
            if fc == 0:
                B.wait("vector", "cV", self.x_done)
            if mm_tick[sl]:
                B.wait("vector", "cT", mm_tick[sl])
            B.drain("vector")
            B.em("vector", lambda e, sl=sl: e.tensor_scalar(
                FB.qa[:, :], FB.wf1[sl][:, :], FB.thc[:, 0:1], None,
                op0=ALU.is_gt))
            B.drain("vector")
            qt = B.tick("vector", lambda e, sl=sl: e.scalar_tensor_tensor(
                FB.w1q[sl][:, :], FB.wf1[sl][:, :], FB.thc[:, 2:3],
                FB.qa[:, :], op0=ALU.is_lt, op1=ALU.subtract), "cV")
            q_tick[sl] = qt
            B.wait("sync", "cT", self._rs1_free)
            nfr = B.dma16("sync", lambda e, fc=fc: e.dma_start(
                out=FB.rs1[0:1, 0:128],
                in_=FB.d_b1[0:1, 128 * fc:128 * (fc + 1)]), "fr")
            B.wait("tensor", "cV", qt)
            B.wait("tensor", "fr", nfr)
            bb = B.t_mm(B.P6[:, 0:128], B.onesr[0:1, :],
                        FB.rs1[0:1, 0:128], True, True, tick=True)
            self._rs1_free = bb
            B.wait("vector", "cT", bb)
            bbv = B.tick("vector", lambda e: e.tensor_copy(
                FB.b1bc[:, :], B.P6[:, 0:128]), "cV")
            B.wait("tensor", "cV", bbv)
            for tt in range(NT):
                pb = B.P1 if tt % 2 == 0 else B.P2
                if hst >= 2:
                    B.wait("tensor", "cV", pv_ev[hst % 2])
                for dd in range(KD):
                    B.t_mm(pb[:, 0:128],
                           FB.xqT[:, TB * dd + 128 * tt:
                                  TB * dd + 128 * (tt + 1)],
                           FB.w1q[sl][:, 128 * dd:128 * (dd + 1)],
                           dd == 0, dd == KD - 1, tick=(dd == KD - 1))
                mt = B.val("cT")
                if tt == NT - 1:
                    mm_tick[sl] = mt
                hb = hst % 2
                B.wait("vector", "cT", mt)
                if hst >= 2:
                    B.wait("vector", "cS", h_ev[hb])
                B.drain("vector")
                ve = B.tick("vector", lambda e, pb=pb, hb=hb, tt=tt:
                            e.scalar_tensor_tensor(
                                FB.htile[hb][:, :], pb[:, 0:128],
                                FB.bx[:, tt:tt + 1], FB.b1bc[:, :],
                                op0=ALU.mult, op1=ALU.add), "cV")
                pv_ev[hb] = ve
                B.wait("scalar", "cV", ve)
                if hst >= 2:
                    B.wait("scalar", "fo%d" % hb, fo_at[hb])
                sg = B.tick("scalar", lambda e, hb=hb: e.activation(
                    FB.hgl[hb][:, :], FB.htile[hb][:, :], AF.Gelu), "cS")
                h_ev[hb] = sg
                B.wait("vector", "cS", sg)
                B.drain("vector")
                B.em("vector", lambda e, hb=hb, tt=tt, fc=fc: e.bn_stats(
                    FB.bnb[:, (tt * KF + fc) * 6:(tt * KF + fc) * 6 + 6],
                    FB.hgl[hb][:, :]))
                B.wait("scalar", "cS", sg)
                fo_at[hb] = B.dma16(
                    "scalar", lambda e, hb=hb, tt=tt, fc=fc: e.dma_start(
                        out=FB.d_h[128 * tt:128 * (tt + 1),
                                   128 * fc:128 * (fc + 1)],
                        in_=FB.hgl[hb][:, :]), "fo%d" % hb)
                hst += 1
        self.f1_fo = (B.val("fo0"), B.val("fo1"))
        self.f1_q = dict(q_tick)
        self.f1_mm = dict(mm_tick)

    def phaseF3(self):
        B = self.B
        FB = self
        B.drain("vector")
        for tt in range(NT):
            vb = B.tick("vector", lambda e, tt=tt: e.bn_aggr(
                FB.mv[:, 2 * tt:2 * (tt + 1)],
                FB.bnb[:, tt * KF * 6:(tt + 1) * KF * 6]), "cV")
            B.drain("vector")
            B.em("vector", lambda e, tt=tt: e.tensor_copy(
                FB.mu[:, tt:tt + 1], FB.mv[:, 2 * tt:2 * tt + 1]))
            B.wait("scalar", "cV", B.val("cV"))
            sq = B.tick("scalar", lambda e, tt=tt: e.activation(
                FB.rho[:, tt:tt + 1], FB.mv[:, 2 * tt + 1:2 * tt + 2],
                AF.Sqrt, bias=B.epscol[:, 0:1]), "cS")
            B.wait("vector", "cS", sq)
            B.em("vector", lambda e, tt=tt: e.reciprocal(
                FB.rho[:, tt:tt + 1], FB.rho[:, tt:tt + 1]))
            B.drain("vector")
        mz = B.tick("vector", lambda e: e.memset(FB.mx2[:, :], 0.0), "cV")
        B.wait("vector", "cV", mz)
        B.wait("vector", "fo0", self.f1_fo[0])
        B.wait("vector", "fo1", self.f1_fo[1])
        self._f3_s1r = {}
        self._f3_fo = {}
        for gc in range(KF // 4):
            B.wait("sync", "cT", getattr(self, "_rs1_free", 0))
            nfr = B.dma16("sync", lambda e, gc=gc: e.dma_start(
                out=FB.rs1[0:1, :],
                in_=FB.d_gamma[0:1, 512 * gc:512 * (gc + 1)]), "fr")
            nfr = B.dma16("sync", lambda e, gc=gc: e.dma_start(
                out=FB.rs2[0:1, :],
                in_=FB.d_beta[0:1, 512 * gc:512 * (gc + 1)]), "fr")
            B.wait("tensor", "fr", nfr)
            gb = B.t_mm(B.P6[:, 0:512], B.onesr[0:1, :],
                        FB.rs1[0:1, :], True, True, tick=True)
            B.wait("vector", "cT", gb)
            B.tick("vector", lambda e: e.tensor_copy(
                FB.gbc[:, :], B.P6[:, 0:512]), "cV")
            bb = B.t_mm(B.P7[:, 0:512], B.onesr[0:1, :],
                        FB.rs2[0:1, :], True, True, tick=True)
            self._rs1_free = bb
            B.wait("vector", "cT", bb)
            B.tick("vector", lambda e: e.tensor_copy(
                FB.bbc[:, :], B.P7[:, 0:512]), "cV")
            B.drain("vector")
            for tt in range(NT):
                i = gc * NT + tt
                sl = i % 2
                osem = "fo" if sl == 0 else "foB"
                lsem = "fx" if sl == 0 else "fxB"
                s1s = FB.s1[:, 512 * sl:512 * (sl + 1)]
                s2s = FB.s2[:, 512 * sl:512 * (sl + 1)]
                B.wait("sync", "cV", self._f3_s1r.get(sl, 0))
                nfx = B.dma16("sync", lambda e, tt=tt, gc=gc, s1s=s1s:
                              e.dma_start(
                    out=s1s,
                    in_=FB.d_h[128 * tt:128 * (tt + 1),
                               512 * gc:512 * (gc + 1)]), lsem)
                B.wait("vector", lsem, nfx)
                B.wait("vector", osem, self._f3_fo.get(sl, 0))
                B.drain("vector")
                s1r = B.tick("vector", lambda e, tt=tt, s1s=s1s, s2s=s2s:
                             e.tensor_scalar(
                    s2s, s1s, FB.mu[:, tt:tt + 1],
                    FB.rho[:, tt:tt + 1], op0=ALU.subtract, op1=ALU.mult),
                    "cV")
                self._f3_s1r[sl] = s1r
                B.drain("vector")
                B.em("vector", lambda e, s2s=s2s: e.tensor_tensor(
                    s2s, s2s, FB.gbc[:, :], op=ALU.mult))
                B.drain("vector")
                B.em("vector", lambda e, s2s=s2s: e.tensor_tensor(
                    s2s, s2s, FB.bbc[:, :], op=ALU.add))
                B.drain("vector")
                B.em("vector", lambda e, s2s=s2s: e.tensor_reduce(
                    FB.fst[:, 8:9], s2s,
                    axis=mybir.AxisListType.X, op=ALU.max,
                    apply_absolute_value=True))
                B.drain("vector")
                B.em("vector", lambda e, tt=tt: e.tensor_tensor(
                    FB.mx2[:, tt:tt + 1], FB.mx2[:, tt:tt + 1],
                    FB.fst[:, 8:9], op=ALU.max))
                B.drain("vector")
                hv = B.tick("vector", lambda e: e.tensor_copy(
                    FB.fst[:, 10:11], FB.fst[:, 10:11]), "cV")
                B.wait("scalar", "cV", hv)
                self._f3_fo[sl] = B.dma16(
                    "scalar", lambda e, tt=tt, gc=gc, s2s=s2s: e.dma_start(
                        out=FB.d_h[128 * tt:128 * (tt + 1),
                                   512 * gc:512 * (gc + 1)],
                        in_=s2s), osem)
        self.f3_fo = (B.val("fo"), B.val("foB"))

    def phaseF4(self):
        B = self.B
        FB = self
        B.drain("vector")
        for tt in range(NT):
            B.em("vector", lambda e, tt=tt: e.tensor_scalar(
                FB.fst[:, 8:9], FB.mx2[:, tt:tt + 1], 1e-5, None,
                op0=ALU.max))
            B.drain("vector")
            B.em("vector", lambda e, tt=tt: e.scalar_tensor_tensor(
                FB.b2neg[:, tt:tt + 1], FB.fst[:, 8:9], -1.0 / 127.0,
                FB.swc[:, 1:2], op0=ALU.mult, op1=ALU.mult))
            B.em("vector", lambda e: e.reciprocal(
                FB.fst[:, 9:10], FB.fst[:, 8:9]))
            B.drain("vector")
            B.em("vector", lambda e, tt=tt: e.tensor_scalar(
                FB.s2q[:, tt:tt + 1], FB.fst[:, 9:10], 127.0, None,
                op0=ALU.mult))
            B.drain("vector")
        B.wait("vector", "fo", self.f3_fo[0])
        B.wait("vector", "foB", self.f3_fo[1])
        self.fev = [0] * KD
        mx = max(self._f3_s1r.values()) if self._f3_s1r else 0
        s1w = {0: mx, 1: mx}
        B.wait("sync", "fx", B.val("fx"))
        B.wait("sync", "fxB", B.val("fxB"))
        tkt = 0
        for tt in range(NT):
            for gc in range(4):
                for hh in range(2):
                    i = (tt * 4 + gc) * 2 + hh
                    sl = i % 2
                    lsem = "fx" if sl == 0 else "fxB"
                    s1h = FB.s1[:, 1024 * sl:1024 * (sl + 1)]
                    s2h = FB.s2[:, 1024 * sl:1024 * (sl + 1)]
                    qbh = FB.qbf[:, 1024 * sl:1024 * (sl + 1)]
                    B.wait("sync", "cV", s1w.get(sl, 0))
                    nfx = B.dma16("sync",
                                  lambda e, tt=tt, gc=gc, hh=hh, s1h=s1h:
                                  e.dma_start(
                        out=s1h,
                        in_=FB.d_h[128 * tt:128 * (tt + 1),
                                   2048 * gc + 1024 * hh:
                                   2048 * gc + 1024 * (hh + 1)]), lsem)
                    B.wait("vector", lsem, nfx)
                    if tkt:
                        B.wait("vector", "cT", tkt)
                    B.drain("vector")
                    B.em("vector", lambda e, tt=tt, s1h=s1h: e.tensor_scalar(
                        s1h, s1h, FB.s2q[:, tt:tt + 1], None,
                        op0=ALU.mult))
                    B.drain("vector")
                    B.em("vector", lambda e, s1h=s1h, s2h=s2h: e.tensor_copy(
                        s2h.bitcast(mybir.dt.int32), s1h))
                    B.drain("vector")
                    vq = B.tick("vector", lambda e, s2h=s2h, qbh=qbh:
                                e.tensor_copy(
                        qbh, s2h.bitcast(mybir.dt.int32)), "cV")
                    s1w[sl] = vq
                    B.wait("tensor", "cV", vq)
                    for k in range(8):
                        fj = 16 * gc + 8 * hh + k
                        pb = (B.P4 if k % 2 == 0
                              else B.P5)[:, 0:64].bitcast(BF16)
                        kk = 8 * hh + k
                        if self.fev[kk - 2]:
                            B.wait("tensor", "cV", self.fev[kk - 2])
                        tk = B.tick("tensor",
                                    lambda e, pb=pb, sl=sl, k=k:
                                    e.transpose(
                            pb, FB.qbf[:, 1024 * sl + 128 * k:
                                       1024 * sl + 128 * (k + 1)],
                            B.identb[:, :]), "cT")
                        if k == 7:
                            tkt = tk
                        B.wait("vector", "cT", tk)
                        self.fev[kk] = B.tick(
                            "vector",
                            lambda e, pb=pb, fj=fj, tt=tt: e.tensor_copy(
                                FB.hqT[:, TB * fj + 128 * tt:
                                       TB * fj + 128 * (tt + 1)], pb), "cV")
        self._f4_s1r = max(s1w.values())
        self.f4_done = B.val("cV")

    def phaseF5(self):
        B = self.B
        FB = self

        bt = B.val("cV")
        B.wait("tensor", "cV", bt)
        tb = B.tick("tensor", lambda e: e.transpose(
            B.P4[0:NT, 0:128], FB.b2neg[:, 0:NT], B.ident[:, :]), "cT")
        B.wait("vector", "cT", tb)
        bv = B.tick("vector", lambda e: e.tensor_copy(
            FB.row8[0:NT, :], B.P4[0:NT, 0:128]), "cV")
        B.wait("sync", "cV", bv)
        nbr = B.dma16("sync", lambda e: e.dma_start(
            out=FB.b2row[0:1, :].rearrange("o (t p) -> o t p", p=128),
            in_=FB.row8[0:NT, :].rearrange("t p -> o t p", o=1)
            if False else FB.row8[0:NT, :]), "fr")
        B.wait("tensor", "fr", nbr)
        prevc = 0
        for hh in range(2):
            if prevc:
                B.wait("tensor", "cV", prevc)
            th = B.t_mm(B.P7[:, 0:512], B.onesr[0:1, :],
                        FB.b2row[0:1, 512 * hh:512 * (hh + 1)], True, True,
                        tick=True)
            B.wait("vector", "cT", th)
            prevc = B.tick("vector", lambda e, hh=hh: e.tensor_copy(
                FB.bet2b[:, 512 * hh:512 * (hh + 1)], B.P7[:, 0:512]), "cV")
        nfx = B.dma16("sync", lambda e: e.dma_start(
            out=FB.b2c[:, :],
            in_=FB.d_b2.rearrange("o (j p) -> (o p) j", p=128),
            allow_slow_non_contiguous=True), "fx")
        bet_done = B.val("cV")

        q_tick = {0: 0, 1: 0}
        f5mm = {0: 0, 1: 0}
        nld = dict(self.f1_nld)
        yout_tick = {}
        nfy = 0
        x2tr_done = 0
        for dd in range(KD):
            ppair = [B.P1, B.P2] if dd % 2 == 0 else [B.P3, B.P0]
            B.wait("gpsimd", "cT", x2tr_done)
            if dd == 0:
                B.wait("gpsimd", "cV", self._f4_s1r)
                B.wait("gpsimd", "fx", B.val("fx"))
                B.wait("gpsimd", "fxB", B.val("fxB"))
            for tt in range(NT):
                nfx = B.dma16(
                    "gpsimd", lambda e, dd=dd, tt=tt: e.indirect_dma_start(
                        out=FB.x2c[:, 128 * tt:128 * (tt + 1)],
                        out_offset=None,
                        in_=B.d_x2.rearrange("t (j m) -> (t j) m", m=128),
                        element_offset=128 * dd,
                        in_offset=bass_rust.IndirectOffsetOnAxis(
                            ap=FB.xrows16[:, tt:tt + 1], axis=0)), "fg")
            for fg in range(4):
                sl = (4 * dd + fg) % 2
                sem = "fw%d" % sl
                if q_tick[sl]:
                    B.wait("sync", "cV", q_tick[sl])
                elif self.f1_q[sl]:
                    B.wait("sync", "cV", self.f1_q[sl])
                nld[sl] += 1
                B.dma16("sync", lambda e, dd=dd, fg=fg, sl=sl: e.dma_start(
                    out=FB.wf1[sl][:, :].rearrange("p (j m) -> p j m", m=128),
                    in_=FB.d_w2T.rearrange("(j p) m -> p j m", p=128)
                    [:, 16 * fg:16 * (fg + 1), 128 * dd:128 * (dd + 1)]),
                    sem)
                B.wait("vector", sem, 16 * nld[sl])
                if dd == 0 and fg == 0:
                    B.wait("tensor", "cV", self.f4_done)
                if dd <= 1 and fg <= 1:
                    B.wait("vector", "cT", self.f1_mm[sl])
                if f5mm[sl]:
                    B.wait("vector", "cT", f5mm[sl])
                B.drain("vector")
                B.em("vector", lambda e, sl=sl: e.tensor_scalar(
                    FB.qa[:, :], FB.wf1[sl][:, :], FB.thc[:, 1:2], None,
                    op0=ALU.is_gt))
                B.drain("vector")
                qt = B.tick("vector", lambda e, sl=sl: e.scalar_tensor_tensor(
                    FB.w1q[sl][:, :], FB.wf1[sl][:, :], FB.thc[:, 3:4],
                    FB.qa[:, :], op0=ALU.is_lt, op1=ALU.subtract), "cV")
                q_tick[sl] = qt
                B.wait("tensor", "cV", qt)
                if dd >= 2 and fg == 0:
                    B.wait("tensor", "cV", yout_tick[dd - 2])
                for k in range(KD):
                    fj = 16 * fg + k
                    for hh in range(2):
                        t5 = B.t_mm(ppair[hh][:, :],
                                    FB.w1q[sl][:, 128 * k:128 * (k + 1)],
                                    FB.hqT[:, TB * fj + 512 * hh:
                                           TB * fj + 512 * (hh + 1)],
                                    (fg == 0 and k == 0),
                                    fg == 3 and k == KD - 1,
                                    tick=(k == KD - 1 and hh == 1))
                        if k == KD - 1 and hh == 1:
                            f5mm[sl] = t5
            mt = B.val("cT")
            B.wait("vector", "cT", mt)
            if dd >= 1:
                B.wait("vector", "fy", nfy)
            B.drain("vector")
            for hh in range(2):
                if hh == 1:
                    ye = B.tick("vector",
                                lambda e, ppair=ppair, hh=hh: e.tensor_tensor(
                                    FB.ytT[:, 512 * hh:512 * (hh + 1)],
                                    ppair[hh][:, :],
                                    FB.bet2b[:, 512 * hh:512 * (hh + 1)],
                                    op=ALU.mult), "cV")
                else:
                    B.em("vector",
                         lambda e, ppair=ppair, hh=hh: e.tensor_tensor(
                             FB.ytT[:, 512 * hh:512 * (hh + 1)],
                             ppair[hh][:, :],
                             FB.bet2b[:, 512 * hh:512 * (hh + 1)],
                             op=ALU.mult))
            B.drain("vector")
            B.wait("tensor", "cV", ye)
            B.wait("tensor", "fg", nfx)
            sts = []
            for tt in range(NT):
                pb = (B.P4 if tt % 2 == 0 else B.P5)[:, 128:256]
                if tt >= 2:
                    B.wait("tensor", "cV", sts[tt - 2])
                tk = B.tick("tensor", lambda e, pb=pb, tt=tt: e.transpose(
                    pb, FB.x2c[:, 128 * tt:128 * (tt + 1)], B.ident[:, :]),
                    "cT")
                B.wait("vector", "cT", tk)
                st = B.tick("vector", lambda e, pb=pb, tt=tt, dd=dd:
                            e.scalar_tensor_tensor(
                                FB.ytT[:, 128 * tt:128 * (tt + 1)],
                                FB.ytT[:, 128 * tt:128 * (tt + 1)],
                                FB.b2c[:, dd:dd + 1], pb,
                                op0=ALU.add, op1=ALU.add), "cV")
                B.drain("vector")
                sts.append(st)
            yout_tick[dd] = ye
            x2tr_done = B.val("cT")
            B.wait("scalar", "cV", sts[-1])
            nfy = B.dma16("scalar", lambda e, dd=dd: e.dma_start(
                out=FB.d_yT[128 * dd:128 * (dd + 1), :],
                in_=FB.ytT[:, :]), "fy")
        B.wait("gpsimd", "fy", nfy)


def ffn_host_inputs(w1, w2, b1, gamma, beta, b2):
    return {
        "ffn_w1T": np.ascontiguousarray(w1.T),
        "ffn_w2T": np.ascontiguousarray(w2.T),
        "ffn_b1": np.ascontiguousarray(b1.reshape(1, FF)),
        "ffn_gamma": np.ascontiguousarray(gamma.reshape(1, FF)),
        "ffn_beta": np.ascontiguousarray(beta.reshape(1, FF)),
        "ffn_b2": np.ascontiguousarray(b2.reshape(1, D)),
    }


import concourse.bass as bass
from concourse.bass_utils import run_bass_kernel_spmd

N_CORES = 8
SEQ = NCH * CK
TOK_PER_CORE = SEQ // N_CORES


def _xrows_input(core):
    idx = (TOK_PER_CORE * core + np.arange(TOK_PER_CORE, dtype=np.int32)
           ).reshape(NT, 128)
    return {"ffn_xrows": np.ascontiguousarray(idx.T),
            "ffn_xrows16": np.ascontiguousarray(16 * idx.T)}


def _build_program():
    nc = bass.Bass("TRN2", target_bir_lowering=False)
    T = TTTBuilder(nc)
    PW = PWStream(T)
    T.declare(early=lambda core: PW.declare())
    qs = [PW.npairs // NCH + (1 if i < PW.npairs % NCH else 0)
          for i in range(NCH)]

    def hookfn(s, sub):
        if sub == 0:
            PW.hook(qs[s])

    T.emit(step_hook=hookfn)
    barrier = {s: T.val(s) for s in
               ("cV", "cT", "cS", "sx2", "cc", "sbo", "sbi",
                "sx", "sxc0", "sxc1", "sxc2")}
    ttt_names = [it[0] for it in T.ctx[T._sbuf_mark:]
                 if isinstance(it, tuple)]
    T.free_sbuf(ttt_names)
    FB = FFNBuilder(T)
    FB.declare(pw=PW)
    FB.emit(barrier=barrier, pw=PW)
    with nc.Block() as block:
        T.replay(block)
    return nc


def kernel(**inputs):
    x = np.ascontiguousarray(inputs["L0_attn_x"][0], dtype=np.float32)
    W0 = np.asarray(inputs["L0_attn_W0"], np.float32)
    b0 = np.asarray(inputs["L0_attn_b0"], np.float32)
    W1 = np.asarray(inputs["L0_attn_W1"], np.float32)
    b1a = np.asarray(inputs["L0_attn_b1"], np.float32)
    lr_w = np.asarray(inputs["L0_attn_lr_w"], np.float32)
    lr_b = np.asarray(inputs["L0_attn_lr_b"], np.float32)
    fg_w = np.asarray(inputs["L0_attn_fg_w"], np.float32)
    fg_b = np.asarray(inputs["L0_attn_fg_b"], np.float32)
    fw1 = np.asarray(inputs["L0_ffn_w1"], np.float32)
    fb1 = np.asarray(inputs["L0_ffn_b1"], np.float32)
    gamma = np.asarray(inputs["L0_ffn_gamma"], np.float32)
    beta = np.asarray(inputs["L0_ffn_beta"], np.float32)
    fw2 = np.asarray(inputs["L0_ffn_w2"], np.float32)
    fb2 = np.asarray(inputs["L0_ffn_b2"], np.float32)

    nc = _build_program()
    ffn_common = ffn_host_inputs(fw1, fw2, fb1, gamma, beta, fb2)
    in_maps = []
    for r in range(N_CORES):
        im = host_inputs(x, W0, b0, W1, b1a, lr_w, lr_b, fg_w, fg_b, r)
        im.update(ffn_common)
        im.update(_xrows_input(r))
        in_maps.append(im)
    res = run_bass_kernel_spmd(nc, in_maps, core_ids=list(range(N_CORES)))
    out = np.concatenate(
        [np.ascontiguousarray(res.results[r]["ffn_yT"]).T
         for r in range(N_CORES)], axis=0)
    return out.reshape(1, SEQ, D).astype(np.float32)


# version 07 (final)
# speedup vs baseline: 1.2589x; 1.0315x over previous
import sys
if "/opt/trn_rl_repo" not in sys.path:
    sys.path.insert(0, "/opt/trn_rl_repo")

import os
import numpy as np
import concourse.mybir as mybir

SKIP_BIASMM = os.environ.get("TTT_SKIP_BIASMM", "0") == "1"
SKIP_GRADS = os.environ.get("TTT_SKIP_GRADS", "0") == "1"
SKIP_GATES = os.environ.get("TTT_SKIP_GATES", "0") == "1"
TR_F32 = os.environ.get("TTT_TR_F32", "0") == "1"
SKIP_AR = os.environ.get("TTT_SKIP_AR", "0") == "1"
PHASE = int(os.environ.get("TTT_PHASE", "99"))
SUB = int(os.environ.get("TTT_SUB", "99"))

F32 = mybir.dt.float32
F32R = mybir.dt.float32r
AF = mybir.ActivationFunctionType
ALU = mybir.AluOpType

NCH = 64
D = 2048
MC = 256
CK = 128
KD = 16
LR_MEM = 0.01
GFAC = 2.0 / (CK * D)


class Core:
    def __init__(self, nc, n_cores=8):
        self.nc = nc
        self.ops = {e: [] for e in ("tensor", "vector", "scalar", "sync", "gpsimd")}
        self.cnt = {}
        self.n_cores = n_cores
        self.ctx = []
        self.sems = {}


    def em(self, eng, fn):
        self.ops[eng].append(fn)

    def drain(self, eng):
        self.ops[eng].append(lambda e: e.drain())

    def tick(self, eng, fn, sem_name):
        sem = self.sems[sem_name]
        self.ops[eng].append(lambda e, fn=fn, sem=sem: fn(e).then_inc(sem, 1))
        return self.inc(sem_name, 1)

    def dma16(self, eng, fn, sem_name):
        sem = self.sems[sem_name]
        self.ops[eng].append(lambda e, fn=fn, sem=sem: fn(e).then_inc(sem, 16))
        return self.inc(sem_name, 16)

    def wait(self, eng, sem_name, val):
        if val <= 0:
            return
        sem = self.sems[sem_name]
        self.ops[eng].append(lambda e, sem=sem, val=val: e.wait_ge(sem, val))

    def inc(self, name, by):
        self.cnt[name] = self.cnt.get(name, 0) + by
        return self.cnt[name]

    def val(self, name):
        return self.cnt.get(name, 0)

    def t_mm(self, psum_ap, lhsT, rhs, start, stop, tick=False):
        def fn(e, a=psum_ap, l=lhsT, r=rhs, st=start, sp=stop):
            return e.matmul(a, l, r, start=st, stop=sp)
        if tick:
            return self.tick("tensor", fn, "cT")
        self.em("tensor", fn)

    def t_tr(self, psum_ap, in_ap, f32r):
        if TR_F32 and f32r:
            in_ap = in_ap.bitcast(F32)
            f32r = False
        ident = self.identr[:, :] if f32r else self.ident[:, :]
        if f32r:
            psum_ap = psum_ap.bitcast(F32R)

        def fn(e, a=psum_ap, i=in_ap, idt=ident):
            return e.transpose(a, i, idt)
        return self.tick("tensor", fn, "cT")

    def new_sem(self, name):
        cm = self.nc.semaphore(name)
        self.sems[name] = cm.__enter__()
        self.ctx.append(cm)

    def sbuf(self, name, shape, dtyp=F32):
        cm = self.nc.sbuf_tensor(name, shape, dtyp)
        h = cm.__enter__()
        self.ctx.append((name, cm))
        return h

    def free_sbuf(self, names):
        kept = []
        for item in reversed(self.ctx):
            if isinstance(item, tuple) and item[0] in names:
                item[1].__exit__(None, None, None)
            else:
                kept.append(item)
        self.ctx = kept[::-1]

    def declare_common(self):
        nc = self.nc
        dt = nc.dram_tensor
        self.d_ident = dt("ttt_ident", [128, 128], F32, kind="ExternalInput").ap()
        self.d_onesr = dt("ttt_onesr", [1, 128], F32, kind="ExternalInput").ap()
        self.d_onesc = dt("ttt_onesc", [128, 1], F32, kind="ExternalInput").ap()
        self.d_epscol = dt("ttt_epscol", [128, 1], F32, kind="ExternalInput").ap()
        self.d_inv128 = dt("ttt_inv128", [1, 1], F32, kind="ExternalInput").ap()
        self.d_identr = dt("ttt_identr", [128, 128], F32R, kind="ExternalInput").ap()
        self.d_onesrr = dt("ttt_onesrr", [1, 128], F32R, kind="ExternalInput").ap()
        self.d_identb = dt("ttt_identb", [128, 128], mybir.dt.bfloat16,
                           kind="ExternalInput").ap()
        self.ident = self.sbuf("identsb", [128, 128])
        self.identb = self.sbuf("identbsb", [128, 128], mybir.dt.bfloat16)
        self.identr = self.sbuf("identrsb", [128, 128], F32R)
        self.onesrr = self.sbuf("onesrrsb", [1, 128], F32R)
        self.onesr = self.sbuf("onesrsb", [1, 128])
        self.onesc = self.sbuf("onescsb", [128, 1])
        self.epscol = self.sbuf("epscol", [128, 1])
        self.inv128 = self.sbuf("inv128", [1, 1])
        ps = nc.psum_tensor
        self.P0 = ps("PS0", [128, 512], F32).__enter__()
        self.P1 = ps("PS1", [128, 512], F32).__enter__()
        self.P2 = ps("PS2", [128, 512], F32).__enter__()
        self.P3 = ps("PS3", [128, 512], F32).__enter__()
        self.P4 = ps("PS4", [128, 512], F32).__enter__()
        self.P5 = ps("PS5", [128, 512], F32).__enter__()
        self.P6 = ps("PS6", [128, 512], F32).__enter__()
        self.P7 = ps("PS7", [128, 512], F32).__enter__()
        for s in ("cT", "cV", "cS"):
            self.new_sem(s)

    def load_common(self):
        self.new_sem("sconst")
        B = self

        def ld(e, dst, src):
            return e.dma_start(out=dst, in_=src)
        B.dma16("sync", lambda e: ld(e, B.ident[:, :], B.d_ident), "sconst")
        B.dma16("sync", lambda e: ld(e, B.identr[:, :], B.d_identr), "sconst")
        B.dma16("sync", lambda e: ld(e, B.identb[:, :], B.d_identb), "sconst")
        B.dma16("sync", lambda e: ld(e, B.onesrr[:, :], B.d_onesrr), "sconst")
        B.dma16("sync", lambda e: ld(e, B.onesr[:, :], B.d_onesr), "sconst")
        B.dma16("sync", lambda e: ld(e, B.onesc[:, :], B.d_onesc), "sconst")
        B.dma16("sync", lambda e: ld(e, B.epscol[:, :], B.d_epscol), "sconst")
        B.dma16("sync", lambda e: ld(e, B.inv128[:, :], B.d_inv128), "sconst")
        return B.val("sconst")

    def replay(self, block):
        ops = self.ops

        @block.tensor
        def _(e):
            for f in ops["tensor"]:
                f(e)

        @block.vector
        def _(e):
            for f in ops["vector"]:
                f(e)

        @block.scalar
        def _(e):
            for f in ops["scalar"]:
                f(e)

        @block.sync
        def _(e):
            for f in ops["sync"]:
                f(e)

        @block.gpsimd
        def _(e):
            for f in ops["gpsimd"]:
                f(e)


def common_inputs():
    import ml_dtypes
    return {
        "ttt_identb": np.eye(128, dtype=np.float32).astype(ml_dtypes.bfloat16),
        "ttt_ident": np.eye(128, dtype=np.float32),
        "ttt_identr": np.eye(128, dtype=np.float32),
        "ttt_onesr": np.ones((1, 128), np.float32),
        "ttt_onesrr": np.ones((1, 128), np.float32),
        "ttt_onesc": np.ones((128, 1), np.float32),
        "ttt_epscol": np.full((128, 1), 1e-5, np.float32),
        "ttt_inv128": np.full((1, 1), 1.0 / 128, np.float32),
    }



class TTTBuilder(Core):
    def __init__(self, nc, n_cores=8):
        super().__init__(nc, n_cores)
        self.chunk_sx = {}
        self.fwd_z_tick = {}
        self.pred_free = 0
        self.square_tick = {}
        self.tdh_tick = {}
        self.x2_tick = 0
        self.dsb_free = 0
        self.prep_ct = {}

    def declare(self, x2_dst=None, early=None):
        nc = self.nc
        dt = nc.dram_tensor
        self.d_x = dt("ttt_x", [NCH * CK, D], F32, kind="ExternalInput").ap()
        self.d_W0T = dt("ttt_W0T", [D, MC], F32, kind="ExternalInput").ap()
        self.d_b0 = dt("ttt_b0", [1, MC], F32, kind="ExternalInput").ap()
        self.d_W1T = dt("ttt_W1T", [MC, D], F32, kind="ExternalInput").ap()
        self.d_W1n = dt("ttt_W1n", [D, MC], F32R, kind="ExternalInput").ap()
        self.d_b1 = dt("ttt_b1", [1, D], F32R, kind="ExternalInput").ap()
        self.d_lrfgT = dt("ttt_lrfgT", [D, 2], F32, kind="ExternalInput").ap()
        self.d_lrb = dt("ttt_lrb", [1, 1], F32, kind="ExternalInput").ap()
        self.d_fgb = dt("ttt_fgb", [1, 1], F32, kind="ExternalInput").ap()
        if x2_dst is None:
            x2_dst = dt("ttt_x2", [NCH * CK, D], F32, kind="ExternalOutput").ap()
        self.d_x2 = x2_dst
        self.bin = [dt(f"ttt_arin{i}", [2 * CK, D], F32) for i in range(2)]
        self.bout = [dt(f"ttt_arout{i}", [2 * CK, D], F32) for i in range(2)]

        self.declare_common()
        if early is not None:
            early(self)
        self._sbuf_mark = len(self.ctx)
        S = self.sbuf
        self.W0T = S("W0T", [128, KD * MC])
        self.W1T = S("W1T", [128, 2 * D])
        self.W1n = S("W1n", [128, KD * MC], F32R)
        self.b0 = S("b0sb", [128, 2])
        self.b1 = S("b1sb", [1, D], F32R)
        self.lrfgT = S("lrfgT", [128, KD * 2])
        self.lrbcol = S("lrbcol", [128, 1])
        self.fgb11 = S("fgb11", [1, 1])
        self.lrb11 = S("lrb11", [1, 1])
        self.xraw = [S(f"xraw{i}", [128, D]) for i in range(3)]
        self.ctok = [S(f"ctok{i}", [128, D]) for i in range(2)]
        self.cT = S("cTs", [128, KD * 256])
        self.zsb = [S(f"zsb{i}", [128, 2 * MC]) for i in range(2)]
        self.H2 = [S(f"H2sb{i}", [128, 2 * MC]) for i in range(2)]
        self.pred = S("predsb", [128, 2 * D])
        self.part = S("partsb", [128, 2 * D])
        self.dsb = S("dsb", [128, D], F32R)
        self.dT = S("dTsb", [128, KD * 128], F32R)
        self.hs = S("hssb", [128, MC], F32R)
        self.dzf = S("dzfsb", [128, MC], F32R)
        self.dzt = S("dztsb", [128, MC], F32R)
        self.x2sb = S("x2sb", [128, D])
        self.sg = S("sgsb", [128, 2 * 128])
        self.sp1 = S("sp1sb", [128, 2 * 128])
        self.sp2 = S("sp2sb", [128, 2 * 128])
        self.stat = S("statsb", [128, 8])
        self.gred = S("gredsb", [128, 2])
        self.scal = [S(f"scal{i}", [1, 4]) for i in range(3)]
        self.fk = [S(f"fkcol{i}", [128, 2]) for i in range(3)]
        self.k8m = [S(f"k8mat{i}", [128, 128], F32R) for i in range(3)]
        self.k8r = S("k8row", [1, 128])
        self.gb0c = S("gb0c", [128, 2])
        self.ctokr = S("ctokr", [128, D], F32R)
        self.k2sb = S("k2sb", [128, 256], F32R)

        for s in ("sx", "sxc0", "sxc1", "sxc2", "sbo", "sbi", "sx2", "cc"):
            self.new_sem(s)

    TTT_BIG = ("W0T", "W1T", "W1n", "xraw0", "xraw1", "xraw2", "ctok0",
               "ctok1", "cTs", "zsb0", "zsb1", "H2sb0", "H2sb1",
               "predsb", "partsb", "dsb", "dTsb", "hssb", "dzfsb", "dztsb",
               "x2sb", "sgsb", "sp1sb", "sp2sb", "k8mat0", "k8mat1",
               "k8mat2", "ctokr", "k2sb", "lrfgT")

    def emit(self, step_hook=None):
        self.step_hook = step_hook
        if PHASE == 0:
            self.prologue_noflood()
            self.debug_exit()
            return
        self.prologue()
        if PHASE == 1:
            self.debug_exit()
            return
        for s in range(min(NCH, PHASE - 1)):
            self.step(s)
        if PHASE - 1 < NCH:
            self.debug_exit()
            return
        self.epilogue()

    def prologue_noflood(self):
        real_fwd = self.fwd
        self.fwd = lambda s: None
        try:
            self.prologue()
        finally:
            self.fwd = real_fwd

    def debug_exit(self):
        B = self
        B.wait("gpsimd", "cV", B.val("cV"))
        B.wait("gpsimd", "cT", B.val("cT"))
        B.wait("gpsimd", "cS", B.val("cS"))
        B.dma16("gpsimd", lambda e: e.dma_start(
            out=B.d_x2[0:128, :], in_=B.ctok[0][:, :]), "sx2")
        B.wait("gpsimd", "sx2", B.val("sx2"))

    def prologue(self):
        B = self

        def ld(e, dst, src, allow_slow_non_contiguous=False):
            return e.dma_start(out=dst, in_=src,
                               allow_slow_non_contiguous=allow_slow_non_contiguous)

        B.dma16("sync", lambda e: ld(
            e, B.W0T[:, :].rearrange("p (j m) -> p j m", m=MC),
            B.d_W0T.rearrange("(j p) m -> p j m", p=128)), "sx")
        B.dma16("sync", lambda e: ld(
            e, B.W1T[:, :].rearrange("p (t m) -> p t m", m=D),
            B.d_W1T.rearrange("(t p) m -> p t m", p=128)), "sx")
        B.dma16("sync", lambda e: ld(
            e, B.W1n[:, :].rearrange("p (j m) -> p j m", m=MC),
            B.d_W1n.rearrange("(j p) m -> p j m", p=128)), "sx")
        B.dma16("sync", lambda e: ld(
            e, B.b0[:, :], B.d_b0.rearrange("o (t p) -> (o p) t", p=128),
        allow_slow_non_contiguous=True), "sx")
        B.dma16("sync", lambda e: ld(e, B.b1[:, :], B.d_b1), "sx")
        B.dma16("sync", lambda e: ld(
            e, B.lrfgT[:, :].rearrange("p (j m) -> p j m", m=2),
            B.d_lrfgT.rearrange("(j p) m -> p j m", p=128)), "sx")
        B.dma16("sync", lambda e: ld(e, B.fgb11[:, :], B.d_fgb), "sx")
        B.dma16("sync", lambda e: ld(e, B.lrb11[:, :], B.d_lrb), "sx")
        ncst = B.load_common()
        nload = B.val("sx")
        B.wait("vector", "sconst", ncst)
        B.wait("tensor", "sconst", ncst)
        B.wait("scalar", "sconst", ncst)

        for c in range(2):
            B.chunk_sx[c] = ("sxc%d" % (c % 3), B.dma16("sync", lambda e, c=c: ld(
                e, B.xraw[c % 3][:, :], B.d_x[128 * c:128 * (c + 1), :]),
                "sxc%d" % (c % 3)))

        B.wait("vector", "sx", nload)
        B.em("vector", lambda e: e.tensor_scalar(
            B.b1[:, :], B.b1[:, :], 0.125, None, op0=ALU.mult))
        B.part_zero = B.tick(
            "vector", lambda e: e.memset(B.part[:, 0:D], 0.0), "cV")
        B.wait("tensor", "sx", nload)
        tb = B.t_mm(B.P6[:, 0:1], B.onesr[0:1, :], B.lrb11[0:1, :], True, True,
                    tick=True)
        B.wait("vector", "cT", tb)
        B.tick("vector", lambda e: e.tensor_copy(B.lrbcol[:, :], B.P6[:, 0:1]),
               "cV")

        self.prep(0)
        self.prep(1)
        self.fwd(0)

    def zwin(self, s):
        B = self
        css, csn = s % 2, (s + 1) % 2
        new_ch = s + 1 <= NCH - 1
        B.wait("vector", "sbi", 32 * (s + 1))
        B.wait("tensor", "cV",
               self.prep_ct.get(s + 1 if new_ch else s, 0))
        if new_ch:
            first = True
            for t2 in range(2):
                for jj in range(KD):
                    B.t_mm(B.P0[:, 128 * t2:128 * (t2 + 1)],
                           B.W0T[:, MC * jj + 128 * t2:
                                 MC * jj + 128 * t2 + 128],
                           B.cT[:, 256 * jj + 128 * csn:
                                256 * jj + 128 * csn + 128],
                           first, t2 == 1 and jj == KD - 1,
                           tick=(t2 == 1 and jj == KD - 1))
                    first = False
        first = True
        if new_ch:
            for jj in range(KD):
                B.t_mm(B.P3[:, 0:128],
                       B.cT[:, 256 * jj + 128 * css:
                            256 * jj + 128 * css + 128],
                       B.cT[:, 256 * jj + 128 * csn:
                            256 * jj + 128 * csn + 128],
                       first, False)
                first = False
        for jj in range(KD):
            B.t_mm(B.P3[:, 128:256],
                   B.cT[:, 256 * jj + 128 * css:256 * jj + 128 * css + 128],
                   B.cT[:, 256 * jj + 128 * css:256 * jj + 128 * css + 128],
                   first, jj == KD - 1, tick=(jj == KD - 1))
            first = False
        tks = B.val("cT")
        B.wait("vector", "cT", tks)
        if new_ch:
            B.em("vector", lambda e: e.tensor_copy(
                B.part[:, 0:256], B.P0[:, 0:256]))
            B.em("vector", lambda e: e.tensor_copy(
                B.k2sb[:, 128:256], B.P3[:, 0:128]))
        self.zwin_ev = B.tick("vector", lambda e: e.tensor_copy(
            B.k2sb[:, 0:128], B.P3[:, 128:256]), "cV")
        self.fwd_z_tick[s + 1] = tks

    def prep(self, c):
        B = self
        sl = c % 3
        sl2 = c % 2
        xr = B.xraw[sl]
        B.wait("scalar", B.chunk_sx[c][0], B.chunk_sx[c][1])
        if c >= 3:
            B.wait("scalar", "sx2", 16 * (c - 2))
        B.wait("scalar", "cV", B.x2_tick)
        ssq = B.tick("scalar", lambda e: e.activation(
            B.x2sb[:, :], xr[:, :], AF.Square), "cS")
        B.square_tick[c] = ssq
        B.wait("vector", "cS", ssq)
        B.wait("vector", "cT", B.fwd_z_tick.get(c - 1, 0))
        B.em("vector", lambda e: e.tensor_reduce(
            B.stat[:, 2:3], xr[:, :], axis=mybir.AxisListType.X, op=ALU.add))
        B.em("vector", lambda e: e.tensor_reduce(
            B.stat[:, 3:4], B.x2sb[:, :], axis=mybir.AxisListType.X, op=ALU.add))
        B.drain("vector")
        B.em("vector", lambda e: e.tensor_scalar(
            B.stat[:, 0:1], B.stat[:, 2:3], 1.0 / D, None, op0=ALU.mult))
        B.em("vector", lambda e: e.tensor_scalar(
            B.stat[:, 4:5], B.stat[:, 3:4], 1.0 / D, None, op0=ALU.mult))
        B.drain("vector")
        B.em("vector", lambda e: e.tensor_tensor(
            B.stat[:, 5:6], B.stat[:, 0:1], B.stat[:, 0:1], op=ALU.mult))
        B.drain("vector")
        vv = B.tick("vector", lambda e: e.tensor_tensor(
            B.stat[:, 6:7], B.stat[:, 4:5], B.stat[:, 5:6], op=ALU.subtract),
            "cV")
        B.wait("scalar", "cV", vv)
        ssd = B.tick("scalar", lambda e: e.activation(
            B.stat[:, 5:6], B.stat[:, 6:7], AF.Sqrt,
            bias=B.epscol[:, 0:1]), "cS")
        B.wait("vector", "cS", ssd)
        B.em("vector", lambda e: e.reciprocal(B.stat[:, 1:2], B.stat[:, 5:6]))
        B.drain("vector")
        vc = B.tick("vector", lambda e, sl2=sl2: e.tensor_scalar(
            B.ctok[sl2][:, :], xr[:, :], B.stat[:, 0:1], B.stat[:, 1:2],
            op0=ALU.subtract, op1=ALU.mult), "cV")

        B.wait("tensor", "cV", vc)
        evac_t = []
        for jj in range(KD):
            if jj >= 2:
                B.wait("tensor", "cV", evac_t[jj - 2])
            pb = (B.P4 if jj % 2 == 0 else B.P5)[:, 0:128]
            tt = B.t_tr(pb, B.ctok[sl2][:, 128 * jj:128 * (jj + 1)], f32r=False)
            B.wait("vector", "cT", tt)
            ev = B.tick("vector", lambda e, pb=pb, jj=jj, sl2=sl2: e.tensor_copy(
                B.cT[:, 256 * jj + 128 * sl2:256 * jj + 128 * sl2 + 128],
                pb), "cV")
            evac_t.append(ev)
        self.prep_ct[c] = evac_t[-1]

        B.wait("tensor", "cV", evac_t[-1])
        for jj in range(KD):
            B.t_mm(B.P6[:, 0:2],
                   B.cT[:, 256 * jj + 128 * sl2:256 * jj + 128 * sl2 + 128],
                   B.lrfgT[:, 2 * jj:2 * (jj + 1)], jj == 0, jj == KD - 1,
                   tick=(jj == KD - 1))
        tg = B.val("cT")
        B.wait("scalar", "cT", tg)
        B.em("scalar", lambda e: e.activation(
            B.gred[:, 0:1], B.P6[:, 0:1], AF.Sigmoid, bias=B.lrbcol[:, 0:1]))
        sg2 = B.tick("scalar", lambda e: e.copy(B.gred[:, 1:2], B.P6[:, 1:2]),
                     "cS")
        B.wait("tensor", "cS", sg2)
        tr2 = B.t_mm(B.P6[0:1, 4:6], B.onesc[:, 0:1], B.gred[:, :], True, True,
                     tick=True)
        sc = B.scal[sl]
        B.wait("scalar", "cT", tr2)
        sf = B.tick("scalar", lambda e, sc=sc: e.activation(
            sc[0:1, 0:1], B.P6[0:1, 5:6], AF.Sigmoid,
            bias=B.fgb11[0:1, 0:1], scale=B.inv128[0:1, 0:1]), "cS")
        B.wait("vector", "cS", sf)
        B.em("vector", lambda e, sc=sc: e.tensor_scalar(
            sc[0:1, 1:2], B.P6[0:1, 4:5], LR_MEM / 128.0 * GFAC, None,
            op0=ALU.mult))
        B.drain("vector")
        B.em("vector", lambda e, sc=sc: e.tensor_scalar(
            sc[0:1, 2:3], sc[0:1, 1:2], 0.125, None, op0=ALU.mult))
        B.drain("vector")
        vk = B.tick("vector", lambda e, sc=sc: e.tensor_scalar(
            B.k8r[0:1, :], B.onesr[0:1, :], sc[0:1, 2:3], None, op0=ALU.mult),
            "cV")
        B.wait("tensor", "cV", vk)
        B.t_mm(B.P6[:, 8:10], B.onesr[0:1, :], sc[0:1, 0:2], True, True)
        tbc = B.t_mm(B.P7[:, 0:128], B.onesr[0:1, :], B.k8r[0:1, :], True, True,
                     tick=True)
        B.wait("vector", "cT", tbc)
        B.em("vector", lambda e, sl=sl: e.tensor_copy(
            B.fk[sl][:, 0:2], B.P6[:, 8:10]))
        B.tick("vector", lambda e, sl=sl: e.tensor_copy(
            B.k8m[sl][:, :], B.P7[:, 0:128]), "cV")

    def fwd(self, s):
        B = self
        halves = []
        if s >= 1:
            halves.append((0, (s - 1) % 2))
        if s <= NCH - 1:
            halves.append((1, s % 2))
        slot = s % 2

        B.wait("tensor", "cV", B.val("cV"))
        if len(halves) == 2:
            lo, hi = 0, 256
        else:
            lo = 128 * halves[0][1]
            hi = lo + 128
        if s == 0:
            first = True
            for jj in range(KD):
                for t2 in range(2):
                    last = (jj == KD - 1 and t2 == 1)
                    csl = halves[0][1]
                    off = 256 * jj + 128 * csl
                    B.t_mm(B.P0[:, 256 * t2 + 128 * csl:
                                256 * t2 + 128 * csl + 128],
                           B.W0T[:, MC * jj + 128 * t2:
                                 MC * jj + 128 * t2 + 128],
                           B.cT[:, off:off + 128],
                           first, last, tick=last)
                    first = False
            tz = B.val("cT")
            B.fwd_z_tick[s] = tz
            B.wait("scalar", "cT", tz)
            B.wait("vector", "cT", tz)
            B.drain("vector")
            for t2 in range(2):
                def sil(e, t2=t2, slot=slot, lo=lo, hi=hi):
                    return e.activation(
                        B.H2[slot][:, 256 * t2 + lo:256 * t2 + hi],
                        B.P0[:, 256 * t2 + lo:256 * t2 + hi], AF.Silu,
                        bias=B.b0[:, t2:t2 + 1])
                if t2 == 1:
                    sh = B.tick("scalar", sil, "cS")
                else:
                    B.em("scalar", sil)

                def zc(e, t2=t2, slot=slot, lo=lo, hi=hi):
                    return e.tensor_scalar(
                        B.zsb[slot][:, 256 * t2 + lo:256 * t2 + hi],
                        B.P0[:, 256 * t2 + lo:256 * t2 + hi],
                        B.b0[:, t2:t2 + 1], None, op0=ALU.add)
                if t2 == 1:
                    shv = B.tick("vector", zc, "cV")
                else:
                    B.em("vector", zc)
        else:
            fprev = B.fk[(s - 1) % 3][:, 0:1]
            pslot = (s - 1) % 2
            ncorr = 2 * len(halves)
            ic = 0
            for h, csl in halves:
                k2ap = (B.k2sb[:, 0:128] if h == 0
                        else B.k2sb[:, 128:256])
                for t2 in range(2):
                    B.t_mm(B.P0[:, 256 * csl + 128 * t2:
                                256 * csl + 128 * t2 + 128],
                           B.dzt[:, 128 * t2:128 * (t2 + 1)],
                           k2ap, ic == 0, ic == ncorr - 1,
                           tick=(ic == ncorr - 1))
                    ic += 1
            tz = B.val("cT")
            B.wait("vector", "cT", tz)
            B.drain("vector")
            cticks = []
            for h, csl in halves:
                for t2 in range(2):
                    if h == 0:
                        zsrc = B.zsb[pslot][:, 256 * t2 + 128 * csl:
                                            256 * t2 + 128 * csl + 128]
                    else:
                        zsrc = B.part[:, 128 * t2:128 * (t2 + 1)]
                    dst = B.zsb[slot][:, 256 * t2 + 128 * csl:
                                      256 * t2 + 128 * csl + 128]
                    B.em("vector", lambda e, dst=dst, zsrc=zsrc,
                         fprev=fprev, csl=csl, t2=t2:
                         e.scalar_tensor_tensor(
                             dst, zsrc, fprev,
                             B.P0[:, 256 * csl + 128 * t2:
                                  256 * csl + 128 * t2 + 128],
                             op0=ALU.mult, op1=ALU.subtract))
                    B.drain("vector")
                    ct = B.tick("vector", lambda e, dst=dst, t2=t2:
                                e.tensor_scalar(
                                    dst, dst, B.b0[:, t2:t2 + 1], None,
                                    op0=ALU.add), "cV")
                    B.drain("vector")
                    cticks.append(ct)
            shv = cticks[-1]
            B.wait("scalar", "cV", shv)
            for t2 in range(2):
                def sil(e, t2=t2, slot=slot, lo=lo, hi=hi):
                    return e.activation(
                        B.H2[slot][:, 256 * t2 + lo:256 * t2 + hi],
                        B.zsb[slot][:, 256 * t2 + lo:256 * t2 + hi],
                        AF.Silu)
                if t2 == 1:
                    sh = B.tick("scalar", sil, "cS")
                else:
                    B.em("scalar", sil)

        B.wait("tensor", "cS", sh)
        B.wait("tensor", "cV", shv)
        if s >= 1:
            B.wait("vector", "sbi", 32 * s)
        pv = []
        for hi, (h, csl) in enumerate(halves):
            for ch in range(4):
                pb = B.P1 if (hi * 4 + ch) % 2 == 0 else B.P2
                if len(pv) >= 2:
                    B.wait("tensor", "cV", pv[-2])
                for t2 in range(2):
                    B.t_mm(pb[:, :],
                           B.H2[slot][:, 256 * t2 + 128 * csl:
                                      256 * t2 + 128 * csl + 128],
                           B.W1T[:, D * t2 + 512 * ch:D * t2 + 512 * (ch + 1)],
                           t2 == 0, False)
                if SKIP_BIASMM:
                    tpm = B.t_mm(pb[:, :],
                                 B.H2[slot][:, 128 * h:128 * h + 128],
                                 B.W1T[:, 512 * ch:512 * (ch + 1)],
                                 False, True, tick=True)
                else:
                    tpm = B.tick("tensor", lambda e, pb=pb, ch=ch: e.matmul(
                        pb[:, :], B.onesrr[0:1, :],
                        B.b1[0:1, 512 * ch:512 * (ch + 1)],
                        start=False, stop=True), "cT")
                B.wait("vector", "cT", tpm)
                ev = B.tick("vector",
                            lambda e, pb=pb, h=h, ch=ch: e.tensor_copy(
                                B.part[:, D * h + 512 * ch:
                                       D * h + 512 * (ch + 1)], pb[:, :]),
                            "cV")
                pv.append(ev)
        B.wait("scalar", "cV", pv[-1])
        if s == 0:
            B.wait("scalar", "cV", B.part_zero)
        if s >= 2:
            B.wait("scalar", "cc", s - 1)
        for h in range(2):
            B.dma16("scalar", lambda e, h=h, slot=slot: e.dma_start(
                out=B.bin[slot][128 * h:128 * (h + 1), :],
                in_=B.part[:, D * h:D * (h + 1)]), "sbi")
        B.wait("gpsimd", "sbi", B.val("sbi"))
        if s >= 2:
            B.wait("gpsimd", "sbo", 32 * (s - 1))
        B.em("gpsimd", lambda e, slot=slot, sem=B.sems["cc"]: e.collective_compute(
            "AllReduce", ALU.add, replica_groups=[list(range(B.n_cores))],
            ins=[B.bin[slot].ap().opt()],
            outs=[B.bout[slot].ap().opt()]).then_inc(sem, 1))
        B.inc("cc", 1)

        B.wait("sync", "cc", B.val("cc"))
        B.wait("sync", "cV", B.pred_free)
        for h in range(2):
            B.dma16("sync", lambda e, h=h, slot=slot: e.dma_start(
                out=B.pred[:, D * h:D * (h + 1)],
                in_=B.bout[slot][128 * h:128 * (h + 1), :]), "sbo")

    def step(self, s):
        B = self
        slot = s % 2
        csl = s % 3
        csl2 = s % 2
        kap_ap = B.fk[csl][:, 1:2]
        f_ap = B.fk[csl][:, 0:1]
        f11 = B.scal[csl][0:1, 0:1]

        if self.step_hook is not None:
            self.step_hook(s, 0)
        self.zwin(s)

        B.drain("vector")
        B.wait("vector", "sbo", 32 * (s + 1))
        B.wait("vector", "cT", B.dsb_free)
        vd = B.tick("vector", lambda e, csl2=csl2: e.tensor_tensor(
            B.dsb[:, :], B.pred[:, D:2 * D], B.ctok[csl2][:, :],
            op=ALU.subtract), "cV")
        if s >= 1:
            if s >= 2:
                B.wait("vector", "sx2", 16 * (s - 1))
            B.wait("vector", "cS", B.square_tick.get(s + 1, 0))
            vx2 = B.tick("vector", lambda e, s=s: e.tensor_tensor(
                B.x2sb[:, :], B.xraw[(s - 1) % 3][:, :],
                B.pred[:, 0:D], op=ALU.add), "cV")
            B.wait("gpsimd", "cV", vx2)
            B.dma16("gpsimd", lambda e, s=s: e.dma_start(
                out=B.d_x2[128 * (s - 1):128 * s, :], in_=B.x2sb[:, :]), "sx2")
            B.pred_free = vx2
            B.x2_tick = vx2
        else:
            B.pred_free = vd

        if s + 2 <= NCH - 1:
            if s >= 1:
                B.wait("sync", "sx2", 16 * s)
            B.chunk_sx[s + 2] = ("sxc%d" % ((s + 2) % 3),
                                 B.dma16("sync", lambda e, s=s: e.dma_start(
                                     out=B.xraw[(s + 2) % 3][:, :],
                                     in_=B.d_x[128 * (s + 2):128 * (s + 3), :]),
                                     "sxc%d" % ((s + 2) % 3)))

        B.wait("tensor", "cV", vd)
        evt = []
        for jj in range(KD):
            if jj >= 2:
                B.wait("tensor", "cV", evt[jj - 2])
            pb = (B.P4 if jj % 2 == 0 else B.P5)[:, 0:128]
            tt = B.t_tr(pb, B.dsb[:, 128 * jj:128 * (jj + 1)], f32r=True)
            B.wait("vector", "cT", tt)
            ev = B.tick("vector", lambda e, pb=pb, jj=jj: e.tensor_copy(
                B.dT[:, 128 * jj:128 * (jj + 1)], pb), "cV")
            evt.append(ev)

        if SUB <= 2:
            return
        B.tick("vector", lambda e, csl2=csl2: e.tensor_copy(
            B.ctokr[:, :], B.ctok[csl2][:, :]), "cV")

        B.wait("tensor", "cV", evt[-1])
        for t2 in range(2):
            pb = (B.P4 if t2 % 2 == 0 else B.P5)[:, 128:256]
            th = B.t_tr(pb, B.H2[slot][:, 256 * t2 + 128 * slot:
                                       256 * t2 + 128 * slot + 128],
                        f32r=False)
            B.wait("vector", "cT", th)
            B.tick("vector", lambda e, pb=pb, t2=t2, kap_ap=kap_ap:
                   e.tensor_scalar(B.hs[:, 128 * t2:128 * (t2 + 1)], pb,
                                   kap_ap, None, op0=ALU.mult), "cV")
        sh = B.val("cV")

        if SUB <= 3:
            return
        B.wait("tensor", "cV", evt[-1])
        first = True
        for jj in range(KD):
            for mc in range(2):
                last = (jj == KD - 1 and mc == 1)
                B.t_mm(B.P3[:, 128 * mc:128 * (mc + 1)],
                       B.W1n[:, MC * jj + 128 * mc:MC * jj + 128 * mc + 128],
                       B.dT[:, 128 * jj:128 * (jj + 1)],
                       first, last, tick=last)
                first = False
        tdh = B.val("cT")
        B.tdh_tick[s] = tdh

        if SUB <= 4:
            return
        B.wait("tensor", "cV", sh)
        gwv = []
        for t2 in range(2):
            for ch in range(4):
                pb = B.P1 if (t2 * 4 + ch) % 2 == 0 else B.P2
                if len(gwv) >= 2:
                    B.wait("tensor", "cV", gwv[-2])
                tm = B.t_mm(pb[:, :],
                            B.hs[:, 128 * t2:128 * (t2 + 1)],
                            B.dsb[:, 512 * ch:512 * (ch + 1)],
                            True, True, tick=True)
                B.wait("vector", "cT", tm)
                ev = B.tick(
                    "vector",
                    lambda e, pb=pb, t2=t2, ch=ch, f_ap=f_ap:
                    e.scalar_tensor_tensor(
                        B.W1T[:, D * t2 + 512 * ch:D * t2 + 512 * (ch + 1)],
                        B.W1T[:, D * t2 + 512 * ch:D * t2 + 512 * (ch + 1)],
                        f_ap, pb[:, :], op0=ALU.mult, op1=ALU.subtract), "cV")
                gwv.append(ev)

        if SUB <= 5:
            return
        if self.step_hook is not None:
            self.step_hook(s, 1)
        B.wait("scalar", "cT", tdh)
        for t2 in range(2):
            zh = B.zsb[slot][:, 256 * t2 + 128 * slot:
                             256 * t2 + 128 * slot + 128]
            B.tick("scalar", lambda e, t2=t2, zh=zh: e.activation(
                B.sg[:, 128 * t2:128 * (t2 + 1)], zh, AF.Sigmoid), "cS")
        ssg = B.val("cS")
        B.wait("vector", "cS", ssg)
        for t2 in range(2):
            zh = B.zsb[slot][:, 256 * t2 + 128 * slot:
                             256 * t2 + 128 * slot + 128]
            sgh = B.sg[:, 128 * t2:128 * (t2 + 1)]
            s1 = B.sp1[:, 128 * t2:128 * (t2 + 1)]
            s2 = B.sp2[:, 128 * t2:128 * (t2 + 1)]
            dzh = B.dzf[:, 128 * t2:128 * (t2 + 1)]
            B.em("vector", lambda e, zh=zh, sgh=sgh, s1=s1: e.tensor_tensor(
                s1, zh, sgh, op=ALU.mult))
            B.drain("vector")
            B.em("vector", lambda e, sgh=sgh, s1=s1, s2=s2: e.tensor_tensor(
                s2, s1, sgh, op=ALU.mult))
            B.drain("vector")
            B.em("vector", lambda e, s1=s1, s2=s2: e.tensor_tensor(
                s1, s1, s2, op=ALU.subtract))
            B.drain("vector")
            B.em("vector", lambda e, sgh=sgh, s1=s1: e.tensor_tensor(
                s1, sgh, s1, op=ALU.add))
            B.drain("vector")
            B.tick("vector", lambda e, t2=t2, s1=s1, dzh=dzh: e.tensor_tensor(
                dzh, B.P3[:, 128 * t2:128 * (t2 + 1)], s1, op=ALU.mult), "cV")
        vdz = B.val("cV")

        if SUB <= 6:
            return
        B.wait("tensor", "cV", vdz)
        for t2 in range(2):
            pb = (B.P4 if t2 % 2 == 0 else B.P5)[:, 256:384]
            tt = B.t_tr(pb, B.dzf[:, 128 * t2:128 * (t2 + 1)], f32r=True)
            B.wait("vector", "cT", tt)
            B.tick("vector", lambda e, pb=pb, t2=t2, kap_ap=kap_ap:
                   e.tensor_scalar(B.dzt[:, 128 * t2:128 * (t2 + 1)], pb,
                                   kap_ap, None, op0=ALU.mult), "cV")
        sdz = B.val("cV")

        B.wait("tensor", "cV", sdz)
        g0v = []
        for jj in range(KD):
            pb = B.P1 if jj % 2 == 0 else B.P2
            if len(g0v) >= 2:
                B.wait("tensor", "cV", g0v[-2])
            tm = B.t_mm(pb[:, 0:256],
                        B.ctokr[:, 128 * jj:128 * (jj + 1)],
                        B.dzt[:, :], True, True, tick=True)
            B.wait("vector", "cT", tm)
            ev = B.tick("vector", lambda e, pb=pb, jj=jj, f_ap=f_ap:
                        e.scalar_tensor_tensor(
                            B.W0T[:, MC * jj:MC * (jj + 1)],
                            B.W0T[:, MC * jj:MC * (jj + 1)],
                            f_ap, pb[:, 0:256],
                            op0=ALU.mult, op1=ALU.subtract), "cV")
            g0v.append(ev)
        w0done = g0v[-1]

        if SUB <= 7:
            return
        gb1v = []
        for ch in range(4):
            if len(gb1v) >= 1:
                B.wait("tensor", "cV", gb1v[-1])
            tm = B.t_mm(B.P7[:, 0:512], B.k8m[csl][:, :],
                        B.dsb[:, 512 * ch:512 * (ch + 1)],
                        True, True, tick=True)
            B.wait("vector", "cT", tm)
            ev = B.tick("vector", lambda e, ch=ch, f11=f11:
                        e.scalar_tensor_tensor(
                            B.b1[0:1, 512 * ch:512 * (ch + 1)],
                            B.b1[0:1, 512 * ch:512 * (ch + 1)],
                            f11, B.P7[0:1, 0:512],
                            op0=ALU.mult, op1=ALU.subtract), "cV")
            gb1v.append(ev)
        B.dsb_free = B.val("cT")

        B.drain("vector")
        for t2 in range(2):
            B.em("vector", lambda e, t2=t2: e.tensor_reduce(
                B.gb0c[:, t2:t2 + 1], B.dzf[:, 128 * t2:128 * (t2 + 1)],
                axis=mybir.AxisListType.X, op=ALU.add))
        B.drain("vector")
        B.em("vector", lambda e, kap_ap=kap_ap: e.tensor_scalar(
            B.gb0c[:, 0:2], B.gb0c[:, 0:2], kap_ap, None, op0=ALU.mult))
        B.drain("vector")
        B.tick("vector", lambda e, f_ap=f_ap: e.scalar_tensor_tensor(
            B.b0[:, 0:2], B.b0[:, 0:2], f_ap, B.gb0c[:, 0:2],
            op0=ALU.mult, op1=ALU.subtract), "cV")
        bdone = B.val("cV")

        if SUB <= 8:
            return
        if self.step_hook is not None:
            self.step_hook(s, 2)
        B.wait("tensor", "cV", bdone)
        self.fwd(s + 1)
        if s + 2 <= NCH - 1:
            self.prep(s + 2)

    def epilogue(self):
        B = self
        s = NCH
        slot = s % 2
        B.wait("vector", "sbo", 32 * (s + 1))
        B.wait("vector", "sx2", 16 * (s - 1))
        vx2 = B.tick("vector", lambda e: e.tensor_tensor(
            B.x2sb[:, :], B.xraw[(s - 1) % 3][:, :],
            B.pred[:, 0:D], op=ALU.add), "cV")
        B.wait("gpsimd", "cV", vx2)
        B.dma16("gpsimd", lambda e: e.dma_start(
            out=B.d_x2[128 * (NCH - 1):128 * NCH, :], in_=B.x2sb[:, :]), "sx2")
        B.wait("gpsimd", "sx2", B.val("sx2"))


def host_inputs(x, W0, b0, W1, b1, lr_w, lr_b, fg_w, fg_b, core):
    r = core
    msl = slice(MC * r, MC * (r + 1))
    return {
        "ttt_x": np.ascontiguousarray(x.reshape(NCH * CK, D)),
        "ttt_W0T": np.ascontiguousarray(W0[msl, :].T),
        "ttt_b0": np.ascontiguousarray(b0[msl].reshape(1, MC)),
        "ttt_W1T": np.ascontiguousarray(W1[:, msl].T),
        "ttt_W1n": np.ascontiguousarray(W1[:, msl]),
        "ttt_b1": np.ascontiguousarray(b1.reshape(1, D)),
        "ttt_lrfgT": np.ascontiguousarray(
            np.stack([lr_w.reshape(D), fg_w.reshape(D)], axis=1)),
        "ttt_lrb": np.asarray(lr_b, np.float32).reshape(1, 1),
        "ttt_fgb": np.asarray(fg_b, np.float32).reshape(1, 1),
        **common_inputs(),
    }


import numpy as np
import bass_rust
import concourse.mybir as mybir

F32 = mybir.dt.float32
BF16 = mybir.dt.bfloat16
AF = mybir.ActivationFunctionType
ALU = mybir.AluOpType

D = 2048
FF = 8192
TB = 1024
NT = 8
KD = 16
KF = 64
NW = 1.0 / (FF * D)


class PWStream:
    NSLOT = 4

    def __init__(self, core):
        self.B = core
        self.nc = core.nc

    def declare(self):
        nc = self.nc
        B = self.B
        dt = nc.dram_tensor
        self.d_w1T = dt("ffn_w1T", [D, FF], F32, kind="ExternalInput").ap()
        self.d_w2T = dt("ffn_w2T", [FF, D], F32, kind="ExternalInput").ap()
        self.pwbuf = B.sbuf("pw_buf", [128, 1024])
        self.pacc = B.sbuf("pw_acc", [128, 4])
        for i in range(2):
            B.new_sem("pw%d" % i)
        self.chunks = []
        for j in range(D // 128):
            for k in range(FF // 256):
                self.chunks.append(
                    (self.d_w1T[128 * j:128 * (j + 1), 256 * k:256 * (k + 1)],
                     0))
        for j in range(FF // 128):
            for k in range(D // 256):
                self.chunks.append(
                    (self.d_w2T[128 * j:128 * (j + 1), 256 * k:256 * (k + 1)],
                     1))
        self.npairs = len(self.chunks) // 2
        self.ci = 0
        self.ri = 0
        self.red_tick = {}
        self._init_done = False

    def _trigger_pair(self):
        B = self.B
        p = self.ci
        q = p % 2
        if p >= 2:
            B.wait("scalar", "cV", self.red_tick.pop(p - 2))
        for h in range(2):
            src, _ = self.chunks[2 * p + h]
            B.dma16("scalar", lambda e, src=src, q=q, h=h: e.dma_start(
                out=self.pwbuf[:, 512 * q + 256 * h:512 * q + 256 * (h + 1)],
                in_=src), "pw%d" % q)
        self.ci += 1

    def _reduce_pair(self):
        B = self.B
        p = self.ri
        _, col = self.chunks[2 * p]
        q = p % 2
        B.wait("vector", "pw%d" % q, 32 * (p // 2 + 1))
        B.drain("vector")
        B.em("vector", lambda e, q=q: e.tensor_reduce(
            self.pacc[:, 2:3], self.pwbuf[:, 512 * q:512 * (q + 1)],
            axis=mybir.AxisListType.X, op=ALU.add,
            apply_absolute_value=True))
        B.drain("vector")
        t = B.tick("vector", lambda e, col=col: e.tensor_tensor(
            self.pacc[:, col:col + 1], self.pacc[:, col:col + 1],
            self.pacc[:, 2:3], op=ALU.add), "cV")
        self.red_tick[p] = t
        self.ri += 1

    def hook(self, quota):
        B = self.B
        if not self._init_done:
            self._init_done = True
            B.tick("vector", lambda e: e.memset(self.pacc[:, :], 0.0), "cV")
            B.drain("vector")
            for _ in range(min(2, self.npairs)):
                self._trigger_pair()
        n = 0
        while n < quota and self.ri < self.npairs:
            self._reduce_pair()
            if self.ci < self.npairs:
                self._trigger_pair()
            n += 1

    def finale(self):
        while self.ri < self.npairs:
            self._reduce_pair()


class FFNBuilder:
    def __init__(self, core_builder):
        self.B = core_builder
        self.nc = core_builder.nc

    def declare(self, pw=None):
        nc = self.nc
        B = self.B
        dt = nc.dram_tensor
        if pw is not None:
            self.d_w1T = pw.d_w1T
            self.d_w2T = pw.d_w2T
        else:
            self.d_w1T = dt("ffn_w1T", [D, FF], F32,
                            kind="ExternalInput").ap()
            self.d_w2T = dt("ffn_w2T", [FF, D], F32,
                            kind="ExternalInput").ap()
        self.d_b1 = dt("ffn_b1", [1, FF], F32, kind="ExternalInput").ap()
        self.d_gamma = dt("ffn_gamma", [1, FF], F32, kind="ExternalInput").ap()
        self.d_beta = dt("ffn_beta", [1, FF], F32, kind="ExternalInput").ap()
        self.d_b2 = dt("ffn_b2", [1, D], F32, kind="ExternalInput").ap()
        self.d_xrows = dt("ffn_xrows", [128, NT], mybir.dt.int32,
                          kind="ExternalInput").ap()
        self.d_xrows16 = dt("ffn_xrows16", [128, NT], mybir.dt.int32,
                            kind="ExternalInput").ap()
        self.d_yT = dt("ffn_yT", [D, TB], F32, kind="ExternalOutput").ap()
        self.d_h = dt("ffn_hscr", [TB, FF], F32)

        S = B.sbuf
        self.hqT = S("f_hqT", [128, KF * TB], BF16)
        self.wf1 = [S(f"f_wf1_{i}", [128, KD * 128]) for i in range(2)]
        self.w1q = [S(f"f_w1q_{i}", [128, KD * 128], BF16) for i in range(2)]
        self.qa = S("f_qa", [128, KD * 128], BF16)
        self.s1 = S("f_s1", [128, D])
        self.s2 = S("f_s2", [128, D])
        self.qbf = S("f_qbf", [128, D], BF16)
        self.bnb = S("f_bnb", [128, NT * KF * 6])
        _qaf = self.qa.bitcast(F32)
        self.gbc = _qaf[:, 0:512]
        self.bbc = _qaf[:, 512:1024]
        self.htile = [S(f"f_ht{i}", [128, 256]) for i in range(2)]
        self.hgl = [S(f"f_hg{i}", [128, 256]) for i in range(2)]
        self.b1bc = self.qbf.bitcast(F32)[:, 0:256]
        self.f1q = [self.hqT[:, TB * 16:TB * 16 + KD * 256],
                    self.hqT[:, TB * 20:TB * 20 + KD * 256]]
        self.fst = S("f_stat", [128, 16])
        self.bx = S("f_bx", [128, NT])
        self.mv = S("f_mv", [128, NT * 2])
        self.mu = S("f_mu", [128, NT])
        self.rho = S("f_rho", [128, NT])
        self.mx2 = S("f_mx2", [128, NT])
        self.s2q = S("f_s2q", [128, NT])
        self.b2neg = S("f_b2neg", [128, NT])
        self.thc = S("f_thc", [128, 4])
        self.swc = S("f_swc", [128, 2])
        self.scl = S("f_scl", [1, 8])
        self.b2row = S("f_b2row", [1, TB])
        self.row8 = S("f_row8", [8, 128])
        self.b2c = S("f_b2c", [128, KD])
        self.bet2b = self.s2[:, TB:2 * TB]
        self.ytT = self.s2[:, 0:TB]
        self.x2c = self.s1[:, 0:NT * 128]
        self.rs1 = S("f_rs1", [1, 512])
        self.rs2 = S("f_rs2", [1, 512])
        self.xrows = S("f_xrows_sb", [128, NT], mybir.dt.int32)
        self.xrows16 = S("f_xrows16_sb", [128, NT], mybir.dt.int32)
        self.xqT = self.hqT

        for s in ("fw0", "fw1", "fh0", "fh1", "fo", "fo0", "fo1", "fx",
                  "fy", "fr", "fxr", "fg", "fxB", "foB"):
            B.new_sem(s)

    def emit(self, barrier=None, pw=None):
        B = self.B
        if barrier:
            for eng in ("sync", "gpsimd", "vector", "scalar", "tensor"):
                for sem_name, v in barrier.items():
                    B.wait(eng, sem_name, v)
        B.dma16("sync", lambda e: e.dma_start(
            out=self.xrows[:, :], in_=self.d_xrows), "fxr")
        B.dma16("sync", lambda e: e.dma_start(
            out=self.xrows16[:, :], in_=self.d_xrows16), "fxr")
        self._xrows_fr = B.val("fxr")
        if pw is not None:
            self.phaseW_finale(pw)
        else:
            self.phaseW()
        self.phaseX()
        self.phaseF1()
        self.phaseF3()
        self.phaseF4()
        self.phaseF5()

    def phaseW(self):
        B = self.B
        FB = self
        ms = B.tick("vector", lambda e: e.memset(FB.fst[:, :], 0.0), "cV")
        B.wait("vector", "cV", ms)
        self.rd_tick = {0: 0, 1: 0}
        nld = {0: 0, 1: 0}

        def stream(src_ap, acc_col, i, total_i):
            sl = i % 2
            sem = "fw%d" % sl
            if self.rd_tick[sl]:
                B.wait("sync", "cV", self.rd_tick[sl])
            nld[sl] += 1
            B.dma16("sync", lambda e, src_ap=src_ap, sl=sl: e.dma_start(
                out=FB.wf1[sl][:, :], in_=src_ap), sem)
            B.wait("vector", sem, 16 * nld[sl])
            B.em("vector", lambda e, sl=sl: e.tensor_reduce(
                FB.fst[:, 2:3], FB.wf1[sl][:, :], axis=mybir.AxisListType.X,
                op=ALU.add, apply_absolute_value=True))
            B.drain("vector")
            t = B.tick("vector", lambda e, acc_col=acc_col: e.tensor_tensor(
                FB.fst[:, acc_col:acc_col + 1], FB.fst[:, acc_col:acc_col + 1],
                FB.fst[:, 2:3], op=ALU.add), "cV")
            B.drain("vector")
            self.rd_tick[sl] = t

        i = 0
        for dd in range(KD):
            for cc in range(4):
                stream(self.d_w1T[128 * dd:128 * (dd + 1),
                                  2048 * cc:2048 * (cc + 1)], 0, i, 128)
                i += 1
        for fj in range(KF):
            stream(self.d_w2T[128 * fj:128 * (fj + 1), :], 1, i, 128)
            i += 1
        self.w_nld = dict(nld)
        self._w_tail(FB.fst[:, 0:2])

    def phaseW_finale(self, pw):
        pw.finale()
        self.rd_tick = {0: 0, 1: 0}
        self.w_nld = {0: 0, 1: 0}
        self._w_tail(pw.pacc[:, 0:2])

    def _w_tail(self, acc_ap):
        B = self.B
        FB = self
        B.wait("tensor", "cV", B.val("cV"))
        tm = B.t_mm(B.P6[0:1, 0:2], B.onesc[:, 0:1], acc_ap,
                    True, True, tick=True)
        B.wait("vector", "cT", tm)
        B.em("vector", lambda e: e.tensor_scalar(
            FB.scl[0:1, 0:2], B.P6[0:1, 0:2], NW, 1e-5,
            op0=ALU.mult, op1=ALU.max))
        B.drain("vector")
        B.em("vector", lambda e: e.tensor_scalar(
            FB.scl[0:1, 2:4], FB.scl[0:1, 0:2], 0.5, None, op0=ALU.mult))
        B.drain("vector")
        t = B.tick("vector", lambda e: e.tensor_scalar(
            FB.scl[0:1, 4:6], FB.scl[0:1, 2:4], -1.0, None, op0=ALU.mult),
            "cV")
        B.wait("tensor", "cV", t)
        B.t_mm(B.P6[:, 4:8], B.onesr[0:1, :], FB.scl[0:1, 2:6], True, True)
        tb = B.t_mm(B.P6[:, 8:10], B.onesr[0:1, :], FB.scl[0:1, 0:2],
                    True, True, tick=True)
        B.wait("vector", "cT", tb)
        B.em("vector", lambda e: e.tensor_copy(FB.thc[:, 0:4], B.P6[:, 4:8]))
        self.w_done = B.tick("vector", lambda e: e.tensor_copy(
            FB.swc[:, 0:2], B.P6[:, 8:10]), "cV")

    def phaseX(self):
        B = self.B
        FB = self
        self.xev = [0] * KD
        nfx = B.val("fx")
        B.wait("gpsimd", "fxr", self._xrows_fr)
        for tt in range(NT):
            B.wait("gpsimd", "cV", getattr(self, "_x_s1r", 0))
            nfx = B.dma16("gpsimd", lambda e, tt=tt: e.indirect_dma_start(
                out=FB.s1[:, :], out_offset=None,
                in_=B.d_x2,
                in_offset=bass_rust.IndirectOffsetOnAxis(
                    ap=FB.xrows[:, tt:tt + 1], axis=0)), "fg")
            B.wait("vector", "fg", nfx)
            B.drain("vector")
            for q in range(4):
                B.em("vector", lambda e, q=q, tt=tt: e.bn_stats(
                    FB.bnb[:, 24 * tt + 6 * q:24 * tt + 6 * (q + 1)],
                    FB.s1[:, 512 * q:512 * (q + 1)]))
            B.drain("vector")
            vb = B.tick("vector", lambda e, tt=tt: e.bn_aggr(
                FB.mv[:, 2 * tt:2 * (tt + 1)],
                FB.bnb[:, 24 * tt:24 * tt + 24]), "cV")
            B.wait("scalar", "cV", vb)
            sq = B.tick("scalar", lambda e, tt=tt: e.activation(
                FB.rho[:, tt:tt + 1], FB.mv[:, 2 * tt + 1:2 * tt + 2],
                AF.Sqrt, bias=B.epscol[:, 0:1]), "cS")
            B.wait("vector", "cS", sq)
            B.em("vector", lambda e, tt=tt: e.reciprocal(
                FB.rho[:, tt:tt + 1], FB.rho[:, tt:tt + 1]))
            B.drain("vector")
            B.em("vector", lambda e, tt=tt: e.tensor_scalar(
                FB.s2[:, :], FB.s1[:, :], FB.mv[:, 2 * tt:2 * tt + 1],
                FB.rho[:, tt:tt + 1], op0=ALU.subtract, op1=ALU.mult))
            B.drain("vector")
            B.em("vector", lambda e: e.tensor_reduce(
                FB.fst[:, 4:5], FB.s2[:, :], axis=mybir.AxisListType.X,
                op=ALU.max, apply_absolute_value=True))
            B.drain("vector")
            B.em("vector", lambda e: e.tensor_scalar(
                FB.fst[:, 5:6], FB.fst[:, 4:5], 1e-5, None, op0=ALU.max))
            B.drain("vector")
            if tt == 0:
                B.wait("vector", "cV", self.w_done)
            B.em("vector", lambda e, tt=tt: e.scalar_tensor_tensor(
                FB.bx[:, tt:tt + 1], FB.fst[:, 5:6], -1.0 / 127.0,
                FB.swc[:, 0:1], op0=ALU.mult, op1=ALU.mult))
            B.em("vector", lambda e: e.reciprocal(
                FB.fst[:, 6:7], FB.fst[:, 5:6]))
            B.drain("vector")
            B.em("vector", lambda e: e.tensor_scalar(
                FB.fst[:, 7:8], FB.fst[:, 6:7], 127.0, None, op0=ALU.mult))
            B.drain("vector")
            B.em("vector", lambda e: e.tensor_scalar(
                FB.s1[:, :], FB.s2[:, :], FB.fst[:, 7:8], None,
                op0=ALU.mult))
            B.drain("vector")
            B.em("vector", lambda e: e.tensor_copy(
                FB.s2[:, :].bitcast(mybir.dt.int32), FB.s1[:, :]))
            B.drain("vector")
            vq = B.tick("vector", lambda e: e.tensor_copy(
                FB.qbf[:, :], FB.s2[:, :].bitcast(mybir.dt.int32)), "cV")
            self._x_s1r = vq
            B.wait("tensor", "cV", vq)
            for dd in range(KD):
                pb = (B.P4 if dd % 2 == 0 else B.P5)[:, 0:64].bitcast(BF16)
                if self.xev[dd - 2] if dd >= 2 else 0:
                    B.wait("tensor", "cV", self.xev[dd - 2])
                tk = B.tick("tensor", lambda e, pb=pb, dd=dd: e.transpose(
                    pb, FB.qbf[:, 128 * dd:128 * (dd + 1)],
                    B.identb[:, :]), "cT")
                B.wait("vector", "cT", tk)
                self.xev[dd] = B.tick(
                    "vector", lambda e, pb=pb, dd=dd, tt=tt: e.tensor_copy(
                        FB.xqT[:, TB * dd + 128 * tt:TB * dd + 128 * (tt + 1)],
                        pb), "cV")
        self.x_done = B.val("cV")

    def phaseF1(self):
        B = self.B
        FB = self
        q_tick = {0: 0, 1: 0}
        mm_tick = {0: 0, 1: 0}
        self.f1_nld = {0: self.w_nld[0], 1: self.w_nld[1]}
        h_ev = [0, 0]
        pv_ev = [0, 0]
        fo_at = [0, 0]
        hst = 0
        self._rs1_free = 0
        for fc2 in range(KF // 2):
            sl = fc2 % 2
            qt = 0
            for pc in range(2):
                fc = 2 * fc2 + pc
                psl = fc % 2
                sem = "fw%d" % psl
                if q_tick[psl]:
                    B.wait("sync", "cV", q_tick[psl])
                elif self.rd_tick[psl]:
                    B.wait("sync", "cV", self.rd_tick[psl])
                self.f1_nld[psl] += 1
                B.dma16("sync", lambda e, fc=fc, psl=psl: e.dma_start(
                    out=FB.wf1[psl][:, :].rearrange("p (j m) -> p j m", m=128),
                    in_=FB.d_w1T.rearrange("(j p) m -> p j m", p=128)
                    [:, :, 128 * fc:128 * (fc + 1)]), sem)
                B.wait("vector", sem, 16 * self.f1_nld[psl])
                if fc == 0:
                    B.wait("vector", "cV", self.x_done)
                if pc == 0 and mm_tick[sl]:
                    B.wait("vector", "cT", mm_tick[sl])
                B.drain("vector")
                B.em("vector", lambda e, psl=psl: e.tensor_scalar(
                    FB.qa[:, :], FB.wf1[psl][:, :], FB.thc[:, 0:1], None,
                    op0=ALU.is_gt))
                B.drain("vector")
                qt = B.tick("vector", lambda e, sl=sl, psl=psl, pc=pc:
                            e.scalar_tensor_tensor(
                    FB.f1q[sl].rearrange("p (j f) -> p j f", f=256)
                    [:, :, 128 * pc:128 * (pc + 1)],
                    FB.wf1[psl][:, :].rearrange("p (j m) -> p j m", m=128),
                    FB.thc[:, 2:3], FB.qa[:, :].rearrange(
                        "p (j m) -> p j m", m=128),
                    op0=ALU.is_lt, op1=ALU.subtract), "cV")
                q_tick[psl] = qt
            B.wait("sync", "cT", self._rs1_free)
            nfr = B.dma16("sync", lambda e, fc2=fc2: e.dma_start(
                out=FB.rs1[0:1, 0:256],
                in_=FB.d_b1[0:1, 256 * fc2:256 * (fc2 + 1)]), "fr")
            B.wait("tensor", "cV", qt)
            B.wait("tensor", "fr", nfr)
            bb = B.t_mm(B.P6[:, 0:256], B.onesr[0:1, :],
                        FB.rs1[0:1, 0:256], True, True, tick=True)
            self._rs1_free = bb
            B.wait("vector", "cT", bb)
            bbv = B.tick("vector", lambda e: e.tensor_copy(
                FB.b1bc[:, :], B.P6[:, 0:256]), "cV")
            B.wait("tensor", "cV", bbv)
            for tt in range(NT):
                pb = B.P1 if tt % 2 == 0 else B.P2
                if hst >= 2:
                    B.wait("tensor", "cV", pv_ev[hst % 2])
                for dd in range(KD):
                    B.t_mm(pb[:, 0:256],
                           FB.xqT[:, TB * dd + 128 * tt:
                                  TB * dd + 128 * (tt + 1)],
                           FB.f1q[sl][:, 256 * dd:256 * (dd + 1)],
                           dd == 0, dd == KD - 1, tick=(dd == KD - 1))
                mt = B.val("cT")
                if tt == NT - 1:
                    mm_tick[sl] = mt
                hb = hst % 2
                B.wait("vector", "cT", mt)
                if hst >= 2:
                    B.wait("vector", "cS", h_ev[hb])
                B.drain("vector")
                ve = B.tick("vector", lambda e, pb=pb, hb=hb, tt=tt:
                            e.scalar_tensor_tensor(
                                FB.htile[hb][:, :], pb[:, 0:256],
                                FB.bx[:, tt:tt + 1], FB.b1bc[:, :],
                                op0=ALU.mult, op1=ALU.add), "cV")
                pv_ev[hb] = ve
                B.wait("scalar", "cV", ve)
                if hst >= 2:
                    B.wait("scalar", "fo%d" % hb, fo_at[hb])
                sg = B.tick("scalar", lambda e, hb=hb: e.activation(
                    FB.hgl[hb][:, :], FB.htile[hb][:, :], AF.Gelu), "cS")
                h_ev[hb] = sg
                B.wait("vector", "cS", sg)
                B.drain("vector")
                B.em("vector", lambda e, hb=hb, tt=tt, fc2=fc2: e.bn_stats(
                    FB.bnb[:, (tt * (KF // 2) + fc2) * 6:
                           (tt * (KF // 2) + fc2) * 6 + 6],
                    FB.hgl[hb][:, :]))
                B.wait("scalar", "cS", sg)
                fo_at[hb] = B.dma16(
                    "scalar", lambda e, hb=hb, tt=tt, fc2=fc2: e.dma_start(
                        out=FB.d_h[128 * tt:128 * (tt + 1),
                                   256 * fc2:256 * (fc2 + 1)],
                        in_=FB.hgl[hb][:, :]), "fo%d" % hb)
                hst += 1
        self.f1_fo = (B.val("fo0"), B.val("fo1"))
        self.f1_q = dict(q_tick)
        self.f1_mm = dict(mm_tick)

    def phaseF3(self):
        B = self.B
        FB = self
        B.drain("vector")
        for tt in range(NT):
            vb = B.tick("vector", lambda e, tt=tt: e.bn_aggr(
                FB.mv[:, 2 * tt:2 * (tt + 1)],
                FB.bnb[:, tt * (KF // 2) * 6:(tt + 1) * (KF // 2) * 6]),
                "cV")
            B.drain("vector")
            B.em("vector", lambda e, tt=tt: e.tensor_copy(
                FB.mu[:, tt:tt + 1], FB.mv[:, 2 * tt:2 * tt + 1]))
            B.wait("scalar", "cV", B.val("cV"))
            sq = B.tick("scalar", lambda e, tt=tt: e.activation(
                FB.rho[:, tt:tt + 1], FB.mv[:, 2 * tt + 1:2 * tt + 2],
                AF.Sqrt, bias=B.epscol[:, 0:1]), "cS")
            B.wait("vector", "cS", sq)
            B.em("vector", lambda e, tt=tt: e.reciprocal(
                FB.rho[:, tt:tt + 1], FB.rho[:, tt:tt + 1]))
            B.drain("vector")
        mz = B.tick("vector", lambda e: e.memset(FB.mx2[:, :], 0.0), "cV")
        B.wait("vector", "cV", mz)
        B.wait("vector", "fo0", self.f1_fo[0])
        B.wait("vector", "fo1", self.f1_fo[1])
        self._f3_s1r = {}
        self._f3_fo = {}
        for gc in range(KF // 4):
            B.wait("sync", "cT", getattr(self, "_rs1_free", 0))
            nfr = B.dma16("sync", lambda e, gc=gc: e.dma_start(
                out=FB.rs1[0:1, :],
                in_=FB.d_gamma[0:1, 512 * gc:512 * (gc + 1)]), "fr")
            nfr = B.dma16("sync", lambda e, gc=gc: e.dma_start(
                out=FB.rs2[0:1, :],
                in_=FB.d_beta[0:1, 512 * gc:512 * (gc + 1)]), "fr")
            B.wait("tensor", "fr", nfr)
            gb = B.t_mm(B.P6[:, 0:512], B.onesr[0:1, :],
                        FB.rs1[0:1, :], True, True, tick=True)
            B.wait("vector", "cT", gb)
            B.tick("vector", lambda e: e.tensor_copy(
                FB.gbc[:, :], B.P6[:, 0:512]), "cV")
            bb = B.t_mm(B.P7[:, 0:512], B.onesr[0:1, :],
                        FB.rs2[0:1, :], True, True, tick=True)
            self._rs1_free = bb
            B.wait("vector", "cT", bb)
            B.tick("vector", lambda e: e.tensor_copy(
                FB.bbc[:, :], B.P7[:, 0:512]), "cV")
            B.drain("vector")
            for tt in range(NT):
                i = gc * NT + tt
                sl = i % 2
                osem = "fo" if sl == 0 else "foB"
                lsem = "fx" if sl == 0 else "fxB"
                s1s = FB.s1[:, 512 * sl:512 * (sl + 1)]
                s2s = FB.s2[:, 512 * sl:512 * (sl + 1)]
                B.wait("sync", "cV", self._f3_s1r.get(sl, 0))
                nfx = B.dma16("sync", lambda e, tt=tt, gc=gc, s1s=s1s:
                              e.dma_start(
                    out=s1s,
                    in_=FB.d_h[128 * tt:128 * (tt + 1),
                               512 * gc:512 * (gc + 1)]), lsem)
                B.wait("vector", lsem, nfx)
                B.wait("vector", osem, self._f3_fo.get(sl, 0))
                B.drain("vector")
                s1r = B.tick("vector", lambda e, tt=tt, s1s=s1s, s2s=s2s:
                             e.tensor_scalar(
                    s2s, s1s, FB.mu[:, tt:tt + 1],
                    FB.rho[:, tt:tt + 1], op0=ALU.subtract, op1=ALU.mult),
                    "cV")
                self._f3_s1r[sl] = s1r
                B.drain("vector")
                B.em("vector", lambda e, s2s=s2s: e.tensor_tensor(
                    s2s, s2s, FB.gbc[:, :], op=ALU.mult))
                B.drain("vector")
                B.em("vector", lambda e, s2s=s2s: e.tensor_tensor(
                    s2s, s2s, FB.bbc[:, :], op=ALU.add))
                B.drain("vector")
                B.em("vector", lambda e, s2s=s2s: e.tensor_reduce(
                    FB.fst[:, 8:9], s2s,
                    axis=mybir.AxisListType.X, op=ALU.max,
                    apply_absolute_value=True))
                B.drain("vector")
                B.em("vector", lambda e, tt=tt: e.tensor_tensor(
                    FB.mx2[:, tt:tt + 1], FB.mx2[:, tt:tt + 1],
                    FB.fst[:, 8:9], op=ALU.max))
                B.drain("vector")
                hv = B.tick("vector", lambda e: e.tensor_copy(
                    FB.fst[:, 10:11], FB.fst[:, 10:11]), "cV")
                B.wait("scalar", "cV", hv)
                self._f3_fo[sl] = B.dma16(
                    "scalar", lambda e, tt=tt, gc=gc, s2s=s2s: e.dma_start(
                        out=FB.d_h[128 * tt:128 * (tt + 1),
                                   512 * gc:512 * (gc + 1)],
                        in_=s2s), osem)
        self.f3_fo = (B.val("fo"), B.val("foB"))

    def phaseF4(self):
        B = self.B
        FB = self
        B.drain("vector")
        for tt in range(NT):
            B.em("vector", lambda e, tt=tt: e.tensor_scalar(
                FB.fst[:, 8:9], FB.mx2[:, tt:tt + 1], 1e-5, None,
                op0=ALU.max))
            B.drain("vector")
            B.em("vector", lambda e, tt=tt: e.scalar_tensor_tensor(
                FB.b2neg[:, tt:tt + 1], FB.fst[:, 8:9], -1.0 / 127.0,
                FB.swc[:, 1:2], op0=ALU.mult, op1=ALU.mult))
            B.em("vector", lambda e: e.reciprocal(
                FB.fst[:, 9:10], FB.fst[:, 8:9]))
            B.drain("vector")
            B.em("vector", lambda e, tt=tt: e.tensor_scalar(
                FB.s2q[:, tt:tt + 1], FB.fst[:, 9:10], 127.0, None,
                op0=ALU.mult))
            B.drain("vector")
        B.wait("vector", "fo", self.f3_fo[0])
        B.wait("vector", "foB", self.f3_fo[1])
        self.fev = [0] * KD
        B.wait("vector", "cT", max(self.f1_mm.values()))
        mx = max(self._f3_s1r.values()) if self._f3_s1r else 0
        s1w = {0: mx, 1: mx}
        B.wait("sync", "fx", B.val("fx"))
        B.wait("sync", "fxB", B.val("fxB"))
        tkt = 0
        for tt in range(NT):
            for gc in range(4):
                for hh in range(2):
                    i = (tt * 4 + gc) * 2 + hh
                    sl = i % 2
                    lsem = "fx" if sl == 0 else "fxB"
                    s1h = FB.s1[:, 1024 * sl:1024 * (sl + 1)]
                    s2h = FB.s2[:, 1024 * sl:1024 * (sl + 1)]
                    qbh = FB.qbf[:, 1024 * sl:1024 * (sl + 1)]
                    B.wait("sync", "cV", s1w.get(sl, 0))
                    nfx = B.dma16("sync",
                                  lambda e, tt=tt, gc=gc, hh=hh, s1h=s1h:
                                  e.dma_start(
                        out=s1h,
                        in_=FB.d_h[128 * tt:128 * (tt + 1),
                                   2048 * gc + 1024 * hh:
                                   2048 * gc + 1024 * (hh + 1)]), lsem)
                    B.wait("vector", lsem, nfx)
                    if tkt:
                        B.wait("vector", "cT", tkt)
                    B.drain("vector")
                    B.em("vector", lambda e, tt=tt, s1h=s1h: e.tensor_scalar(
                        s1h, s1h, FB.s2q[:, tt:tt + 1], None,
                        op0=ALU.mult))
                    B.drain("vector")
                    B.em("vector", lambda e, s1h=s1h, s2h=s2h: e.tensor_copy(
                        s2h.bitcast(mybir.dt.int32), s1h))
                    B.drain("vector")
                    vq = B.tick("vector", lambda e, s2h=s2h, qbh=qbh:
                                e.tensor_copy(
                        qbh, s2h.bitcast(mybir.dt.int32)), "cV")
                    s1w[sl] = vq
                    B.wait("tensor", "cV", vq)
                    for k in range(8):
                        fj = 16 * gc + 8 * hh + k
                        pb = (B.P4 if k % 2 == 0
                              else B.P5)[:, 0:64].bitcast(BF16)
                        kk = 8 * hh + k
                        if self.fev[kk - 2]:
                            B.wait("tensor", "cV", self.fev[kk - 2])
                        tk = B.tick("tensor",
                                    lambda e, pb=pb, sl=sl, k=k:
                                    e.transpose(
                            pb, FB.qbf[:, 1024 * sl + 128 * k:
                                       1024 * sl + 128 * (k + 1)],
                            B.identb[:, :]), "cT")
                        if k == 7:
                            tkt = tk
                        B.wait("vector", "cT", tk)
                        self.fev[kk] = B.tick(
                            "vector",
                            lambda e, pb=pb, fj=fj, tt=tt: e.tensor_copy(
                                FB.hqT[:, TB * fj + 128 * tt:
                                       TB * fj + 128 * (tt + 1)], pb), "cV")
        self._f4_s1r = max(s1w.values())
        self.f4_done = B.val("cV")

    def phaseF5(self):
        B = self.B
        FB = self

        bt = B.val("cV")
        B.wait("tensor", "cV", bt)
        tb = B.tick("tensor", lambda e: e.transpose(
            B.P4[0:NT, 0:128], FB.b2neg[:, 0:NT], B.ident[:, :]), "cT")
        B.wait("vector", "cT", tb)
        bv = B.tick("vector", lambda e: e.tensor_copy(
            FB.row8[0:NT, :], B.P4[0:NT, 0:128]), "cV")
        B.wait("sync", "cV", bv)
        nbr = B.dma16("sync", lambda e: e.dma_start(
            out=FB.b2row[0:1, :].rearrange("o (t p) -> o t p", p=128),
            in_=FB.row8[0:NT, :].rearrange("t p -> o t p", o=1)
            if False else FB.row8[0:NT, :]), "fr")
        B.wait("tensor", "fr", nbr)
        prevc = 0
        for hh in range(2):
            if prevc:
                B.wait("tensor", "cV", prevc)
            th = B.t_mm(B.P7[:, 0:512], B.onesr[0:1, :],
                        FB.b2row[0:1, 512 * hh:512 * (hh + 1)], True, True,
                        tick=True)
            B.wait("vector", "cT", th)
            prevc = B.tick("vector", lambda e, hh=hh: e.tensor_copy(
                FB.bet2b[:, 512 * hh:512 * (hh + 1)], B.P7[:, 0:512]), "cV")
        nfx = B.dma16("sync", lambda e: e.dma_start(
            out=FB.b2c[:, :],
            in_=FB.d_b2.rearrange("o (j p) -> (o p) j", p=128),
            allow_slow_non_contiguous=True), "fx")
        bet_done = B.val("cV")

        q_tick = {0: 0, 1: 0}
        f5mm = {0: 0, 1: 0}
        nld = dict(self.f1_nld)
        yout_tick = {}
        nfy = 0
        x2tr_done = 0
        for dd in range(KD):
            ppair = [B.P1, B.P2] if dd % 2 == 0 else [B.P3, B.P0]
            B.wait("gpsimd", "cT", x2tr_done)
            if dd == 0:
                B.wait("gpsimd", "cV", self._f4_s1r)
                B.wait("gpsimd", "fx", B.val("fx"))
                B.wait("gpsimd", "fxB", B.val("fxB"))
            for tt in range(NT):
                nfx = B.dma16(
                    "gpsimd", lambda e, dd=dd, tt=tt: e.indirect_dma_start(
                        out=FB.x2c[:, 128 * tt:128 * (tt + 1)],
                        out_offset=None,
                        in_=B.d_x2.rearrange("t (j m) -> (t j) m", m=128),
                        element_offset=128 * dd,
                        in_offset=bass_rust.IndirectOffsetOnAxis(
                            ap=FB.xrows16[:, tt:tt + 1], axis=0)), "fg")
            for fg in range(4):
                sl = (4 * dd + fg) % 2
                sem = "fw%d" % sl
                if q_tick[sl]:
                    B.wait("sync", "cV", q_tick[sl])
                elif self.f1_q[sl]:
                    B.wait("sync", "cV", self.f1_q[sl])
                nld[sl] += 1
                B.dma16("sync", lambda e, dd=dd, fg=fg, sl=sl: e.dma_start(
                    out=FB.wf1[sl][:, :].rearrange("p (j m) -> p j m", m=128),
                    in_=FB.d_w2T.rearrange("(j p) m -> p j m", p=128)
                    [:, 16 * fg:16 * (fg + 1), 128 * dd:128 * (dd + 1)]),
                    sem)
                B.wait("vector", sem, 16 * nld[sl])
                if dd == 0 and fg == 0:
                    B.wait("tensor", "cV", self.f4_done)
                if dd <= 1 and fg <= 1:
                    B.wait("vector", "cT", self.f1_mm[sl])
                if f5mm[sl]:
                    B.wait("vector", "cT", f5mm[sl])
                B.drain("vector")
                B.em("vector", lambda e, sl=sl: e.tensor_scalar(
                    FB.qa[:, :], FB.wf1[sl][:, :], FB.thc[:, 1:2], None,
                    op0=ALU.is_gt))
                B.drain("vector")
                qt = B.tick("vector", lambda e, sl=sl: e.scalar_tensor_tensor(
                    FB.w1q[sl][:, :], FB.wf1[sl][:, :], FB.thc[:, 3:4],
                    FB.qa[:, :], op0=ALU.is_lt, op1=ALU.subtract), "cV")
                q_tick[sl] = qt
                B.wait("tensor", "cV", qt)
                if dd >= 2 and fg == 0:
                    B.wait("tensor", "cV", yout_tick[dd - 2])
                for k in range(KD):
                    fj = 16 * fg + k
                    for hh in range(2):
                        t5 = B.t_mm(ppair[hh][:, :],
                                    FB.w1q[sl][:, 128 * k:128 * (k + 1)],
                                    FB.hqT[:, TB * fj + 512 * hh:
                                           TB * fj + 512 * (hh + 1)],
                                    (fg == 0 and k == 0),
                                    fg == 3 and k == KD - 1,
                                    tick=(k == KD - 1 and hh == 1))
                        if k == KD - 1 and hh == 1:
                            f5mm[sl] = t5
            mt = B.val("cT")
            B.wait("vector", "cT", mt)
            if dd >= 1:
                B.wait("vector", "fy", nfy)
            B.drain("vector")
            for hh in range(2):
                if hh == 1:
                    ye = B.tick("vector",
                                lambda e, ppair=ppair, hh=hh: e.tensor_tensor(
                                    FB.ytT[:, 512 * hh:512 * (hh + 1)],
                                    ppair[hh][:, :],
                                    FB.bet2b[:, 512 * hh:512 * (hh + 1)],
                                    op=ALU.mult), "cV")
                else:
                    B.em("vector",
                         lambda e, ppair=ppair, hh=hh: e.tensor_tensor(
                             FB.ytT[:, 512 * hh:512 * (hh + 1)],
                             ppair[hh][:, :],
                             FB.bet2b[:, 512 * hh:512 * (hh + 1)],
                             op=ALU.mult))
            B.drain("vector")
            B.wait("tensor", "cV", ye)
            B.wait("tensor", "fg", nfx)
            sts = []
            for tt in range(NT):
                pb = (B.P4 if tt % 2 == 0 else B.P5)[:, 128:256]
                if tt >= 2:
                    B.wait("tensor", "cV", sts[tt - 2])
                tk = B.tick("tensor", lambda e, pb=pb, tt=tt: e.transpose(
                    pb, FB.x2c[:, 128 * tt:128 * (tt + 1)], B.ident[:, :]),
                    "cT")
                B.wait("vector", "cT", tk)
                st = B.tick("vector", lambda e, pb=pb, tt=tt, dd=dd:
                            e.scalar_tensor_tensor(
                                FB.ytT[:, 128 * tt:128 * (tt + 1)],
                                FB.ytT[:, 128 * tt:128 * (tt + 1)],
                                FB.b2c[:, dd:dd + 1], pb,
                                op0=ALU.add, op1=ALU.add), "cV")
                B.drain("vector")
                sts.append(st)
            yout_tick[dd] = ye
            x2tr_done = B.val("cT")
            B.wait("scalar", "cV", sts[-1])
            nfy = B.dma16("scalar", lambda e, dd=dd: e.dma_start(
                out=FB.d_yT[128 * dd:128 * (dd + 1), :],
                in_=FB.ytT[:, :]), "fy")
        B.wait("gpsimd", "fy", nfy)


def ffn_host_inputs(w1, w2, b1, gamma, beta, b2):
    return {
        "ffn_w1T": np.ascontiguousarray(w1.T),
        "ffn_w2T": np.ascontiguousarray(w2.T),
        "ffn_b1": np.ascontiguousarray(b1.reshape(1, FF)),
        "ffn_gamma": np.ascontiguousarray(gamma.reshape(1, FF)),
        "ffn_beta": np.ascontiguousarray(beta.reshape(1, FF)),
        "ffn_b2": np.ascontiguousarray(b2.reshape(1, D)),
    }


import concourse.bass as bass
from concourse.bass_utils import run_bass_kernel_spmd

N_CORES = 8
SEQ = NCH * CK
TOK_PER_CORE = SEQ // N_CORES


def _xrows_input(core):
    idx = (TOK_PER_CORE * core + np.arange(TOK_PER_CORE, dtype=np.int32)
           ).reshape(NT, 128)
    return {"ffn_xrows": np.ascontiguousarray(idx.T),
            "ffn_xrows16": np.ascontiguousarray(16 * idx.T)}


def _build_program():
    nc = bass.Bass("TRN2", target_bir_lowering=False)
    T = TTTBuilder(nc)
    PW = PWStream(T)
    T.declare(early=lambda core: PW.declare())
    qs = [PW.npairs // NCH + (1 if i < PW.npairs % NCH else 0)
          for i in range(NCH)]

    def hookfn(s, sub):
        if sub == 0:
            PW.hook(qs[s])

    T.emit(step_hook=hookfn)
    barrier = {s: T.val(s) for s in
               ("cV", "cT", "cS", "sx2", "cc", "sbo", "sbi",
                "sx", "sxc0", "sxc1", "sxc2")}
    ttt_names = [it[0] for it in T.ctx[T._sbuf_mark:]
                 if isinstance(it, tuple)]
    T.free_sbuf(ttt_names)
    FB = FFNBuilder(T)
    FB.declare(pw=PW)
    FB.emit(barrier=barrier, pw=PW)
    with nc.Block() as block:
        T.replay(block)
    return nc


def kernel(**inputs):
    x = np.ascontiguousarray(inputs["L0_attn_x"][0], dtype=np.float32)
    W0 = np.asarray(inputs["L0_attn_W0"], np.float32)
    b0 = np.asarray(inputs["L0_attn_b0"], np.float32)
    W1 = np.asarray(inputs["L0_attn_W1"], np.float32)
    b1a = np.asarray(inputs["L0_attn_b1"], np.float32)
    lr_w = np.asarray(inputs["L0_attn_lr_w"], np.float32)
    lr_b = np.asarray(inputs["L0_attn_lr_b"], np.float32)
    fg_w = np.asarray(inputs["L0_attn_fg_w"], np.float32)
    fg_b = np.asarray(inputs["L0_attn_fg_b"], np.float32)
    fw1 = np.asarray(inputs["L0_ffn_w1"], np.float32)
    fb1 = np.asarray(inputs["L0_ffn_b1"], np.float32)
    gamma = np.asarray(inputs["L0_ffn_gamma"], np.float32)
    beta = np.asarray(inputs["L0_ffn_beta"], np.float32)
    fw2 = np.asarray(inputs["L0_ffn_w2"], np.float32)
    fb2 = np.asarray(inputs["L0_ffn_b2"], np.float32)

    nc = _build_program()
    ffn_common = ffn_host_inputs(fw1, fw2, fb1, gamma, beta, fb2)
    in_maps = []
    for r in range(N_CORES):
        im = host_inputs(x, W0, b0, W1, b1a, lr_w, lr_b, fg_w, fg_b, r)
        im.update(ffn_common)
        im.update(_xrows_input(r))
        in_maps.append(im)
    res = run_bass_kernel_spmd(nc, in_maps, core_ids=list(range(N_CORES)))
    out = np.concatenate(
        [np.ascontiguousarray(res.results[r]["ffn_yT"]).T
         for r in range(N_CORES)], axis=0)
    return out.reshape(1, SEQ, D).astype(np.float32)
